# version 05
# baseline (speedup 1.0000x reference)
import sys
if '/opt/trn_rl_repo' not in sys.path:
    sys.path.insert(0, '/opt/trn_rl_repo')
import math
from contextlib import ExitStack
import numpy as np
import concourse.bass as bass
import concourse.mybir as mybir
from concourse.bass_utils import run_bass_kernel_spmd
from concourse.alu_op_type import AluOpType as Op

F32 = mybir.dt.float32
BF16 = mybir.dt.bfloat16
I32 = mybir.dt.int32
AF = mybir.ActivationFunctionType
AX = mybir.AxisListType

NC = 8
B, S, D = 16, 2048, 1024
NB = B // NC
H, DK = 16, 64
I0, I1 = 3072, 4096
NDC = D // 128
NTT = S // 512
NKT = S // 128
NOC0 = I0 // 128
NHC1 = I1 // 128
TWO_PI_HI = float(np.float32(2.0 * math.pi))
TWO_PI_LO = float(np.float32(2.0 * math.pi - np.float64(np.float32(2.0 * math.pi))))
INV2PI = float(np.float32(1.0 / (2.0 * math.pi)))
HPI = float(np.float32(0.5 * math.pi))
NUM_BUCKETS, MAX_DISTANCE = 32, 128
SLABW = 308
JSPLIT = 154
DEBUG = 0


def _bucket(rp):
    nb = NUM_BUCKETS // 2
    rb = nb if rp > 0 else 0
    rel = abs(rp)
    me = nb // 2
    if rel < me:
        return rb + rel
    f = me + int(math.log(max(rel, 1) / me) / math.log(MAX_DISTANCE / me) * (nb - me))
    return rb + min(f, nb - 1)


def _host_constants():
    iota = np.arange(S, dtype=np.float32)[None, :]
    M = np.zeros((NUM_BUCKETS, 128 * SLABW), dtype=np.float32)
    for kp in range(128):
        for jq in range(SLABW):
            d = kp - jq + 90
            base = _bucket(91) if jq < JSPLIT else _bucket(-91)
            idx = kp * SLABW + jq
            M[_bucket(d), idx] += 1.0
            M[base, idx] -= 1.0
    return iota, M


class Prog:
    def __init__(self, nc, stk):
        self.nc = nc
        self.ops = {k: [] for k in ("sync", "vector", "scalar", "tensor", "gpsimd")}
        self.sems = {}
        self.cnt = {}
        for s in ("dl", "st", "mm", "vv", "ss", "cc", "ld", "gp", "gl"):
            self.sems[s] = stk.enter_context(nc.semaphore(s))
            self.cnt[s] = 0

    def rec(self, eng, fn, inc=None):
        tgt = []
        if inc:
            for snm, n in inc:
                self.cnt[snm] += n
                tgt.append((self.sems[snm], n))

        def run(e, fn=fn, tgt=tgt):
            r = fn(e)
            for s, n in tgt:
                r.then_inc(s, n)
        self.ops[eng].append(run)

    def wait(self, eng, snm, val=None):
        v = self.cnt[snm] if val is None else val
        s = self.sems[snm]
        self.ops[eng].append(lambda e, s=s, v=v: e.wait_ge(s, v))
        return v

    def drain(self, eng):
        self.ops[eng].append(lambda e: e.drain())

    def barrier(self):
        if self.cnt["dl"] > 0:
            self.wait("sync", "dl")
            self.rec("sync", lambda E: E.nop(), inc=[("ld", 1)])
        if self.cnt["st"] > 0 or self.cnt["gl"] > 0:
            self.wait("gpsimd", "st")
            self.wait("gpsimd", "gl")
            self.rec("gpsimd", lambda E: E.nop(), inc=[("gp", 1)])
        snap = dict(self.cnt)
        for eng in self.ops:
            for snm in ("mm", "vv", "ss", "cc", "ld", "gp"):
                v = snap[snm]
                if v > 0:
                    self.wait(eng, snm, v)

    def emit(self, blk):
        for eng in ("sync", "vector", "scalar", "tensor", "gpsimd"):
            ops = self.ops[eng]
            getattr(blk, eng)(lambda e, ops=ops: [op(e) for op in ops])


def build_nc(debug=DEBUG):
    nc = bass.Bass("TRN2", target_bir_lowering=False)
    dt = nc.dram_tensor

    x_h = dt("x", [NB, D, S], F32, kind="ExternalInput").ap()
    wq_h = dt("wq", [D, D], F32, kind="ExternalInput").ap()
    wk_h = dt("wk", [D, D], F32, kind="ExternalInput").ap()
    wv_h = dt("wv", [D, D], F32, kind="ExternalInput").ap()
    wo_h = dt("wo", [D, D], F32, kind="ExternalInput").ap()
    rb_h = dt("relb", [NUM_BUCKETS, H], F32, kind="ExternalInput").ap()
    wgate_h = dt("wgate", [I0, D], F32, kind="ExternalInput").ap()
    wup_h = dt("wup", [I0, D], F32, kind="ExternalInput").ap()
    wdown_h = dt("wdown", [D, I0], F32, kind="ExternalInput").ap()
    nulog_h = dt("nulog", [1, D], F32, kind="ExternalInput").ap()
    thlog_h = dt("thlog", [1, D], F32, kind="ExternalInput").ap()
    bre_h = dt("bre", [D, D], F32, kind="ExternalInput").ap()
    bim_h = dt("bim", [D, D], F32, kind="ExternalInput").ap()
    cre_h = dt("cre", [D, D], F32, kind="ExternalInput").ap()
    cim_h = dt("cim", [D, D], F32, kind="ExternalInput").ap()
    dvec_h = dt("dvec", [1, D], F32, kind="ExternalInput").ap()
    gml_h = dt("gml", [1, D], F32, kind="ExternalInput").ap()
    w1_h = dt("w1", [I1, D], F32, kind="ExternalInput").ap()
    cw_h = dt("cw", [I1, 3], F32, kind="ExternalInput").ap()
    cb_h = dt("cb", [1, I1], F32, kind="ExternalInput").ap()
    w2_h = dt("w2", [D, I1], F32, kind="ExternalInput").ap()
    iota_h = dt("iota", [1, S], F32, kind="ExternalInput").ap()
    bandm_h = dt("bandm", [NUM_BUCKETS, 128 * SLABW], F32, kind="ExternalInput").ap()
    ident_h = dt("ident_h", [128, 128], F32, kind="ExternalInput").ap()

    out_h = dt("out", [NB, D, S], F32, kind="ExternalOutput").ap()
    dbg = {}
    if debug:
        for nm in ("d_xn", "d_s", "d_x1", "d_s1", "d_g"):
            dbg[nm] = dt(nm, [NB, D, S], F32, kind="ExternalOutput").ap()
        dbg["d_tiny"] = dt("d_tiny", [128, 64], F32, kind="ExternalOutput").ap()
        dbg["d_upt"] = dt("d_upt", [128, 512], F32, kind="ExternalOutput").ap()
        dbg["d_gtt"] = dt("d_gtt", [128, 512], F32, kind="ExternalOutput").ap()
        dbg["d_ot"] = dt("d_ot", [128, 512], F32, kind="ExternalOutput").ap()
        dbg["d_wq"] = dt("d_wq", [128, 1024], F32, kind="ExternalOutput").ap()
        dbg["d_wst"] = dt("d_wst", [128, 1024], F32, kind="ExternalOutput").ap()
        dbg["d_x0"] = dt("d_x0", [128, 2048], F32, kind="ExternalOutput").ap()

    av_d = dt("av_d", [NB, NTT, D, 512], F32)
    s_d = dt("s_d", [NB, D, S], F32)
    x1_d = dt("x1_d", [NB, D, S], F32)
    up_d = dt("up_d", [NOC0, NB * NTT, 128, 512], F32)
    gt_d = dt("gt_d", [NOC0, NB * NTT, 128, 512], F32)
    t_d = dt("t_d", [NOC0, NB * NTT, 128, 512], F32)
    or_d = dt("or_d", [NOC0, NB * NTT, 128, 512], F32)
    s1_d = dt("s1_d", [NB, D, S], F32)
    h1_d = dt("h1_d", [NB, NHC1, 128, S], F32)
    t1_d = dt("t1_d", [NDC, NHC1, 128, 128], BF16)
    t2_d = dt("t2_d", [NHC1, NDC, 128, 128], BF16)
    wdT_d = dt("wdT_d", [NDC, NOC0, 128, 128], F32)
    xq_d = dt("xq_d", [NB, NDC, 128, S], BF16)
    pat_d = dt("pat_d", [H, 128 * SLABW], F32)
    rows_d = dt("rows_d", [16, S], F32)
    par_d = dt("par_d", [4, D], F32)
    ccb_i = dt("ccb_i", [1, 4], F32)
    ccb_o = dt("ccb_o", [1, 4], F32)

    stk = ExitStack()
    eN = stk.enter_context
    p = Prog(nc, stk)
    sbuf = lambda nm, shape, dtp=F32: eN(nc.sbuf_tensor(nm, shape, dtp))

    A1 = sbuf("A1", [128, 16384])
    A2 = sbuf("A2", [128, 12288])
    A3 = sbuf("A3", [128, 8192])
    A4 = sbuf("A4", [128, 7168])
    BFA = sbuf("BFA", [128, 8192], BF16)
    ones = sbuf("ones", [128, 1])
    onesr = sbuf("onesr", [1, 128])
    zt = sbuf("zt", [128, 512])
    tiny = sbuf("tiny", [128, 64])
    rbs = sbuf("rbs", [NUM_BUCKETS, 64])
    prow = sbuf("prow", [1, 640])
    ident = sbuf("ident", [128, 128])
    tsp = sbuf("tsp", [32, 32])
    tsp2 = sbuf("tsp2", [32, 32])
    psb = [eN(nc.psum_tensor(f"pb{i}", [128, 512], F32)) for i in range(8)]

    def dma(eng, out, in_, sem="dl"):
        p.rec(eng, lambda E: E.dma_start(out=out, in_=in_), inc=[(sem, 16)])

    def dma_acc(out, in_):
        p.rec("gpsimd", lambda E: E.dma_start(out=out, in_=in_, accum_op=Op.add),
              inc=[("st", 16)])

    def chainmax(src, col, first, absv=True):
        vred(tiny[:, 49:50], src, Op.max, absv=absv)
        p.drain("vector")
        if first:
            vcopy(tiny[:, col:col + 1], tiny[:, 49:50], inc=True)
        else:
            vtt(tiny[:, col:col + 1], tiny[:, col:col + 1], tiny[:, 49:50], Op.max,
                inc=True)
        p.drain("vector")

    def ldF():
        p.wait("sync", "dl")
        p.rec("sync", lambda E: E.nop(), inc=[("ld", 1)])

    def stF():
        p.wait("gpsimd", "st")
        p.rec("gpsimd", lambda E: E.nop(), inc=[("gp", 1)])

    def glF():
        p.wait("gpsimd", "gl")
        p.rec("gpsimd", lambda E: E.nop(), inc=[("gp", 1)])

    def mm(out, lhsT, rhs, start, stop, inc=False):
        p.rec("tensor", lambda t: t.matmul(out, lhsT, rhs, start=start, stop=stop),
              inc=[("mm", 1)] if inc else None)

    def ttrans(out, in_, inc=False, k=128):
        p.rec("tensor", lambda t: t.transpose(out, in_, ident[0:k, 0:k]),
              inc=[("mm", 1)] if inc else None)

    def vcopy(out, in_, inc=False):
        p.rec("vector", lambda v: v.tensor_copy(out, in_), inc=[("vv", 1)] if inc else None)

    def vtt(out, in0, in1, op, inc=False):
        p.rec("vector", lambda v: v.tensor_tensor(out=out, in0=in0, in1=in1, op=op),
              inc=[("vv", 1)] if inc else None)

    def vts(out, in0, s1, op0, s2=None, op1=None, inc=False):
        if s2 is None:
            p.rec("vector", lambda v: v.tensor_scalar(out=out, in0=in0, scalar1=s1,
                                                      scalar2=None, op0=op0),
                  inc=[("vv", 1)] if inc else None)
        else:
            p.rec("vector", lambda v: v.tensor_scalar(out=out, in0=in0, scalar1=s1,
                                                      scalar2=s2, op0=op0, op1=op1),
                  inc=[("vv", 1)] if inc else None)

    def vstt(out, in0, scalar, op0, in1, op1, inc=False):
        p.rec("vector", lambda v: v.scalar_tensor_tensor(out=out, in0=in0, scalar=scalar,
                                                         in1=in1, op0=op0, op1=op1),
              inc=[("vv", 1)] if inc else None)

    def vttr(out, in0, in1, op0, op1, scalar, accum, inc=False):
        p.rec("vector", lambda v: v.tensor_tensor_reduce(out=out, in0=in0, in1=in1, scale=1.0,
                                                         scalar=scalar, op0=op0, op1=op1,
                                                         accum_out=accum),
              inc=[("vv", 1)] if inc else None)

    def vred(out, in_, op, absv=False, inc=False):
        p.rec("vector", lambda v: v.tensor_reduce(out=out, in_=in_, axis=AX.X, op=op,
                                                  apply_absolute_value=absv),
              inc=[("vv", 1)] if inc else None)

    def vrecip(out, in_, inc=False):
        p.rec("vector", lambda v: v.reciprocal(out, in_), inc=[("vv", 1)] if inc else None)

    def vmemset(ap, val, inc=False):
        p.rec("vector", lambda v: v.memset(ap, val), inc=[("vv", 1)] if inc else None)

    def vscan(out, data0, data1, inc=False):
        p.rec("vector", lambda v: v.tensor_tensor_scan(out=out, data0=data0, data1=data1,
                                                       initial=0.0, op0=Op.mult, op1=Op.add),
              inc=[("vv", 1)] if inc else None)

    def act(out, in_, func, bias=None, scale=None, inc=False):
        kw = {}
        if bias is not None:
            kw["bias"] = bias
        if scale is not None:
            kw["scale"] = scale
        p.rec("scalar", lambda s: s.activation(out=out, in_=in_, func=func, **kw),
              inc=[("ss", 1)] if inc else None)

    def scopy(out, in_, inc=False):
        p.rec("scalar", lambda s: s.copy(out, in_), inc=[("ss", 1)] if inc else None)

    def ptree(col, ncols=128):
        p.wait("tensor", "vv")
        ttrans(psb[3][0:1, 0:ncols], tiny[0:ncols, col:col + 1], inc=True, k=ncols)
        p.wait("vector", "mm")
        vred(tiny[0:1, col:col + 1], psb[3][0:1, 0:ncols], Op.max)
        p.drain("vector")

    def preduce_row(src32, tmpT, tmpR, dstrow, width):
        J = width // 32
        p.rec("vector", lambda v: v.transpose(tmpT, src32))
        p.drain("vector")
        vmemset(tmpR[:, 0:64], -3.0e38)
        p.drain("vector")
        vred(tmpR[:, 0:J], tmpT.rearrange("p (j i) -> p j i", i=32), Op.max)
        p.drain("vector")
        p.rec("vector", lambda v: v.transpose(tmpR[:, 64:128], tmpR[:, 0:64]))
        p.drain("vector")
        p.rec("vector", lambda v: v.memset(tiny[0:1, 63:64], 0.0), inc=[("vv", 1)])
        p.wait("gpsimd", "vv")
        if J <= 32:
            dma("gpsimd", dstrow.rearrange("a2 (j q) -> a2 j q", q=32),
                tmpR[0:J, 64:96], sem="gl")
        else:
            for k in range(J // 32):
                dma("gpsimd", dstrow[0:1, 1024 * k:1024 * k + 1024]
                    .rearrange("a2 (a q) -> a2 a q", q=32),
                    tmpR[:, 64 + 32 * k:96 + 32 * k], sem="gl")
        glF()
        p.wait("vector", "gp")

    def bcast_row(bank, dst, row, n, np_=128):
        for lo in range(0, n, 512):
            w = min(512, n - lo)
            mm(psb[bank][0:np_, 0:w], onesr[0:1, 0:np_], row[0:1, lo:lo + w], True, True,
               inc=True)
            p.wait("scalar", "mm")
            scopy(dst[:, lo:lo + w], psb[bank][0:np_, 0:w], inc=True)
            p.wait("tensor", "ss")

    def bcast_col(bank, dstcol, srccol):
        mm(psb[bank][0:128, 0:1], onesr[0:1, 0:128], tiny[0:1, srccol:srccol + 1],
           True, True, inc=True)
        p.wait("scalar", "mm")
        scopy(tiny[:, dstcol:dstcol + 1], psb[bank][0:128, 0:1], inc=True)
        p.wait("tensor", "ss")
        p.wait("vector", "ss")

    def col_from_prow(bank, dstcol, off):
        mm(psb[bank][0:128, 0:1], prow[0:1, off:off + 128], ones[0:1, 0:1], True, True,
           inc=True)
        p.wait("scalar", "mm")
        scopy(tiny[:, dstcol:dstcol + 1], psb[bank][0:128, 0:1], inc=True)
        p.wait("tensor", "ss")
        p.wait("vector", "ss")

    vmemset(ones[:, :], 1.0)
    vmemset(onesr[:, :], 1.0)
    vmemset(tiny[:, :], 0.0)
    vmemset(zt[:, :], 0.0, inc=True)
    dma("sync", rbs[:, 0:H], rb_h[:, :])
    dma("sync", rbs[0:1, 32:32 + H], rb_h[31:32, :])
    dma("sync", rbs[0:1, 48:48 + H], rb_h[15:16, :])
    dma("sync", ident[:, :], ident_h[:, :])
    nby = 128 * SLABW
    p.wait("tensor", "vv")
    stg = A4[0:NUM_BUCKETS, 0:512]
    pev = A4[0:H, 512:1024]
    for ci in range((nby + 511) // 512):
        lo = ci * 512
        w = min(512, nby - lo)
        stF(); p.wait("sync", "gp")
        dma("sync", stg[:, 0:w], bandm_h[:, lo:lo + w])
        ldF(); p.wait("tensor", "ld")
        bk = ci % 2
        mm(psb[bk][0:H, 0:w], rbs[:, 0:H], stg[:, 0:w], True, True, inc=True)
        p.wait("scalar", "mm")
        scopy(pev[:, 0:w], psb[bk][0:H, 0:w], inc=True)
        p.wait("gpsimd", "ss")
        dma("gpsimd", pat_d.ap()[:, lo:lo + w], pev[:, 0:w], sem="st")
        stF(); p.wait("scalar", "gp")
        pass
    stF(); p.wait("sync", "gp")
    patt = A3[:, 0:H * SLABW]
    dma("sync", patt.rearrange("p (h j) -> p h j", h=H),
        pat_d.ap().rearrange("h (p j) -> p h j", p=128))
    ldF(); p.wait("tensor", "ld")
    bcast_row(2, tiny[:, 0:H], rbs[0:1, 32:32 + H], H)
    bcast_row(3, tiny[:, H:2 * H], rbs[0:1, 48:48 + H], H)
    p.wait("vector", "ss")
    vred(tiny[0:NUM_BUCKETS, 33:34], rbs[:, 0:H], Op.max, absv=True, inc=True)
    p.drain("vector")
    ptree(33, 32)
    vcopy(tiny[0:1, 34:35], tiny[0:1, 33:34], inc=True)
    p.wait("tensor", "vv")
    bcast_col(2, 34, 34)
    p.barrier()

    xnb = A1
    mbc = A2[:, 8192:10240]
    rbc = A2[:, 10240:12288]
    sxr = A2[0:1, 4096:6144]
    sx2r = A2[0:1, 6144:8192]

    def ln_and_xn(li, b, src_d, maxcol=None, first_max=False):
        xt = A4[:, 0:2048]
        x2t = A4[:, 2048:4096]
        for dc in range(NDC):
            p.wait("sync", "mm")
            p.wait("sync", "vv")
            dma("sync", xt, src_d[b, dc * 128:(dc + 1) * 128, :])
            ldF(); p.wait("scalar", "ld")
            p.wait("scalar", "mm")
            act(x2t, xt, AF.Square, inc=True)
            ldF(); p.wait("tensor", "ld")
            p.wait("tensor", "ss")
            for q in range(NTT):
                mm(psb[q][0:1, 0:512], ones[:, :], xt[:, q * 512:(q + 1) * 512],
                   dc == 0, dc == NDC - 1)
                mm(psb[4 + q][0:1, 0:512], ones[:, :], x2t[:, q * 512:(q + 1) * 512],
                   dc == 0, dc == NDC - 1, inc=(q == NTT - 1))
            ldF(); p.wait("vector", "ld")
            vcopy(xnb[:, dc * 2048:(dc + 1) * 2048], xt, inc=True)
        p.wait("vector", "mm")
        for q in range(NTT):
            vts(sxr[0:1, q * 512:(q + 1) * 512], psb[q][0:1, 0:512], 1.0 / D, Op.mult)
            vts(sx2r[0:1, q * 512:(q + 1) * 512], psb[4 + q][0:1, 0:512], 1.0 / D, Op.mult)
        p.drain("vector")
        p.rec("vector", lambda v: v.memset(tiny[0:1, 63:64], 0.0), inc=[("vv", 1)])
        p.wait("gpsimd", "vv")
        dma("gpsimd", rows_d.ap()[4 * li + 2 * b:4 * li + 2 * b + 1, :], sxr, sem="st")
        p.wait("tensor", "vv")
        bcast_row(6, mbc, sxr, 2048)
        stF(); p.wait("vector", "gp")
        p.wait("vector", "ss")
        vtt(sxr, sxr, sxr, Op.mult)
        p.drain("vector")
        vtt(sx2r, sx2r, sxr, Op.subtract)
        p.drain("vector")
        vts(sx2r, sx2r, 1e-5, Op.add)
        p.drain("vector")
        vrecip(sx2r, sx2r, inc=True)
        p.drain("vector")
        p.wait("scalar", "vv")
        act(sx2r, sx2r, AF.Sqrt, inc=True)
        p.wait("gpsimd", "ss")
        dma("gpsimd", rows_d.ap()[4 * li + 2 * b + 1:4 * li + 2 * b + 2, :], sx2r, sem="st")
        p.wait("tensor", "ss")
        bcast_row(7, rbc, sx2r, 2048)
        p.wait("vector", "ss")
        stF(); p.wait("vector", "gp")
        for dc in range(NDC):
            sl = xnb[:, dc * 2048:(dc + 1) * 2048]
            vtt(sl, sl, mbc, Op.subtract)
            p.drain("vector")
            vtt(sl, sl, rbc, Op.mult)
            p.drain("vector")
            if maxcol is not None:
                chainmax(sl, maxcol, first_max and dc == 0)
        p.rec("vector", lambda v: v.memset(tiny[0:1, 63:64], 0.0), inc=[("vv", 1)])
        if debug and li == 0:
            p.barrier()
            for dc in range(NDC):
                dma("gpsimd", dbg["d_xn"][b, dc * 128:(dc + 1) * 128, :],
                    xnb[:, dc * 2048:(dc + 1) * 2048], sem="st")
            p.barrier()

    def rebuild_xn(li, b, src_d):
        dma("sync", sxr, rows_d.ap()[4 * li + 2 * b:4 * li + 2 * b + 1, :])
        dma("sync", sx2r, rows_d.ap()[4 * li + 2 * b + 1:4 * li + 2 * b + 2, :])
        ldF(); p.wait("tensor", "ld")
        bcast_row(6, mbc, sxr, 2048)
        bcast_row(7, rbc, sx2r, 2048)
        p.wait("vector", "ss")
        for dc in range(NDC):
            xt = xnb[:, dc * 2048:(dc + 1) * 2048]
            p.wait("sync", "vv")
            dma("sync", xt, src_d[b, dc * 128:(dc + 1) * 128, :])
            ldF(); p.wait("vector", "ld")
            vtt(xt, xt, mbc, Op.subtract)
            p.drain("vector")
            vtt(xt, xt, rbc, Op.mult, inc=True)
            p.drain("vector")

    def allreduce_max(cols):
        stgr = A2[0:1, 4096:4100]
        vmemset(stgr, -3.0e38)
        p.drain("vector")
        for i, c in enumerate(cols):
            vcopy(stgr[0:1, i:i + 1], tiny[0:1, c:c + 1])
        p.drain("vector")
        p.rec("vector", lambda v: v.memset(tiny[0:1, 63:64], 0.0), inc=[("vv", 1)])
        p.wait("gpsimd", "vv")
        dma("gpsimd", ccb_i.ap(), stgr, sem="st")
        p.wait("gpsimd", "st")
        p.rec("gpsimd", lambda g: g.collective_compute(
            "AllReduce", Op.max, replica_groups=[list(range(NC))],
            ins=[ccb_i.ap().opt()], outs=[ccb_o.ap().opt()]), inc=[("cc", 1)])
        p.wait("gpsimd", "cc")
        dma("gpsimd", stgr, ccb_o.ap(), sem="gl")
        glF()
        p.wait("vector", "gp")
        for i, c in enumerate(cols):
            vcopy(tiny[0:1, c:c + 1], stgr[0:1, i:i + 1])
        p.drain("vector")
        p.rec("vector", lambda v: v.memset(tiny[0:1, 63:64], 0.0), inc=[("vv", 1)])
        p.barrier()

    def finish_scales(specs):
        for i, (src, _, _, _, _) in enumerate(specs):
            ptree(src)
            vcopy(tiny[0:1, 60 + i:61 + i], tiny[0:1, src:src + 1], inc=True)
            p.drain("vector")
        allreduce_max([60 + i for i in range(len(specs))])
        for i, (_, scol, icol, qmax, floor) in enumerate(specs):
            c = 60 + i
            vts(tiny[0:1, c:c + 1], tiny[0:1, c:c + 1], 1.0 / qmax, Op.mult)
            p.drain("vector")
            vts(tiny[0:1, c:c + 1], tiny[0:1, c:c + 1], floor, Op.max)
            p.drain("vector")
            vrecip(tiny[0:1, 58:59], tiny[0:1, c:c + 1], inc=True)
            p.drain("vector")
            p.wait("tensor", "vv")
            mm(psb[2][0:128, 0:1], onesr[0:1, 0:128], tiny[0:1, c:c + 1], True, True, inc=True)
            mm(psb[3][0:128, 0:1], onesr[0:1, 0:128], tiny[0:1, 58:59], True, True, inc=True)
            p.wait("scalar", "mm")
            scopy(tiny[:, scol:scol + 1], psb[2][0:128, 0:1], inc=True)
            scopy(tiny[:, icol:icol + 1], psb[3][0:128, 0:1], inc=True)
            p.wait("vector", "ss")
            p.wait("tensor", "ss")
        p.barrier()

    def attention(b):
        p.barrier()
        qhi = BFA[:, 0:2048]
        qlo = BFA[:, 2048:4096]
        khi = BFA[:, 4096:6144]
        klo = BFA[:, 6144:8192]
        vbuf = A3[:, 4992:4992 + NKT * 130]
        raws = A2[:, 0:NKT * 512]
        wte = A4[:, 0:1024]
        wpair = A4[:, 1024:4096]
        avs = A4[:, 4096:4608]
        avbc = A4[:, 4608:5120]
        rrow_ = A4[0:1, 5120:5632]
        for hp in range(H // 2):
            p.wait("sync", "mm")
            dma("sync", wpair[:, 0:1024].rearrange("p (c m) -> p c m", c=NDC),
                wq_h.rearrange("(c p) m -> p c m", p=128)[:, :, hp * 128:(hp + 1) * 128])
            dma("sync", wpair[:, 1024:2048].rearrange("p (c m) -> p c m", c=NDC),
                wk_h.rearrange("(c p) m -> p c m", p=128)[:, :, hp * 128:(hp + 1) * 128])
            dma("sync", wpair[:, 2048:3072].rearrange("p (c m) -> p c m", c=NDC),
                wv_h.rearrange("(c p) m -> p c m", p=128)[:, :, hp * 128:(hp + 1) * 128])
            ldF(); p.wait("tensor", "ld")
            p.wait("tensor", "vv")
            p.wait("tensor", "ss")
            for q in range(NTT):
                for dc in range(NDC):
                    mm(psb[0][:, 0:512], wpair[:, dc * 128:(dc + 1) * 128],
                       xnb[:, dc * 2048 + q * 512:dc * 2048 + q * 512 + 512],
                       dc == 0, dc == NDC - 1, inc=(dc == NDC - 1))
                for dc in range(NDC):
                    mm(psb[1][:, 0:512], wpair[:, 1024 + dc * 128:1024 + (dc + 1) * 128],
                       xnb[:, dc * 2048 + q * 512:dc * 2048 + q * 512 + 512],
                       dc == 0, dc == NDC - 1, inc=(dc == NDC - 1))
                p.wait("vector", "mm")
                sl = slice(q * 512, q * 512 + 512)
                vcopy(qhi[:, sl], psb[0][:, 0:512])
                vcopy(khi[:, sl], psb[1][:, 0:512])
                p.drain("vector")
                vtt(qlo[:, sl], psb[0][:, 0:512], qhi[:, sl], Op.subtract)
                vtt(klo[:, sl], psb[1][:, 0:512], khi[:, sl], Op.subtract, inc=True)
                p.drain("vector")
                p.wait("tensor", "vv")
            vmemset(vbuf, 1.0, inc=True)
            p.wait("tensor", "vv")
            for kt in range(NKT):
                for dc in range(NDC):
                    mm(psb[2][:, 0:128],
                       xnb[:, dc * 2048 + kt * 128:dc * 2048 + kt * 128 + 128],
                       wpair[:, 2048 + dc * 128:2048 + (dc + 1) * 128],
                       dc == 0, dc == NDC - 1, inc=(dc == NDC - 1))
                p.wait("vector", "mm")
                vcopy(vbuf[:, kt * 130:kt * 130 + 64], psb[2][:, 0:64])
                vcopy(vbuf[:, kt * 130 + 65:kt * 130 + 129], psb[2][:, 64:128], inc=True)
                p.drain("vector")
                p.wait("tensor", "vv")
            for hh in range(2):
                h = 2 * hp + hh
                hsl = slice(64 * hh, 64 * hh + 64)
                for qb in range(NTT):
                    q0 = qb * 512
                    for kt in range(NKT):
                        bk = kt % 2
                        p.wait("tensor", "vv")
                        trip = ((khi, qhi), (khi, qlo), (klo, qhi))
                        for ti, (ka, qa) in enumerate(trip):
                            mm(psb[bk][:, 0:512], ka[hsl, kt * 128:kt * 128 + 128],
                               qa[hsl, q0:q0 + 512], ti == 0, ti == 2, inc=(ti == 2))
                        p.wait("scalar", "mm")
                        p.wait("scalar", "vv")
                        scopy(raws[:, kt * 512:kt * 512 + 512], psb[bk][:, 0:512], inc=True)
                        p.wait("vector", "ss")
                        chainmax(raws[:, kt * 512:kt * 512 + 512], 32, kt == 0, absv=False)
                    for kt in range(NKT):
                        k0 = kt * 128
                        lo = max(k0 - 90, q0)
                        hi = min(k0 + 218, q0 + 512, S)
                        if lo < hi:
                            joff = lo - (k0 - 90)
                            vtt(raws[:, kt * 512 + (lo - q0):kt * 512 + (hi - q0)],
                                raws[:, kt * 512 + (lo - q0):kt * 512 + (hi - q0)],
                                patt[:, h * SLABW + joff:h * SLABW + joff + (hi - lo)],
                                Op.add)
                            p.drain("vector")
                    ptree(32)
                    vcopy(tiny[0:1, 35:36], tiny[0:1, 32:33], inc=True)
                    p.wait("tensor", "vv")
                    bcast_col(3, 35, 35)
                    vtt(tiny[:, 36:37], tiny[:, 35:36], tiny[:, 34:35], Op.add)
                    p.drain("vector")
                    vtt(tiny[:, 37:38], tiny[:, h:h + 1], tiny[:, 36:37], Op.subtract)
                    vtt(tiny[:, 38:39], tiny[:, H + h:H + h + 1], tiny[:, 36:37],
                        Op.subtract, inc=True)
                    p.drain("vector")
                    p.wait("scalar", "vv")
                    for kt in range(NKT):
                        k0 = kt * 128
                        wsl = slice((kt % 2) * 512, (kt % 2) * 512 + 512)
                        p.wait("scalar", "mm")
                        p.drain("scalar")
                        split = min(max(k0 + 64 - q0, 0), 512)
                        if split > 0:
                            act(wte[:, wsl][:, 0:split], raws[:, kt * 512:kt * 512 + split],
                                AF.Exp, bias=tiny[:, 37:38], inc=(split == 512))
                        if split < 512:
                            act(wte[:, wsl][:, split:512],
                                raws[:, kt * 512 + split:kt * 512 + 512],
                                AF.Exp, bias=tiny[:, 38:39], inc=True)
                        p.wait("tensor", "ss")
                        mm(psb[4][0:65, 0:512],
                           vbuf[:, kt * 130 + 65 * hh:kt * 130 + 65 * hh + 65],
                           wte[:, wsl], kt == 0, kt == NKT - 1, inc=True)
                    p.wait("vector", "mm")
                    vrecip(rrow_, psb[4][64:65, 0:512], inc=True)
                    p.wait("tensor", "vv")
                    bcast_row(5, avbc[0:64, :], rrow_, 512, np_=64)
                    p.wait("scalar", "vv")
                    scopy(avs[0:64, :], psb[4][0:64, 0:512], inc=True)
                    p.wait("vector", "ss")
                    vtt(avs[0:64, :], avs[0:64, :], avbc[0:64, :], Op.mult, inc=True)
                    p.drain("vector")
                    p.wait("gpsimd", "vv")
                    dma("gpsimd", av_d.ap()[b, qb, h * 64:(h + 1) * 64, :], avs[0:64, :],
                        sem="st")
                    stF(); p.wait("vector", "gp")
                    stF(); p.wait("scalar", "gp")
                    stF(); p.wait("tensor", "gp")
        p.barrier()
        avl = A4[:, 0:4096]
        wos = A4[:, 4096:5120]
        wev = A4[:, 5120:5632]
        for qb in range(NTT):
            p.wait("sync", "mm")
            for pc in range(NDC):
                dma("sync", avl[:, pc * 512:(pc + 1) * 512],
                    av_d.ap()[b, qb, pc * 128:(pc + 1) * 128, :])
            for dc in range(NDC):
                p.wait("sync", "mm")
                dma("sync", wos.rearrange("p (c m) -> p c m", c=NDC),
                    wo_h.rearrange("(c p) m -> p c m", p=128)[:, :, dc * 128:(dc + 1) * 128])
                ldF(); p.wait("tensor", "ld")
                for pc in range(NDC):
                    mm(psb[dc % 2][:, 0:512], wos[:, pc * 128:(pc + 1) * 128],
                       avl[:, pc * 512:(pc + 1) * 512], pc == 0, pc == NDC - 1,
                       inc=(pc == NDC - 1))
                p.wait("scalar", "mm")
                stF(); p.wait("scalar", "gp")
                scopy(wev, psb[dc % 2][:, 0:512], inc=True)
                p.wait("gpsimd", "ss")
                dma("gpsimd", s_d.ap()[b, dc * 128:(dc + 1) * 128, qb * 512:(qb + 1) * 512],
                    wev, sem="st")
                stF(); p.wait("tensor", "gp")
        if debug:
            p.barrier()
            for dc in range(NDC):
                dma("gpsimd", dbg["d_s"][b, dc * 128:(dc + 1) * 128, :],
                    s_d.ap()[b, dc * 128:(dc + 1) * 128, :], sem="st")
            p.barrier()

    def dequant_tile(stage, nin, sblk, sfull, trash):
        nb_ = nin // 32
        vred(sblk[:, 0:nb_], stage.rearrange("p (nb bs) -> p nb bs", bs=32), Op.max,
             absv=True)
        p.drain("vector")
        for j in range(32):
            vcopy(sfull[:, 0:nin].rearrange("p (nb bs) -> p nb bs", bs=32)[:, :, j:j + 1],
                  sblk[:, 0:nb_].unsqueeze(2))
        p.drain("vector")
        vrecip(trash[:, 0:nin], sfull[:, 0:nin])
        p.drain("vector")
        vstt(stage.bitcast(I32), stage, 7.0, Op.mult, trash[:, 0:nin], Op.mult)
        p.drain("vector")
        vstt(stage, stage.bitcast(I32), 1.0 / 7.0, Op.mult, sfull[:, 0:nin], Op.mult,
             inc=True)
        p.drain("vector")

    def transpose_chunks(stage, nin, cache):
        for ic in range(nin // 128):
            bk = 2 + (ic % 2)
            p.wait("tensor", "ss")
            ttrans(psb[bk][:, 0:128], stage[:, ic * 128:(ic + 1) * 128], inc=True)
            p.wait("scalar", "mm")
            scopy(cache[:, ic * 128:(ic + 1) * 128], psb[bk][:, 0:128], inc=True)
        p.wait("tensor", "ss")

    def ffn0_matmuls():
        p.barrier()
        wstage = A4[:, 0:1024]
        sblk = A4[:, 1024:1056]
        sfull = A4[:, 2048:3072]
        trash = A4[:, 3072:4096]
        cache = A4[:, 4096:5120]
        evs = A4[:, 5120:5632]
        trash2 = A4[:, 5632:6144]
        first_chain = [True, True]
        for b in (1, 0):
            if b == 0:
                rebuild_xn(0, 0, x_h)
            for dc in range(NDC):
                sl = xnb[:, dc * 2048:(dc + 1) * 2048]
                vts(sl.bitcast(I32), sl, tiny[:, 43:44], Op.mult)
                p.drain("vector")
                vts(sl, sl.bitcast(I32), tiny[:, 42:43], Op.mult, inc=True)
                p.drain("vector")
            for iw, (w_h_, sp_d, mcol) in enumerate(((wup_h, up_d, 44), (wgate_h, gt_d, 45))):
                for oc in range(NOC0):
                    p.wait("sync", "mm")
                    p.wait("sync", "vv")
                    dma("sync", wstage, w_h_[oc * 128:(oc + 1) * 128, :])
                    ldF(); p.wait("vector", "ld")
                    p.wait("vector", "mm")
                    dequant_tile(wstage, D, sblk, sfull, trash)
                    if debug and b == 0 and iw == 0 and oc == 0:
                        p.wait("gpsimd", "vv")
                        dma("gpsimd", dbg["d_wst"][:, :], wstage, sem="st")
                        dma("gpsimd", dbg["d_x0"][:, :], xnb[:, 0:2048], sem="st")
                        stF(); p.wait("vector", "gp")
                    p.wait("tensor", "vv")
                    transpose_chunks(wstage, D, cache)
                    if debug and b == 0 and iw == 0 and oc == 0:
                        p.wait("gpsimd", "ss")
                        dma("gpsimd", dbg["d_wq"][:, :], cache, sem="st")
                        stF(); p.wait("scalar", "gp")
                    for qb in range(NTT):
                        bk = qb % 2
                        for ic in range(NDC):
                            mm(psb[bk][:, 0:512], cache[:, ic * 128:(ic + 1) * 128],
                               xnb[:, ic * 2048 + qb * 512:ic * 2048 + qb * 512 + 512],
                               ic == 0, ic == NDC - 1, inc=(ic == NDC - 1))
                        p.wait("scalar", "mm")
                        stF(); p.wait("scalar", "gp")
                        scopy(evs, psb[bk][:, 0:512], inc=True)
                        p.wait("vector", "ss")
                        chainmax(evs, mcol, first_chain[iw])
                        first_chain[iw] = False
                        p.wait("gpsimd", "vv")
                        dma("gpsimd", sp_d.ap()[oc, b * NTT + qb, :, :], evs, sem="st")
                        p.wait("tensor", "vv")

    def ffn0_elemwise():
        p.barrier()
        ga = A4[:, 0:512]
        gb = A4[:, 512:1024]
        gc = A4[:, 1024:1536]
        up = A4[:, 1536:2048]
        trash = A4[:, 2048:2560]
        first = True
        for oc in range(NOC0):
            for tq in range(NB * NTT):
                p.wait("sync", "vv")
                stF(); p.wait("sync", "gp")
                dma("sync", ga, gt_d.ap()[oc, tq, :, :])
                ldF(); p.wait("vector", "ld")
                vts(ga.bitcast(I32), ga, tiny[:, 53:54], Op.mult)
                p.drain("vector")
                vts(ga, ga.bitcast(I32), tiny[:, 52:53], Op.mult, inc=True)
                p.drain("vector")
                p.wait("scalar", "vv")
                act(gb, ga, AF.Sigmoid, inc=True)
                p.wait("vector", "ss")
                vts(gb.bitcast(I32), gb, 65536.0, Op.mult)
                p.drain("vector")
                vts(gb, gb.bitcast(I32), 65535.0, Op.min)
                p.drain("vector")
                vts(gb, gb, 1.0 / 65536.0, Op.mult)
                p.drain("vector")
                vtt(gc, ga, gb, Op.mult)
                p.drain("vector")
                chainmax(gc, 46, first)
                first = False
                p.wait("gpsimd", "vv")
                dma("gpsimd", t_d.ap()[oc, tq, :, :], gc, sem="st")
                stF(); p.wait("vector", "gp")
        finish_scales([(46, 54, 55, 32767.0, 1e-12)])
        first = True
        for oc in range(NOC0):
            for tq in range(NB * NTT):
                p.wait("sync", "vv")
                stF(); p.wait("sync", "gp")
                dma("sync", ga, t_d.ap()[oc, tq, :, :])
                dma("sync", up, up_d.ap()[oc, tq, :, :])
                ldF(); p.wait("vector", "ld")
                vts(ga.bitcast(I32), ga, tiny[:, 55:56], Op.mult)
                p.drain("vector")
                vts(ga, ga.bitcast(I32), tiny[:, 54:55], Op.mult)
                vts(up.bitcast(I32), up, tiny[:, 51:52], Op.mult)
                p.drain("vector")
                vts(up, up.bitcast(I32), tiny[:, 50:51], Op.mult)
                p.drain("vector")
                vtt(gc, ga, up, Op.mult)
                p.drain("vector")
                chainmax(gc, 47, first)
                first = False
                p.wait("gpsimd", "vv")
                dma("gpsimd", or_d.ap()[oc, tq, :, :], gc, sem="st")
                stF(); p.wait("vector", "gp")
        finish_scales([(47, 56, 57, 32767.0, 1e-12)])

    def prep_wdownT():
        p.barrier()
        wstage = A4[:, 0:3072]
        sblk = A4[:, 3072:3168]
        sfull = A2[:, 0:3072]
        trash = A2[:, 3072:6144]
        cachec = A2[:, 6144:6144 + 3072]
        for dc in range(NDC):
            p.wait("sync", "vv")
            stF(); p.wait("sync", "gp")
            dma("sync", wstage, wdown_h[dc * 128:(dc + 1) * 128, :])
            ldF(); p.wait("vector", "ld")
            stF(); p.wait("vector", "gp")
            dequant_tile(wstage, I0, sblk, sfull, trash)
            p.wait("tensor", "vv")
            transpose_chunks(wstage, I0, cachec)
            p.wait("gpsimd", "ss")
            dma("gpsimd", wdT_d.ap()[dc, :, :, :].rearrange("c p m -> p c m"),
                cachec.rearrange("p (c m) -> p c m", c=NOC0), sem="st")
            stF(); p.wait("scalar", "gp")

    def ffn0_down_combine():
        p.barrier()
        gtl = A4[:, 3712:4224]
        xtl = A4[:, 4224:4736]
        stl = A4[:, 4736:5248]
        wdc0 = A4[:, 0:3072]
        oq = A1
        for b in range(NB):
            for qb in range(NTT):
                tq = b * NTT + qb
                for oc in range(NOC0):
                    sl = oq[:, oc * 512:(oc + 1) * 512]
                    p.wait("sync", "vv")
                    dma("sync", sl, or_d.ap()[oc, tq, :, :])
                    ldF(); p.wait("vector", "ld")
                    vts(sl.bitcast(I32), sl, tiny[:, 57:58], Op.mult)
                    p.drain("vector")
                    vts(sl, sl.bitcast(I32), tiny[:, 56:57], Op.mult, inc=True)
                    p.drain("vector")
                for dc in range(NDC):
                    p.wait("sync", "mm")
                    dma("sync", wdc0.rearrange("p (c m) -> p c m", c=NOC0),
                        wdT_d.ap()[dc, :, :, :].rearrange("c p m -> p c m"))
                    ldF(); p.wait("tensor", "ld")
                    p.wait("tensor", "vv")
                    bk = dc % 2
                    for oc in range(NOC0):
                        mm(psb[bk][:, 0:512], wdc0[:, oc * 128:(oc + 1) * 128],
                           oq[:, oc * 512:(oc + 1) * 512], oc == 0, oc == NOC0 - 1,
                           inc=(oc == NOC0 - 1))
                    p.wait("scalar", "mm")
                    p.wait("scalar", "vv")
                    act(gtl, psb[bk][:, 0:512], AF.Silu, inc=True)
                    p.wait("sync", "ss")
                    dma("sync", xtl, x_h[b, dc * 128:(dc + 1) * 128, qb * 512:(qb + 1) * 512])
                    dma("sync", stl, s_d.ap()[b, dc * 128:(dc + 1) * 128, qb * 512:(qb + 1) * 512])
                    ldF(); p.wait("vector", "ld")
                    p.wait("vector", "ss")
                    vtt(gtl, gtl, stl, Op.mult)
                    p.drain("vector")
                    vtt(gtl, gtl, xtl, Op.add, inc=True)
                    p.drain("vector")
                    p.wait("gpsimd", "vv")
                    dma("gpsimd", x1_d.ap()[b, dc * 128:(dc + 1) * 128, qb * 512:(qb + 1) * 512],
                        gtl, sem="st")
                    if debug:
                        dma("gpsimd", dbg["d_x1"][b, dc * 128:(dc + 1) * 128,
                                                  qb * 512:(qb + 1) * 512], gtl, sem="st")
                    stF(); p.wait("vector", "gp")
                    stF(); p.wait("scalar", "gp")

    def prep_ternary(w_h_, nrows, ncols, wscol, t_dram):
        p.barrier()
        stage = A4[:, 0:ncols]
        astage = A2[:, 0:ncols]
        srow = A2[0:1, 4096:4097]
        nrc = nrows // 128
        for rc in range(nrc):
            p.wait("sync", "mm")
            p.wait("sync", "ss")
            dma("sync", stage, w_h_[rc * 128:(rc + 1) * 128, :])
            ldF(); p.wait("scalar", "ld")
            p.wait("scalar", "mm")
            act(astage, stage, AF.Abs, inc=True)
            p.wait("tensor", "ss")
            for lo in range(0, ncols, 512):
                w = min(512, ncols - lo)
                mm(psb[6][0:1, 0:w], ones[:, :], astage[:, lo:lo + w],
                   rc == 0 and lo == 0, rc == nrc - 1 and lo + 512 >= ncols,
                   inc=(lo + 512 >= ncols))
        p.wait("vector", "mm")
        vred(srow, psb[6][0:1, 0:512], Op.add)
        p.drain("vector")
        vts(srow, srow, 1.0 / (nrows * ncols), Op.mult)
        p.drain("vector")
        vts(srow, srow, 1e-5, Op.max)
        p.drain("vector")
        vcopy(tiny[0:1, wscol + 1:wscol + 2], srow)
        p.drain("vector")
        vrecip(tiny[0:1, wscol:wscol + 1], srow, inc=True)
        p.drain("vector")
        p.wait("tensor", "vv")
        bcast_col(2, wscol + 2, wscol)
        bcast_col(3, wscol + 6, wscol + 1)
        bstage = BFA[:, 0:512]
        for rc in range(nrc):
            p.wait("sync", "vv")
            stF(); p.wait("sync", "gp")
            dma("sync", stage, w_h_[rc * 128:(rc + 1) * 128, :])
            ldF(); p.wait("vector", "ld")
            p.wait("vector", "mm")
            vts(stage.bitcast(I32), stage, tiny[:, wscol + 2:wscol + 3], Op.mult)
            p.drain("vector")
            vts(stage, stage.bitcast(I32), 1.0, Op.min)
            p.drain("vector")
            vts(stage, stage, -1.0, Op.max, inc=True)
            p.drain("vector")
            p.wait("tensor", "vv")
            for ic in range(ncols // 128):
                bk = 2 + (ic % 2)
                stF(); p.wait("tensor", "gp")
                ttrans(psb[bk][:, 0:128], stage[:, ic * 128:(ic + 1) * 128], inc=True)
                p.wait("vector", "mm")
                vcopy(bstage[:, 0:128], psb[bk][:, 0:128], inc=True)
                p.drain("vector")
                p.wait("gpsimd", "vv")
                dma("gpsimd", t_dram.ap()[ic, rc, :, :], bstage[:, 0:128], sem="st")
                stF(); p.wait("vector", "gp")

    def lru(b):
        p.barrier()
        cosb = A3[:, 0:2048]
        sinb = A3[:, 2048:4096]
        u_re = A3[:, 4096:6144]
        u_im = A3[:, 6144:8192]
        w_re = A4[:, 0:2048]
        w_im = A4[:, 2048:4096]
        rbt = A4[:, 4096:6144]
        ring = A4[:, 6144:6656]
        ev = A4[:, 6656:7168]
        tr1 = A2[:, 0:512]
        tr2 = A2[:, 512:1024]
        kint = A2[:, 1024:1536]
        rem = A2[:, 1536:2048]
        iorow = A2[0:1, 2048:2560]
        for pc in range(8):
            p.wait("sync", "vv")
            p.wait("sync", "ss")
            dma("sync", prow[0:1, 0:128], par_d.ap()[0:1, pc * 128:pc * 128 + 128])
            dma("sync", prow[0:1, 128:256], par_d.ap()[2:3, pc * 128:pc * 128 + 128])
            dma("sync", prow[0:1, 256:384], par_d.ap()[1:2, pc * 128:pc * 128 + 128])
            ldF(); p.wait("tensor", "ld")
            col_from_prow(4, 7, 0)
            col_from_prow(4, 8, 128)
            vts(tiny[:, 9:10], tiny[:, 8:9], -1.0, Op.mult)
            p.drain("vector")
            vmemset(rbt, 1.0)
            p.drain("vector")
            vts(rbt, rbt, tiny[:, 7:8], Op.mult, inc=True)
            p.drain("vector")
            for nt in range(NTT):
                tsl = slice(nt * 512, nt * 512 + 512)
                p.wait("sync", "mm")
                dma("sync", iorow, iota_h[0:1, tsl])
                ldF(); p.wait("tensor", "ld")
                p.wait("tensor", "vv")
                p.wait("tensor", "ss")
                mm(psb[6][0:128, 0:512], prow[0:1, 256:384], iorow, True, True, inc=True)
                p.wait("vector", "mm")
                vts(kint.bitcast(I32), psb[6][:, 0:512], INV2PI, Op.mult)
                p.drain("vector")
                vstt(rem, kint.bitcast(I32), -TWO_PI_HI, Op.mult, psb[6][:, 0:512], Op.add)
                p.drain("vector")
                vstt(rem, kint.bitcast(I32), -TWO_PI_LO, Op.mult, rem, Op.add)
                p.drain("vector")
                vts(rem, rem, 3.141592, Op.min)
                p.drain("vector")
                vts(rem, rem, -3.141592, Op.max, inc=True)
                p.drain("vector")
                p.wait("scalar", "vv")
                act(sinb[:, tsl], rem, AF.Sin, inc=True)
                p.wait("vector", "ss")
                vts(kint.bitcast(I32), psb[6][:, 0:512], INV2PI, Op.mult, 0.25, Op.add)
                p.drain("vector")
                vstt(rem, kint.bitcast(I32), -TWO_PI_HI, Op.mult, psb[6][:, 0:512], Op.add)
                p.drain("vector")
                vstt(rem, kint.bitcast(I32), -TWO_PI_LO, Op.mult, rem, Op.add)
                p.drain("vector")
                vts(rem, rem, HPI, Op.add)
                p.drain("vector")
                vts(rem, rem, 3.141592, Op.min)
                p.drain("vector")
                vts(rem, rem, -3.141592, Op.max, inc=True)
                p.drain("vector")
                p.wait("scalar", "vv")
                act(cosb[:, tsl], rem, AF.Sin, inc=True)
                p.wait("vector", "ss")
            p.wait("tensor", "ss")
            for hc in range(NDC):
                p.wait("sync", "mm")
                dma("sync", ring[:, 0:128],
                    bre_h[hc * 128:(hc + 1) * 128, pc * 128:(pc + 1) * 128])
                dma("sync", ring[:, 128:256],
                    bim_h[hc * 128:(hc + 1) * 128, pc * 128:(pc + 1) * 128])
                ldF(); p.wait("tensor", "ld")
                for nt in range(NTT):
                    xsl = xnb[:, hc * 2048 + nt * 512:hc * 2048 + nt * 512 + 512]
                    mm(psb[nt][:, 0:512], ring[:, 0:128], xsl, hc == 0, hc == NDC - 1)
                    mm(psb[4 + nt][:, 0:512], ring[:, 128:256], xsl, hc == 0, hc == NDC - 1,
                       inc=(nt == NTT - 1))
            p.wait("vector", "mm")
            for nt in range(NTT):
                tsl = slice(nt * 512, nt * 512 + 512)
                vtt(tr1, psb[nt][:, 0:512], cosb[:, tsl], Op.mult)
                vtt(tr2, psb[4 + nt][:, 0:512], sinb[:, tsl], Op.mult)
                p.drain("vector")
                vtt(u_re[:, tsl], tr1, tr2, Op.add)
                p.drain("vector")
                vtt(tr1, psb[4 + nt][:, 0:512], cosb[:, tsl], Op.mult)
                vtt(tr2, psb[nt][:, 0:512], sinb[:, tsl], Op.mult)
                p.drain("vector")
                vtt(u_im[:, tsl], tr1, tr2, Op.subtract, inc=True)
                p.drain("vector")
            p.wait("tensor", "vv")
            vscan(w_re, rbt, u_re)
            p.drain("vector")
            vscan(w_im, rbt, u_im, inc=True)
            p.drain("vector")
            for nt in range(NTT):
                tsl = slice(nt * 512, nt * 512 + 512)
                vtt(tr1, w_re[:, tsl], cosb[:, tsl], Op.mult)
                vtt(tr2, w_im[:, tsl], sinb[:, tsl], Op.mult)
                p.drain("vector")
                vtt(u_re[:, tsl], tr1, tr2, Op.subtract)
                p.drain("vector")
                vtt(tr1, w_im[:, tsl], cosb[:, tsl], Op.mult)
                vtt(tr2, w_re[:, tsl], sinb[:, tsl], Op.mult)
                p.drain("vector")
                vtt(u_im[:, tsl], tr1, tr2, Op.add, inc=True)
                p.drain("vector")
            for hc in range(NDC):
                p.wait("sync", "mm")
                p.wait("sync", "vv")
                dma("sync", ring[:, 256:384],
                    cre_h[pc * 128:(pc + 1) * 128, hc * 128:(hc + 1) * 128])
                dma("sync", ring[:, 384:512],
                    cim_h[pc * 128:(pc + 1) * 128, hc * 128:(hc + 1) * 128])
                if pc == 0:
                    p.wait("sync", "ss")
                    dma("sync", prow[0:1, 384:512],
                        par_d.ap()[3:4, hc * 128:hc * 128 + 128])
                ldF(); p.wait("vector", "ld")
                vts(ring[:, 256:384], ring[:, 256:384], tiny[:, 8:9], Op.mult)
                p.drain("vector")
                vts(ring[:, 384:512], ring[:, 384:512], tiny[:, 9:10], Op.mult, inc=True)
                p.drain("vector")
                if pc == 0:
                    ldF(); p.wait("tensor", "ld")
                    col_from_prow(5, 11, 384)
                p.wait("tensor", "vv")
                for nt in range(NTT):
                    tsl = slice(nt * 512, nt * 512 + 512)
                    bk = 6 + (nt % 2)
                    mm(psb[bk][:, 0:512], ring[:, 256:384], u_re[:, tsl], True, False)
                    mm(psb[bk][:, 0:512], ring[:, 384:512], u_im[:, tsl], False, True,
                       inc=True)
                    p.wait("scalar", "mm")
                    stF(); p.wait("scalar", "gp")
                    scopy(ev, psb[bk][:, 0:512], inc=True)
                    if pc == 0:
                        p.wait("vector", "ss")
                        vstt(ev, xnb[:, hc * 2048 + nt * 512:hc * 2048 + nt * 512 + 512],
                             tiny[:, 11:12], Op.mult, ev, Op.add, inc=True)
                        p.drain("vector")
                        p.wait("gpsimd", "vv")
                        dma("gpsimd", s1_d.ap()[b, hc * 128:(hc + 1) * 128,
                                                nt * 512:nt * 512 + 512], ev, sem="st")
                        stF(); p.wait("vector", "gp")
                    else:
                        p.wait("gpsimd", "ss")
                        dma_acc(s1_d.ap()[b, hc * 128:(hc + 1) * 128,
                                          nt * 512:nt * 512 + 512], ev)
                        stF(); p.wait("scalar", "gp")
                    stF(); p.wait("tensor", "gp")
        if debug:
            p.barrier()
            for dc in range(NDC):
                dma("gpsimd", dbg["d_s1"][b, dc * 128:(dc + 1) * 128, :],
                    s1_d.ap()[b, dc * 128:(dc + 1) * 128, :], sem="st")
            p.barrier()

    def ffn1(b):
        p.barrier()
        scb = A2[:, 0:512]
        sq = A2[:, 512:1024]
        tmax = A2[:, 1024:1536]
        fbc_all = A2[:, 4608:6656]
        for qb in range(NTT):
            tsl = slice(qb * 512, qb * 512 + 512)
            for dc in range(NDC):
                sl = xnb[:, dc * 2048 + qb * 512:dc * 2048 + qb * 512 + 512]
                if dc == 0:
                    vtt(tmax, sl, sl, Op.mult)
                else:
                    vtt(sq, sl, sl, Op.mult)
                    p.drain("vector")
                    vtt(tmax, tmax, sq, Op.max)
                p.drain("vector")
            p.rec("vector", lambda v: v.memset(tiny[0:1, 63:64], 0.0), inc=[("vv", 1)])
            p.wait("tensor", "vv")
            for k in range(4):
                ttrans(psb[6][:, k * 128:(k + 1) * 128], tmax[:, k * 128:(k + 1) * 128],
                       inc=(k == 3))
            p.wait("vector", "mm")
            mcol = A2[:, 2048:2052]
            vred(mcol, psb[6][:, 0:512].rearrange("p (k i) -> p k i", i=128), Op.max,
                 inc=True)
            p.drain("vector")
            p.wait("scalar", "vv")
            act(mcol, mcol, AF.Sqrt, inc=True)
            p.wait("vector", "ss")
            vts(mcol, mcol, 1e-5, Op.max)
            p.drain("vector")
            fcol = A2[:, 2052:2056]
            vts(fcol, mcol, 1.0 / 127.0, Op.mult)
            p.drain("vector")
            vts(fcol, fcol, tiny[:, 26:27], Op.mult)
            p.drain("vector")
            vrecip(mcol, mcol)
            p.drain("vector")
            vts(mcol, mcol, 127.0, Op.mult, inc=True)
            p.drain("vector")
            p.wait("tensor", "vv")
            ttrans(psb[6][0:4, 0:128], mcol)
            ttrans(psb[7][0:4, 0:128], fcol, inc=True)
            p.wait("scalar", "mm")
            scrows = A2[0:4, 2176:2304]
            frows = A2[0:4, 2304:2432]
            scopy(scrows, psb[6][0:4, 0:128])
            scopy(frows, psb[7][0:4, 0:128], inc=True)
            p.wait("gpsimd", "ss")
            scrow_r = A2[0:1, 2432:2944]
            frow_r = A2[0:1, 2944:3456]
            dma("gpsimd", scrow_r.rearrange("a (k i) -> a k i", i=128), scrows.unsqueeze(2).transpose([0, 2, 1]) if False else scrows, sem="gl")
            dma("gpsimd", frow_r.rearrange("a (k i) -> a k i", i=128), frows, sem="gl")
            glF()
            p.wait("tensor", "gp")
            p.wait("vector", "gp")
            bcast_row(4, scb, scrow_r, 512)
            bcast_row(5, fbc_all[:, qb * 512:qb * 512 + 512], frow_r, 512)
            p.wait("vector", "ss")
            for dc in range(NDC):
                sl = xnb[:, dc * 2048 + qb * 512:dc * 2048 + qb * 512 + 512]
                vtt(sq.bitcast(I32), sl, scb, Op.mult)
                p.drain("vector")
                vcopy(BFA[:, dc * 512:dc * 512 + 512], sq.bitcast(I32), inc=True)
                p.drain("vector")
            p.wait("gpsimd", "vv")
            p.wait("gpsimd", "st")
            for dc in range(NDC):
                dma("gpsimd", xq_d.ap()[b, dc, :, tsl], BFA[:, dc * 512:dc * 512 + 512],
                    sem="st")
            stF(); p.wait("vector", "gp")
        hfull = A3[:, 0:2048]
        h2 = A3[:, 2048:4096]
        hmax = A2[:, 8192:10240]
        t1c = BFA[:, 4096:5120]
        rhs = BFA[:, 5120:5632]
        cwc = A2[:, 4096:4099]
        vmemset(hmax, 0.0, inc=True)
        for hc in range(NHC1):
            p.wait("sync", "vv")
            p.wait("sync", "ss")
            for ic in range(NDC):
                dma("sync", t1c[:, ic * 128:(ic + 1) * 128], t1_d.ap()[ic, hc, :, :])
            dma("sync", cwc, cw_h[hc * 128:(hc + 1) * 128, :])
            dma("sync", A2[0:1, 4100:4228], cb_h[0:1, hc * 128:(hc + 1) * 128])
            ldF(); p.wait("tensor", "ld")
            mm(psb[5][0:128, 0:1], A2[0:1, 4100:4228], ones[0:1, 0:1],
               True, True, inc=True)
            p.wait("scalar", "mm")
            scopy(tiny[:, 12:13], psb[5][0:128, 0:1], inc=True)
            p.wait("vector", "ss")
            p.wait("tensor", "ss")
            for qb in range(NTT):
                for ic in range(NDC):
                    p.wait("sync", "mm")
                    dma("sync", rhs, xq_d.ap()[b, ic, :, qb * 512:qb * 512 + 512])
                    ldF(); p.wait("tensor", "ld")
                    mm(psb[qb % 2][:, 0:512], t1c[:, ic * 128:(ic + 1) * 128], rhs,
                       ic == 0, ic == NDC - 1, inc=True)
                p.wait("vector", "mm")
                vtt(hfull[:, qb * 512:qb * 512 + 512], psb[qb % 2][:, 0:512],
                    fbc_all[:, qb * 512:qb * 512 + 512], Op.mult, inc=True)
                p.drain("vector")
                p.wait("tensor", "vv")
            vts(h2, hfull, cwc[:, 1:2], Op.mult, tiny[:, 12:13], Op.add)
            p.drain("vector")
            vstt(h2[:, 1:2048], hfull[:, 0:2047], cwc[:, 0:1], Op.mult, h2[:, 1:2048], Op.add)
            p.drain("vector")
            vstt(h2[:, 0:2047], hfull[:, 1:2048], cwc[:, 2:3], Op.mult, h2[:, 0:2047], Op.add,
                 inc=True)
            p.drain("vector")
            p.wait("scalar", "vv")
            act(hfull, h2, AF.Silu, inc=True)
            p.wait("vector", "ss")
            vtt(h2, hfull, hfull, Op.mult)
            p.drain("vector")
            vtt(hmax, hmax, h2, Op.max, inc=True)
            p.drain("vector")
            p.wait("gpsimd", "vv")
            dma("gpsimd", h1_d.ap()[b, hc, :, :], hfull, sem="st")
            stF(); p.wait("vector", "gp")
            stF(); p.wait("scalar", "gp")
        p.wait("tensor", "vv")
        p.wait("tensor", "ss")
        for k in range(16):
            ttrans(psb[4 + k // 4][:, (k % 4) * 128:(k % 4) * 128 + 128],
                   hmax[:, k * 128:(k + 1) * 128], inc=(k == 15))
        p.wait("vector", "mm")
        m16 = A2[:, 2048:2064]
        for bk in range(4):
            vred(m16[:, bk * 4:(bk + 1) * 4],
                 psb[4 + bk][:, 0:512].rearrange("p (k i) -> p k i", i=128), Op.max,
                 inc=(bk == 3))
            p.drain("vector")
        p.wait("scalar", "vv")
        act(m16, m16, AF.Sqrt, inc=True)
        p.wait("vector", "ss")
        vts(m16, m16, 1e-5, Op.max)
        p.drain("vector")
        f16 = A2[:, 2064:2080]
        vts(f16, m16, 1.0 / 127.0, Op.mult)
        p.drain("vector")
        vts(f16, f16, tiny[:, 29:30], Op.mult)
        p.drain("vector")
        vrecip(m16, m16)
        p.drain("vector")
        vts(m16, m16, 127.0, Op.mult, inc=True)
        p.drain("vector")
        p.wait("tensor", "vv")
        ttrans(psb[4][0:16, 0:128], m16)
        ttrans(psb[5][0:16, 0:128], f16, inc=True)
        p.wait("scalar", "mm")
        scrows2 = A2[0:16, 2176:2304]
        frows2 = A2[0:16, 2304:2432]
        scopy(scrows2, psb[4][0:16, 0:128])
        scopy(frows2, psb[5][0:16, 0:128], inc=True)
        p.wait("gpsimd", "ss")
        sc2row_r = A2[0:1, 10240:12288]
        f2row_r = A2[0:1, 6656:8704]
        dma("gpsimd", sc2row_r.rearrange("a (k i) -> a k i", i=128), scrows2, sem="gl")
        dma("gpsimd", f2row_r.rearrange("a (k i) -> a k i", i=128), frows2, sem="gl")
        glF()
        p.wait("tensor", "gp")
        p.wait("vector", "gp")
        t2c = BFA[:, 4096:5120]
        sc2b = A2[:, 0:512]
        f2b = A2[:, 512:1024]
        hrt = A2[:, 1024:1536]
        g1t = A2[:, 1536:2048]
        x1t = A2[:, 2560:3072]
        s1t = A2[:, 3072:3584]
        qrh = BFA[:, 5120:5632]
        for qb in range(NTT):
            tsl = slice(qb * 512, qb * 512 + 512)
            p.wait("tensor", "vv")
            p.wait("tensor", "ss")
            bcast_row(4, sc2b, sc2row_r[0:1, tsl], 512)
            bcast_row(5, f2b, f2row_r[0:1, tsl], 512)
            p.wait("sync", "ss")
            for hc in range(NHC1):
                p.wait("sync", "vv")
                dma("sync", hrt, h1_d.ap()[b, hc, :, tsl])
                for ic in range(NDC):
                    dma("sync", t2c[:, ic * 128:(ic + 1) * 128], t2_d.ap()[hc, ic, :, :])
                ldF(); p.wait("vector", "ld")
                vtt(hrt.bitcast(I32), hrt, sc2b, Op.mult)
                p.drain("vector")
                vcopy(qrh, hrt.bitcast(I32), inc=True)
                p.drain("vector")
                p.wait("tensor", "vv")
                ldF(); p.wait("tensor", "ld")
                for ic in range(NDC):
                    mm(psb[ic][:, 0:512], t2c[:, ic * 128:(ic + 1) * 128], qrh,
                       hc == 0, hc == NHC1 - 1, inc=(ic == NDC - 1))
                p.wait("sync", "mm")
            p.wait("vector", "mm")
            for ic in range(NDC):
                vtt(g1t, psb[ic][:, 0:512], f2b, Op.mult, inc=True)
                p.drain("vector")
                p.wait("scalar", "vv")
                act(g1t, g1t, AF.Silu, inc=True)
                p.wait("sync", "ss")
                dma("sync", x1t, x1_d.ap()[b, ic * 128:(ic + 1) * 128, tsl])
                dma("sync", s1t, s1_d.ap()[b, ic * 128:(ic + 1) * 128, tsl])
                ldF(); p.wait("vector", "ld")
                p.wait("vector", "ss")
                vtt(g1t, g1t, s1t, Op.mult)
                p.drain("vector")
                vtt(g1t, g1t, x1t, Op.add, inc=True)
                p.drain("vector")
                p.wait("gpsimd", "vv")
                dma("gpsimd", out_h[b, ic * 128:(ic + 1) * 128, tsl], g1t, sem="st")
                stF(); p.wait("vector", "gp")
                stF(); p.wait("scalar", "gp")

    ln_and_xn(0, 0, x_h, maxcol=40, first_max=True)
    attention(0)
    ln_and_xn(0, 1, x_h, maxcol=40, first_max=False)
    attention(1)
    finish_scales([(40, 42, 43, 32767.0, 1e-12)])
    ffn0_matmuls()
    finish_scales([(44, 50, 51, 32767.0, 1e-12), (45, 52, 53, 32767.0, 1e-12)])
    ffn0_elemwise()
    prep_wdownT()
    ffn0_down_combine()
    p.barrier()
    if debug:
        dma("gpsimd", dbg["d_tiny"][:, :], tiny[:, :], sem="st")
        dma("gpsimd", dbg["d_upt"][:, :], up_d.ap()[0, 0, :, :], sem="st")
        dma("gpsimd", dbg["d_gtt"][:, :], gt_d.ap()[0, 0, :, :], sem="st")
        dma("gpsimd", dbg["d_ot"][:, :], or_d.ap()[0, 0, :, :], sem="st")
        p.barrier()
    prep_ternary(w1_h, I1, D, 20, t1_d)
    prep_ternary(w2_h, D, I1, 23, t2_d)
    prw = A2[0:1, 0:1024]
    for i, src in enumerate((nulog_h, thlog_h, gml_h, dvec_h)):
        p.wait("sync", "ss")
        stF(); p.wait("sync", "gp")
        dma("sync", prw, src[0:1, :])
        ldF(); p.wait("scalar", "ld")
        if i == 0:
            act(prw, prw, AF.Exp, inc=True)
            p.drain("scalar")
            act(prw, prw, AF.Exp, scale=-1.0, inc=True)
        elif i in (1, 2):
            act(prw, prw, AF.Exp, inc=True)
        else:
            scopy(prw, prw, inc=True)
        p.wait("gpsimd", "ss")
        dma("gpsimd", par_d.ap()[i:i + 1, :], prw, sem="st")
        stF(); p.wait("scalar", "gp")
    p.barrier()
    for b in range(NB):
        ln_and_xn(1, b, x1_d.ap())
        lru(b)
        ffn1(b)
    p.barrier()
    blk = eN(nc.Block())
    p.emit(blk)
    stk.close()
    return nc, dbg


_IOTA, _BANDM = None, None


def kernel(**inputs):
    global _IOTA, _BANDM
    if _IOTA is None:
        _IOTA, _BANDM = _host_constants()
    nc, dbg = build_nc(DEBUG)
    idn = np.eye(128, dtype=np.float32)
    x = np.asarray(inputs["L0_attn_hidden_states"])
    in_maps = []
    for c in range(NC):
        shard = np.ascontiguousarray(x[c * NB:(c + 1) * NB].transpose(0, 2, 1))
        in_maps.append({
            "x": shard,
            "wq": np.asarray(inputs["L0_attn_Wq"]),
            "wk": np.asarray(inputs["L0_attn_Wk"]),
            "wv": np.asarray(inputs["L0_attn_Wv"]),
            "wo": np.asarray(inputs["L0_attn_Wo"]),
            "relb": np.asarray(inputs["L0_attn_rel_bias"]),
            "wgate": np.asarray(inputs["L0_ffn_w_gate"]),
            "wup": np.asarray(inputs["L0_ffn_w_up"]),
            "wdown": np.asarray(inputs["L0_ffn_w_down"]),
            "nulog": np.asarray(inputs["L1_attn_nu_log"])[None, :],
            "thlog": np.asarray(inputs["L1_attn_theta_log"])[None, :],
            "bre": np.ascontiguousarray(np.asarray(inputs["L1_attn_B_re"]).T),
            "bim": np.ascontiguousarray(np.asarray(inputs["L1_attn_B_im"]).T),
            "cre": np.ascontiguousarray(np.asarray(inputs["L1_attn_C_re"]).T),
            "cim": np.ascontiguousarray(np.asarray(inputs["L1_attn_C_im"]).T),
            "dvec": np.asarray(inputs["L1_attn_D"])[None, :],
            "gml": np.asarray(inputs["L1_attn_gamma_log"])[None, :],
            "w1": np.asarray(inputs["L1_ffn_w1"]),
            "cw": np.ascontiguousarray(np.asarray(inputs["L1_ffn_conv_w"]).reshape(I1, 3)),
            "cb": np.asarray(inputs["L1_ffn_conv_b"])[None, :],
            "w2": np.asarray(inputs["L1_ffn_w2"]),
            "iota": _IOTA,
            "bandm": _BANDM,
            "ident_h": idn,
        })
    res = run_bass_kernel_spmd(nc, in_maps, core_ids=list(range(NC)))
    out = np.empty((B, S, D), dtype=np.float32)
    for c in range(NC):
        o = np.asarray(res.results[c]["out"])
        out[c * NB:(c + 1) * NB] = o.transpose(0, 2, 1)
    return out


# version 06
# speedup vs baseline: 1.1003x; 1.1003x over previous
import sys
if '/opt/trn_rl_repo' not in sys.path:
    sys.path.insert(0, '/opt/trn_rl_repo')
import math
from contextlib import ExitStack
import numpy as np
import concourse.bass as bass
import concourse.mybir as mybir
from concourse.bass_utils import run_bass_kernel_spmd
from concourse.alu_op_type import AluOpType as Op

F32 = mybir.dt.float32
BF16 = mybir.dt.bfloat16
I32 = mybir.dt.int32
AF = mybir.ActivationFunctionType
AX = mybir.AxisListType

NC = 8
B, S, D = 16, 2048, 1024
NB = B // NC
H, DK = 16, 64
I0, I1 = 3072, 4096
NDC = D // 128
NTT = S // 512
NKT = S // 128
NOC0 = I0 // 128
NHC1 = I1 // 128
TWO_PI_HI = float(np.float32(2.0 * math.pi))
TWO_PI_LO = float(np.float32(2.0 * math.pi - np.float64(np.float32(2.0 * math.pi))))
INV2PI = float(np.float32(1.0 / (2.0 * math.pi)))
HPI = float(np.float32(0.5 * math.pi))
NUM_BUCKETS, MAX_DISTANCE = 32, 128
SLABW = 308
JSPLIT = 154
DEBUG = 0


def _bucket(rp):
    nb = NUM_BUCKETS // 2
    rb = nb if rp > 0 else 0
    rel = abs(rp)
    me = nb // 2
    if rel < me:
        return rb + rel
    f = me + int(math.log(max(rel, 1) / me) / math.log(MAX_DISTANCE / me) * (nb - me))
    return rb + min(f, nb - 1)


def _host_constants():
    iota = np.arange(S, dtype=np.float32)[None, :]
    M = np.zeros((NUM_BUCKETS, 128 * SLABW), dtype=np.float32)
    for kp in range(128):
        for jq in range(SLABW):
            d = kp - jq + 90
            base = _bucket(91) if jq < JSPLIT else _bucket(-91)
            idx = kp * SLABW + jq
            M[_bucket(d), idx] += 1.0
            M[base, idx] -= 1.0
    return iota, M


class Prog:
    def __init__(self, nc, stk):
        self.nc = nc
        self.ops = {k: [] for k in ("sync", "vector", "scalar", "tensor", "gpsimd")}
        self.sems = {}
        self.cnt = {}
        for s in ("dl", "st", "mm", "vv", "ss", "cc", "ld", "gp", "gl"):
            self.sems[s] = stk.enter_context(nc.semaphore(s))
            self.cnt[s] = 0

    def rec(self, eng, fn, inc=None):
        tgt = []
        if inc:
            for snm, n in inc:
                self.cnt[snm] += n
                tgt.append((self.sems[snm], n))

        def run(e, fn=fn, tgt=tgt):
            r = fn(e)
            for s, n in tgt:
                r.then_inc(s, n)
        self.ops[eng].append(run)

    def wait(self, eng, snm, val=None):
        v = self.cnt[snm] if val is None else val
        s = self.sems[snm]
        self.ops[eng].append(lambda e, s=s, v=v: e.wait_ge(s, v))
        return v

    def drain(self, eng):
        self.ops[eng].append(lambda e: e.drain())

    def barrier(self):
        if self.cnt["dl"] > 0:
            self.wait("sync", "dl")
            self.rec("sync", lambda E: E.nop(), inc=[("ld", 1)])
        if self.cnt["st"] > 0 or self.cnt["gl"] > 0:
            self.wait("gpsimd", "st")
            self.wait("gpsimd", "gl")
            self.rec("gpsimd", lambda E: E.nop(), inc=[("gp", 1)])
        snap = dict(self.cnt)
        for eng in self.ops:
            for snm in ("mm", "vv", "ss", "cc", "ld", "gp"):
                v = snap[snm]
                if v > 0:
                    self.wait(eng, snm, v)

    def emit(self, blk):
        for eng in ("sync", "vector", "scalar", "tensor", "gpsimd"):
            ops = self.ops[eng]
            getattr(blk, eng)(lambda e, ops=ops: [op(e) for op in ops])


def build_nc(debug=DEBUG):
    nc = bass.Bass("TRN2", target_bir_lowering=False)
    dt = nc.dram_tensor

    x_h = dt("x", [NB, D, S], F32, kind="ExternalInput").ap()
    wq_h = dt("wq", [D, D], F32, kind="ExternalInput").ap()
    wk_h = dt("wk", [D, D], F32, kind="ExternalInput").ap()
    wv_h = dt("wv", [D, D], F32, kind="ExternalInput").ap()
    wo_h = dt("wo", [D, D], F32, kind="ExternalInput").ap()
    rb_h = dt("relb", [NUM_BUCKETS, H], F32, kind="ExternalInput").ap()
    wgate_h = dt("wgate", [I0, D], F32, kind="ExternalInput").ap()
    wup_h = dt("wup", [I0, D], F32, kind="ExternalInput").ap()
    wdown_h = dt("wdown", [D, I0], F32, kind="ExternalInput").ap()
    nulog_h = dt("nulog", [1, D], F32, kind="ExternalInput").ap()
    thlog_h = dt("thlog", [1, D], F32, kind="ExternalInput").ap()
    bre_h = dt("bre", [D, D], F32, kind="ExternalInput").ap()
    bim_h = dt("bim", [D, D], F32, kind="ExternalInput").ap()
    cre_h = dt("cre", [D, D], F32, kind="ExternalInput").ap()
    cim_h = dt("cim", [D, D], F32, kind="ExternalInput").ap()
    dvec_h = dt("dvec", [1, D], F32, kind="ExternalInput").ap()
    gml_h = dt("gml", [1, D], F32, kind="ExternalInput").ap()
    w1_h = dt("w1", [I1, D], F32, kind="ExternalInput").ap()
    cw_h = dt("cw", [I1, 3], F32, kind="ExternalInput").ap()
    cb_h = dt("cb", [1, I1], F32, kind="ExternalInput").ap()
    w2_h = dt("w2", [D, I1], F32, kind="ExternalInput").ap()
    iota_h = dt("iota", [1, S], F32, kind="ExternalInput").ap()
    bandm_h = dt("bandm", [NUM_BUCKETS, 128 * SLABW], F32, kind="ExternalInput").ap()
    ident_h = dt("ident_h", [128, 128], F32, kind="ExternalInput").ap()

    out_h = dt("out", [NB, D, S], F32, kind="ExternalOutput").ap()
    dbg = {}
    if debug:
        for nm in ("d_xn", "d_s", "d_x1", "d_s1", "d_g"):
            dbg[nm] = dt(nm, [NB, D, S], F32, kind="ExternalOutput").ap()
        dbg["d_tiny"] = dt("d_tiny", [128, 64], F32, kind="ExternalOutput").ap()
        dbg["d_upt"] = dt("d_upt", [128, 512], F32, kind="ExternalOutput").ap()
        dbg["d_gtt"] = dt("d_gtt", [128, 512], F32, kind="ExternalOutput").ap()
        dbg["d_ot"] = dt("d_ot", [128, 512], F32, kind="ExternalOutput").ap()
        dbg["d_wq"] = dt("d_wq", [128, 1024], F32, kind="ExternalOutput").ap()
        dbg["d_wst"] = dt("d_wst", [128, 1024], F32, kind="ExternalOutput").ap()
        dbg["d_x0"] = dt("d_x0", [128, 2048], F32, kind="ExternalOutput").ap()

    av_d = dt("av_d", [NB, NTT, D, 512], F32)
    s_d = dt("s_d", [NB, D, S], F32)
    x1_d = dt("x1_d", [NB, D, S], F32)
    up_d = dt("up_d", [NOC0, NB * NTT, 128, 512], F32)
    gt_d = dt("gt_d", [NOC0, NB * NTT, 128, 512], F32)
    t_d = dt("t_d", [NOC0, NB * NTT, 128, 512], F32)
    or_d = dt("or_d", [NOC0, NB * NTT, 128, 512], F32)
    s1_d = dt("s1_d", [NB, D, S], F32)
    h1_d = dt("h1_d", [NB, NHC1, 128, S], F32)
    t1_d = dt("t1_d", [NDC, NHC1, 128, 128], BF16)
    t2_d = dt("t2_d", [NHC1, NDC, 128, 128], BF16)
    wdT_d = dt("wdT_d", [NDC, NOC0, 128, 128], F32)
    xq_d = dt("xq_d", [NB, NDC, 128, S], BF16)
    pat_d = dt("pat_d", [H, 128 * SLABW], F32)
    rows_d = dt("rows_d", [16, S], F32)
    par_d = dt("par_d", [4, D], F32)
    ccb_i = dt("ccb_i", [1, 4], F32)
    ccb_o = dt("ccb_o", [1, 4], F32)

    stk = ExitStack()
    eN = stk.enter_context
    p = Prog(nc, stk)
    sbuf = lambda nm, shape, dtp=F32: eN(nc.sbuf_tensor(nm, shape, dtp))

    A1 = sbuf("A1", [128, 16384])
    A2 = sbuf("A2", [128, 12288])
    A3 = sbuf("A3", [128, 8192])
    A4 = sbuf("A4", [128, 7168])
    BFA = sbuf("BFA", [128, 8192], BF16)
    ones = sbuf("ones", [128, 1])
    onesr = sbuf("onesr", [1, 128])
    zt = sbuf("zt", [128, 512])
    tiny = sbuf("tiny", [128, 64])
    rbs = sbuf("rbs", [NUM_BUCKETS, 64])
    prow = sbuf("prow", [1, 640])
    ident = sbuf("ident", [128, 128])
    iotas = sbuf("iotas", [1, S])
    tsp = sbuf("tsp", [32, 32])
    tsp2 = sbuf("tsp2", [32, 32])
    psb = [eN(nc.psum_tensor(f"pb{i}", [128, 512], F32)) for i in range(8)]

    def dma(eng, out, in_, sem="dl"):
        p.rec(eng, lambda E: E.dma_start(out=out, in_=in_), inc=[(sem, 16)])

    def dma_acc(out, in_):
        p.rec("gpsimd", lambda E: E.dma_start(out=out, in_=in_, accum_op=Op.add),
              inc=[("st", 16)])

    def chainmax(src, col, first, absv=True):
        vred(tiny[:, 49:50], src, Op.max, absv=absv)
        p.drain("vector")
        if first:
            vcopy(tiny[:, col:col + 1], tiny[:, 49:50], inc=True)
        else:
            vtt(tiny[:, col:col + 1], tiny[:, col:col + 1], tiny[:, 49:50], Op.max,
                inc=True)
        p.drain("vector")

    def ldF():
        p.wait("sync", "dl")
        p.rec("sync", lambda E: E.nop(), inc=[("ld", 1)])

    def stF():
        p.wait("gpsimd", "st")
        p.rec("gpsimd", lambda E: E.nop(), inc=[("gp", 1)])

    def glF():
        p.wait("gpsimd", "gl")
        p.rec("gpsimd", lambda E: E.nop(), inc=[("gp", 1)])

    def mm(out, lhsT, rhs, start, stop, inc=False):
        p.rec("tensor", lambda t: t.matmul(out, lhsT, rhs, start=start, stop=stop),
              inc=[("mm", 1)] if inc else None)

    def ttrans(out, in_, inc=False, k=128):
        p.rec("tensor", lambda t: t.transpose(out, in_, ident[0:k, 0:k]),
              inc=[("mm", 1)] if inc else None)

    def vcopy(out, in_, inc=False):
        p.rec("vector", lambda v: v.tensor_copy(out, in_), inc=[("vv", 1)] if inc else None)

    def vtt(out, in0, in1, op, inc=False):
        p.rec("vector", lambda v: v.tensor_tensor(out=out, in0=in0, in1=in1, op=op),
              inc=[("vv", 1)] if inc else None)

    def vts(out, in0, s1, op0, s2=None, op1=None, inc=False):
        if s2 is None:
            p.rec("vector", lambda v: v.tensor_scalar(out=out, in0=in0, scalar1=s1,
                                                      scalar2=None, op0=op0),
                  inc=[("vv", 1)] if inc else None)
        else:
            p.rec("vector", lambda v: v.tensor_scalar(out=out, in0=in0, scalar1=s1,
                                                      scalar2=s2, op0=op0, op1=op1),
                  inc=[("vv", 1)] if inc else None)

    def vstt(out, in0, scalar, op0, in1, op1, inc=False):
        p.rec("vector", lambda v: v.scalar_tensor_tensor(out=out, in0=in0, scalar=scalar,
                                                         in1=in1, op0=op0, op1=op1),
              inc=[("vv", 1)] if inc else None)

    def vttr(out, in0, in1, op0, op1, scalar, accum, inc=False):
        p.rec("vector", lambda v: v.tensor_tensor_reduce(out=out, in0=in0, in1=in1, scale=1.0,
                                                         scalar=scalar, op0=op0, op1=op1,
                                                         accum_out=accum),
              inc=[("vv", 1)] if inc else None)

    def vred(out, in_, op, absv=False, inc=False):
        p.rec("vector", lambda v: v.tensor_reduce(out=out, in_=in_, axis=AX.X, op=op,
                                                  apply_absolute_value=absv),
              inc=[("vv", 1)] if inc else None)

    def vrecip(out, in_, inc=False):
        p.rec("vector", lambda v: v.reciprocal(out, in_), inc=[("vv", 1)] if inc else None)

    def vmemset(ap, val, inc=False):
        p.rec("vector", lambda v: v.memset(ap, val), inc=[("vv", 1)] if inc else None)

    def vscan(out, data0, data1, inc=False):
        p.rec("vector", lambda v: v.tensor_tensor_scan(out=out, data0=data0, data1=data1,
                                                       initial=0.0, op0=Op.mult, op1=Op.add),
              inc=[("vv", 1)] if inc else None)

    def act(out, in_, func, bias=None, scale=None, inc=False):
        kw = {}
        if bias is not None:
            kw["bias"] = bias
        if scale is not None:
            kw["scale"] = scale
        p.rec("scalar", lambda s: s.activation(out=out, in_=in_, func=func, **kw),
              inc=[("ss", 1)] if inc else None)

    def scopy(out, in_, inc=False):
        p.rec("scalar", lambda s: s.copy(out, in_), inc=[("ss", 1)] if inc else None)

    def ptree(col, ncols=128):
        p.wait("tensor", "vv")
        ttrans(psb[3][0:1, 0:ncols], tiny[0:ncols, col:col + 1], inc=True, k=ncols)
        p.wait("vector", "mm")
        vred(tiny[0:1, col:col + 1], psb[3][0:1, 0:ncols], Op.max)
        p.drain("vector")

    def preduce_row(src32, tmpT, tmpR, dstrow, width):
        J = width // 32
        p.rec("vector", lambda v: v.transpose(tmpT, src32))
        p.drain("vector")
        vmemset(tmpR[:, 0:64], -3.0e38)
        p.drain("vector")
        vred(tmpR[:, 0:J], tmpT.rearrange("p (j i) -> p j i", i=32), Op.max)
        p.drain("vector")
        p.rec("vector", lambda v: v.transpose(tmpR[:, 64:128], tmpR[:, 0:64]))
        p.drain("vector")
        p.rec("vector", lambda v: v.memset(tiny[0:1, 63:64], 0.0), inc=[("vv", 1)])
        p.wait("gpsimd", "vv")
        if J <= 32:
            dma("gpsimd", dstrow.rearrange("a2 (j q) -> a2 j q", q=32),
                tmpR[0:J, 64:96], sem="gl")
        else:
            for k in range(J // 32):
                dma("gpsimd", dstrow[0:1, 1024 * k:1024 * k + 1024]
                    .rearrange("a2 (a q) -> a2 a q", q=32),
                    tmpR[:, 64 + 32 * k:96 + 32 * k], sem="gl")
        glF()
        p.wait("vector", "gp")

    def bcast_row(bank, dst, row, n, np_=128):
        for lo in range(0, n, 512):
            w = min(512, n - lo)
            mm(psb[bank][0:np_, 0:w], onesr[0:1, 0:np_], row[0:1, lo:lo + w], True, True,
               inc=True)
            p.wait("scalar", "mm")
            scopy(dst[:, lo:lo + w], psb[bank][0:np_, 0:w], inc=True)
            p.wait("tensor", "ss")

    def bcast_col(bank, dstcol, srccol):
        mm(psb[bank][0:128, 0:1], onesr[0:1, 0:128], tiny[0:1, srccol:srccol + 1],
           True, True, inc=True)
        p.wait("scalar", "mm")
        scopy(tiny[:, dstcol:dstcol + 1], psb[bank][0:128, 0:1], inc=True)
        p.wait("tensor", "ss")
        p.wait("vector", "ss")

    def col_from_prow(bank, dstcol, off):
        mm(psb[bank][0:128, 0:1], prow[0:1, off:off + 128], ones[0:1, 0:1], True, True,
           inc=True)
        p.wait("scalar", "mm")
        scopy(tiny[:, dstcol:dstcol + 1], psb[bank][0:128, 0:1], inc=True)
        p.wait("tensor", "ss")
        p.wait("vector", "ss")

    vmemset(ones[:, :], 1.0)
    vmemset(onesr[:, :], 1.0)
    vmemset(tiny[:, :], 0.0)
    vmemset(zt[:, :], 0.0, inc=True)
    dma("sync", rbs[:, 0:H], rb_h[:, :])
    dma("sync", rbs[0:1, 32:32 + H], rb_h[31:32, :])
    dma("sync", iotas[0:1, :], iota_h[0:1, :])
    dma("sync", rbs[0:1, 48:48 + H], rb_h[15:16, :])
    dma("sync", ident[:, :], ident_h[:, :])
    nby = 128 * SLABW
    p.wait("tensor", "vv")
    stg = A4[0:NUM_BUCKETS, 0:512]
    pev = A4[0:H, 512:1024]
    for ci in range((nby + 511) // 512):
        lo = ci * 512
        w = min(512, nby - lo)
        stF(); p.wait("sync", "gp")
        dma("sync", stg[:, 0:w], bandm_h[:, lo:lo + w])
        ldF(); p.wait("tensor", "ld")
        bk = ci % 2
        mm(psb[bk][0:H, 0:w], rbs[:, 0:H], stg[:, 0:w], True, True, inc=True)
        p.wait("scalar", "mm")
        scopy(pev[:, 0:w], psb[bk][0:H, 0:w], inc=True)
        p.wait("gpsimd", "ss")
        dma("gpsimd", pat_d.ap()[:, lo:lo + w], pev[:, 0:w], sem="st")
        stF(); p.wait("scalar", "gp")
        pass
    stF(); p.wait("sync", "gp")
    patt = A3[:, 0:H * SLABW]
    dma("sync", patt.rearrange("p (h j) -> p h j", h=H),
        pat_d.ap().rearrange("h (p j) -> p h j", p=128))
    ldF(); p.wait("tensor", "ld")
    bcast_row(2, tiny[:, 0:H], rbs[0:1, 32:32 + H], H)
    bcast_row(3, tiny[:, H:2 * H], rbs[0:1, 48:48 + H], H)
    p.wait("vector", "ss")
    vred(tiny[0:NUM_BUCKETS, 33:34], rbs[:, 0:H], Op.max, absv=True, inc=True)
    p.drain("vector")
    ptree(33, 32)
    vcopy(tiny[0:1, 34:35], tiny[0:1, 33:34], inc=True)
    p.wait("tensor", "vv")
    bcast_col(2, 34, 34)
    p.barrier()

    xnb = A1
    mbc = A2[:, 8192:10240]
    rbc = A2[:, 10240:12288]
    sxr = A2[0:1, 4096:6144]
    sx2r = A2[0:1, 6144:8192]

    def ln_and_xn(li, b, src_d, maxcol=None, first_max=False):
        xt = A4[:, 0:2048]
        x2t = A4[:, 2048:4096]
        for dc in range(NDC):
            p.wait("sync", "mm")
            p.wait("sync", "vv")
            dma("sync", xt, src_d[b, dc * 128:(dc + 1) * 128, :])
            ldF(); p.wait("scalar", "ld")
            p.wait("scalar", "mm")
            act(x2t, xt, AF.Square, inc=True)
            ldF(); p.wait("tensor", "ld")
            p.wait("tensor", "ss")
            for q in range(NTT):
                mm(psb[q][0:1, 0:512], ones[:, :], xt[:, q * 512:(q + 1) * 512],
                   dc == 0, dc == NDC - 1)
                mm(psb[4 + q][0:1, 0:512], ones[:, :], x2t[:, q * 512:(q + 1) * 512],
                   dc == 0, dc == NDC - 1, inc=(q == NTT - 1))
            ldF(); p.wait("vector", "ld")
            vcopy(xnb[:, dc * 2048:(dc + 1) * 2048], xt, inc=True)
        p.wait("vector", "mm")
        for q in range(NTT):
            vts(sxr[0:1, q * 512:(q + 1) * 512], psb[q][0:1, 0:512], 1.0 / D, Op.mult)
            vts(sx2r[0:1, q * 512:(q + 1) * 512], psb[4 + q][0:1, 0:512], 1.0 / D, Op.mult)
        p.drain("vector")
        p.rec("vector", lambda v: v.memset(tiny[0:1, 63:64], 0.0), inc=[("vv", 1)])
        p.wait("gpsimd", "vv")
        dma("gpsimd", rows_d.ap()[4 * li + 2 * b:4 * li + 2 * b + 1, :], sxr, sem="st")
        p.wait("tensor", "vv")
        bcast_row(6, mbc, sxr, 2048)
        stF(); p.wait("vector", "gp")
        p.wait("vector", "ss")
        vtt(sxr, sxr, sxr, Op.mult)
        p.drain("vector")
        vtt(sx2r, sx2r, sxr, Op.subtract)
        p.drain("vector")
        vts(sx2r, sx2r, 1e-5, Op.add)
        p.drain("vector")
        vrecip(sx2r, sx2r, inc=True)
        p.drain("vector")
        p.wait("scalar", "vv")
        act(sx2r, sx2r, AF.Sqrt, inc=True)
        p.wait("gpsimd", "ss")
        dma("gpsimd", rows_d.ap()[4 * li + 2 * b + 1:4 * li + 2 * b + 2, :], sx2r, sem="st")
        p.wait("tensor", "ss")
        bcast_row(7, rbc, sx2r, 2048)
        p.wait("vector", "ss")
        stF(); p.wait("vector", "gp")
        for dc in range(NDC):
            sl = xnb[:, dc * 2048:(dc + 1) * 2048]
            vtt(sl, sl, mbc, Op.subtract)
            p.drain("vector")
            vtt(sl, sl, rbc, Op.mult)
            p.drain("vector")
            if maxcol is not None:
                chainmax(sl, maxcol, first_max and dc == 0)
        p.rec("vector", lambda v: v.memset(tiny[0:1, 63:64], 0.0), inc=[("vv", 1)])
        if debug and li == 0:
            p.barrier()
            for dc in range(NDC):
                dma("gpsimd", dbg["d_xn"][b, dc * 128:(dc + 1) * 128, :],
                    xnb[:, dc * 2048:(dc + 1) * 2048], sem="st")
            p.barrier()

    def rebuild_xn(li, b, src_d):
        dma("sync", sxr, rows_d.ap()[4 * li + 2 * b:4 * li + 2 * b + 1, :])
        dma("sync", sx2r, rows_d.ap()[4 * li + 2 * b + 1:4 * li + 2 * b + 2, :])
        ldF(); p.wait("tensor", "ld")
        bcast_row(6, mbc, sxr, 2048)
        bcast_row(7, rbc, sx2r, 2048)
        p.wait("vector", "ss")
        for dc in range(NDC):
            xt = xnb[:, dc * 2048:(dc + 1) * 2048]
            p.wait("sync", "vv")
            dma("sync", xt, src_d[b, dc * 128:(dc + 1) * 128, :])
            ldF(); p.wait("vector", "ld")
            vtt(xt, xt, mbc, Op.subtract)
            p.drain("vector")
            vtt(xt, xt, rbc, Op.mult, inc=True)
            p.drain("vector")

    def allreduce_max(cols):
        stgr = A2[0:1, 4096:4100]
        vmemset(stgr, -3.0e38)
        p.drain("vector")
        for i, c in enumerate(cols):
            vcopy(stgr[0:1, i:i + 1], tiny[0:1, c:c + 1])
        p.drain("vector")
        p.rec("vector", lambda v: v.memset(tiny[0:1, 63:64], 0.0), inc=[("vv", 1)])
        p.wait("gpsimd", "vv")
        dma("gpsimd", ccb_i.ap(), stgr, sem="st")
        p.wait("gpsimd", "st")
        p.rec("gpsimd", lambda g: g.collective_compute(
            "AllReduce", Op.max, replica_groups=[list(range(NC))],
            ins=[ccb_i.ap().opt()], outs=[ccb_o.ap().opt()]), inc=[("cc", 1)])
        p.wait("gpsimd", "cc")
        dma("gpsimd", stgr, ccb_o.ap(), sem="gl")
        glF()
        p.wait("vector", "gp")
        for i, c in enumerate(cols):
            vcopy(tiny[0:1, c:c + 1], stgr[0:1, i:i + 1])
        p.drain("vector")
        p.rec("vector", lambda v: v.memset(tiny[0:1, 63:64], 0.0), inc=[("vv", 1)])
        p.barrier()

    def finish_scales(specs):
        for i, (src, _, _, _, _) in enumerate(specs):
            ptree(src)
            vcopy(tiny[0:1, 60 + i:61 + i], tiny[0:1, src:src + 1], inc=True)
            p.drain("vector")
        allreduce_max([60 + i for i in range(len(specs))])
        for i, (_, scol, icol, qmax, floor) in enumerate(specs):
            c = 60 + i
            vts(tiny[0:1, c:c + 1], tiny[0:1, c:c + 1], 1.0 / qmax, Op.mult)
            p.drain("vector")
            vts(tiny[0:1, c:c + 1], tiny[0:1, c:c + 1], floor, Op.max)
            p.drain("vector")
            vrecip(tiny[0:1, 58:59], tiny[0:1, c:c + 1], inc=True)
            p.drain("vector")
            p.wait("tensor", "vv")
            mm(psb[2][0:128, 0:1], onesr[0:1, 0:128], tiny[0:1, c:c + 1], True, True, inc=True)
            mm(psb[3][0:128, 0:1], onesr[0:1, 0:128], tiny[0:1, 58:59], True, True, inc=True)
            p.wait("scalar", "mm")
            scopy(tiny[:, scol:scol + 1], psb[2][0:128, 0:1], inc=True)
            scopy(tiny[:, icol:icol + 1], psb[3][0:128, 0:1], inc=True)
            p.wait("vector", "ss")
            p.wait("tensor", "ss")
        p.barrier()

    def attention(b):
        p.barrier()
        qhi = BFA[:, 0:2048]
        qlo = BFA[:, 2048:4096]
        khi = BFA[:, 4096:6144]
        klo = BFA[:, 6144:8192]
        vbuf = A3[:, 4992:4992 + NKT * 130]
        raws = A2[:, 0:NKT * 512]
        wte = A4[:, 0:1024]
        wpair = A4[:, 1024:4096]
        avs = A4[:, 4096:4608]
        avbc = A4[:, 4608:5120]
        rrow_ = A4[0:1, 5120:5632]
        for hp in range(H // 2):
            p.wait("sync", "mm")
            dma("sync", wpair[:, 0:1024].rearrange("p (c m) -> p c m", c=NDC),
                wq_h.rearrange("(c p) m -> p c m", p=128)[:, :, hp * 128:(hp + 1) * 128])
            dma("sync", wpair[:, 1024:2048].rearrange("p (c m) -> p c m", c=NDC),
                wk_h.rearrange("(c p) m -> p c m", p=128)[:, :, hp * 128:(hp + 1) * 128])
            dma("sync", wpair[:, 2048:3072].rearrange("p (c m) -> p c m", c=NDC),
                wv_h.rearrange("(c p) m -> p c m", p=128)[:, :, hp * 128:(hp + 1) * 128])
            ldF(); p.wait("tensor", "ld")
            p.wait("tensor", "vv")
            p.wait("tensor", "ss")
            for q in range(NTT):
                for dc in range(NDC):
                    mm(psb[0][:, 0:512], wpair[:, dc * 128:(dc + 1) * 128],
                       xnb[:, dc * 2048 + q * 512:dc * 2048 + q * 512 + 512],
                       dc == 0, dc == NDC - 1, inc=(dc == NDC - 1))
                for dc in range(NDC):
                    mm(psb[1][:, 0:512], wpair[:, 1024 + dc * 128:1024 + (dc + 1) * 128],
                       xnb[:, dc * 2048 + q * 512:dc * 2048 + q * 512 + 512],
                       dc == 0, dc == NDC - 1, inc=(dc == NDC - 1))
                p.wait("vector", "mm")
                sl = slice(q * 512, q * 512 + 512)
                vcopy(qhi[:, sl], psb[0][:, 0:512])
                vcopy(khi[:, sl], psb[1][:, 0:512])
                p.drain("vector")
                vtt(qlo[:, sl], psb[0][:, 0:512], qhi[:, sl], Op.subtract)
                vtt(klo[:, sl], psb[1][:, 0:512], khi[:, sl], Op.subtract, inc=True)
                p.drain("vector")
                p.wait("tensor", "vv")
            vmemset(vbuf, 1.0, inc=True)
            p.wait("tensor", "vv")
            for kt in range(NKT):
                for dc in range(NDC):
                    mm(psb[2][:, 0:128],
                       xnb[:, dc * 2048 + kt * 128:dc * 2048 + kt * 128 + 128],
                       wpair[:, 2048 + dc * 128:2048 + (dc + 1) * 128],
                       dc == 0, dc == NDC - 1, inc=(dc == NDC - 1))
                p.wait("vector", "mm")
                vcopy(vbuf[:, kt * 130:kt * 130 + 64], psb[2][:, 0:64])
                vcopy(vbuf[:, kt * 130 + 65:kt * 130 + 129], psb[2][:, 64:128], inc=True)
                p.drain("vector")
                p.wait("tensor", "vv")
            for hh in range(2):
                h = 2 * hp + hh
                hsl = slice(64 * hh, 64 * hh + 64)
                for qb in range(NTT):
                    q0 = qb * 512
                    for kt in range(NKT):
                        bk = kt % 2
                        p.wait("tensor", "vv")
                        trip = ((khi, qhi), (khi, qlo), (klo, qhi))
                        for ti, (ka, qa) in enumerate(trip):
                            mm(psb[bk][:, 0:512], ka[hsl, kt * 128:kt * 128 + 128],
                               qa[hsl, q0:q0 + 512], ti == 0, ti == 2, inc=(ti == 2))
                        p.wait("scalar", "mm")
                        p.wait("scalar", "vv")
                        scopy(raws[:, kt * 512:kt * 512 + 512], psb[bk][:, 0:512], inc=True)
                        p.wait("vector", "ss")
                        chainmax(raws[:, kt * 512:kt * 512 + 512], 32, kt == 0, absv=False)
                    for kt in range(NKT):
                        k0 = kt * 128
                        lo = max(k0 - 90, q0)
                        hi = min(k0 + 218, q0 + 512, S)
                        if lo < hi:
                            joff = lo - (k0 - 90)
                            vtt(raws[:, kt * 512 + (lo - q0):kt * 512 + (hi - q0)],
                                raws[:, kt * 512 + (lo - q0):kt * 512 + (hi - q0)],
                                patt[:, h * SLABW + joff:h * SLABW + joff + (hi - lo)],
                                Op.add)
                            p.drain("vector")
                    ptree(32)
                    vcopy(tiny[0:1, 35:36], tiny[0:1, 32:33], inc=True)
                    p.wait("tensor", "vv")
                    bcast_col(3, 35, 35)
                    vtt(tiny[:, 36:37], tiny[:, 35:36], tiny[:, 34:35], Op.add)
                    p.drain("vector")
                    vtt(tiny[:, 37:38], tiny[:, h:h + 1], tiny[:, 36:37], Op.subtract)
                    vtt(tiny[:, 38:39], tiny[:, H + h:H + h + 1], tiny[:, 36:37],
                        Op.subtract, inc=True)
                    p.drain("vector")
                    p.wait("scalar", "vv")
                    for kt in range(NKT):
                        k0 = kt * 128
                        wsl = slice((kt % 2) * 512, (kt % 2) * 512 + 512)
                        p.wait("scalar", "mm")
                        p.drain("scalar")
                        split = min(max(k0 + 64 - q0, 0), 512)
                        if split > 0:
                            act(wte[:, wsl][:, 0:split], raws[:, kt * 512:kt * 512 + split],
                                AF.Exp, bias=tiny[:, 37:38], inc=(split == 512))
                        if split < 512:
                            act(wte[:, wsl][:, split:512],
                                raws[:, kt * 512 + split:kt * 512 + 512],
                                AF.Exp, bias=tiny[:, 38:39], inc=True)
                        p.wait("tensor", "ss")
                        mm(psb[4][0:65, 0:512],
                           vbuf[:, kt * 130 + 65 * hh:kt * 130 + 65 * hh + 65],
                           wte[:, wsl], kt == 0, kt == NKT - 1, inc=True)
                    p.wait("vector", "mm")
                    vrecip(rrow_, psb[4][64:65, 0:512], inc=True)
                    p.wait("tensor", "vv")
                    bcast_row(5, avbc[0:64, :], rrow_, 512, np_=64)
                    p.wait("scalar", "vv")
                    scopy(avs[0:64, :], psb[4][0:64, 0:512], inc=True)
                    p.wait("vector", "ss")
                    vtt(avs[0:64, :], avs[0:64, :], avbc[0:64, :], Op.mult, inc=True)
                    p.drain("vector")
                    p.wait("gpsimd", "vv")
                    dma("gpsimd", av_d.ap()[b, qb, h * 64:(h + 1) * 64, :], avs[0:64, :],
                        sem="st")
                    stF(); p.wait("vector", "gp")
                    stF(); p.wait("scalar", "gp")
                    stF(); p.wait("tensor", "gp")
        p.barrier()
        avl = A4[:, 0:4096]
        wos = A4[:, 4096:5120]
        wev = A4[:, 5120:5632]
        for qb in range(NTT):
            p.wait("sync", "mm")
            for pc in range(NDC):
                dma("sync", avl[:, pc * 512:(pc + 1) * 512],
                    av_d.ap()[b, qb, pc * 128:(pc + 1) * 128, :])
            for dc in range(NDC):
                p.wait("sync", "mm")
                dma("sync", wos.rearrange("p (c m) -> p c m", c=NDC),
                    wo_h.rearrange("(c p) m -> p c m", p=128)[:, :, dc * 128:(dc + 1) * 128])
                ldF(); p.wait("tensor", "ld")
                for pc in range(NDC):
                    mm(psb[dc % 2][:, 0:512], wos[:, pc * 128:(pc + 1) * 128],
                       avl[:, pc * 512:(pc + 1) * 512], pc == 0, pc == NDC - 1,
                       inc=(pc == NDC - 1))
                p.wait("scalar", "mm")
                stF(); p.wait("scalar", "gp")
                scopy(wev, psb[dc % 2][:, 0:512], inc=True)
                p.wait("gpsimd", "ss")
                dma("gpsimd", s_d.ap()[b, dc * 128:(dc + 1) * 128, qb * 512:(qb + 1) * 512],
                    wev, sem="st")
                stF(); p.wait("tensor", "gp")
        if debug:
            p.barrier()
            for dc in range(NDC):
                dma("gpsimd", dbg["d_s"][b, dc * 128:(dc + 1) * 128, :],
                    s_d.ap()[b, dc * 128:(dc + 1) * 128, :], sem="st")
            p.barrier()

    def dequant_tile(stage, nin, sblk, sfull, trash):
        nb_ = nin // 32
        vred(sblk[:, 0:nb_], stage.rearrange("p (nb bs) -> p nb bs", bs=32), Op.max,
             absv=True)
        p.drain("vector")
        for j in range(32):
            vcopy(sfull[:, 0:nin].rearrange("p (nb bs) -> p nb bs", bs=32)[:, :, j:j + 1],
                  sblk[:, 0:nb_].unsqueeze(2))
        p.drain("vector")
        vrecip(trash[:, 0:nin], sfull[:, 0:nin])
        p.drain("vector")
        vstt(stage.bitcast(I32), stage, 7.0, Op.mult, trash[:, 0:nin], Op.mult)
        p.drain("vector")
        vstt(stage, stage.bitcast(I32), 1.0 / 7.0, Op.mult, sfull[:, 0:nin], Op.mult,
             inc=True)
        p.drain("vector")

    def transpose_chunks(stage, nin, cache):
        for ic in range(nin // 128):
            bk = 2 + (ic % 2)
            p.wait("tensor", "ss")
            ttrans(psb[bk][:, 0:128], stage[:, ic * 128:(ic + 1) * 128], inc=True)
            p.wait("scalar", "mm")
            scopy(cache[:, ic * 128:(ic + 1) * 128], psb[bk][:, 0:128], inc=True)
        p.wait("tensor", "ss")

    def ffn0_matmuls():
        p.barrier()
        wstage = A4[:, 0:1024]
        sblk = A4[:, 1024:1056]
        sfull = A4[:, 2048:3072]
        trash = A4[:, 3072:4096]
        cache = A4[:, 4096:5120]
        evs = A4[:, 5120:5632]
        trash2 = A4[:, 5632:6144]
        first_chain = [True, True]
        for b in (1, 0):
            if b == 0:
                rebuild_xn(0, 0, x_h)
            for dc in range(NDC):
                sl = xnb[:, dc * 2048:(dc + 1) * 2048]
                vts(sl.bitcast(I32), sl, tiny[:, 43:44], Op.mult)
                p.drain("vector")
                vts(sl, sl.bitcast(I32), tiny[:, 42:43], Op.mult, inc=True)
                p.drain("vector")
            for iw, (w_h_, sp_d, mcol) in enumerate(((wup_h, up_d, 44), (wgate_h, gt_d, 45))):
                for oc in range(NOC0):
                    p.wait("sync", "mm")
                    p.wait("sync", "vv")
                    dma("sync", wstage, w_h_[oc * 128:(oc + 1) * 128, :])
                    ldF(); p.wait("vector", "ld")
                    p.wait("vector", "mm")
                    dequant_tile(wstage, D, sblk, sfull, trash)
                    if debug and b == 0 and iw == 0 and oc == 0:
                        p.wait("gpsimd", "vv")
                        dma("gpsimd", dbg["d_wst"][:, :], wstage, sem="st")
                        dma("gpsimd", dbg["d_x0"][:, :], xnb[:, 0:2048], sem="st")
                        stF(); p.wait("vector", "gp")
                    p.wait("tensor", "vv")
                    transpose_chunks(wstage, D, cache)
                    if debug and b == 0 and iw == 0 and oc == 0:
                        p.wait("gpsimd", "ss")
                        dma("gpsimd", dbg["d_wq"][:, :], cache, sem="st")
                        stF(); p.wait("scalar", "gp")
                    for qb in range(NTT):
                        bk = qb % 2
                        for ic in range(NDC):
                            mm(psb[bk][:, 0:512], cache[:, ic * 128:(ic + 1) * 128],
                               xnb[:, ic * 2048 + qb * 512:ic * 2048 + qb * 512 + 512],
                               ic == 0, ic == NDC - 1, inc=(ic == NDC - 1))
                        p.wait("scalar", "mm")
                        stF(); p.wait("scalar", "gp")
                        scopy(evs, psb[bk][:, 0:512], inc=True)
                        p.wait("vector", "ss")
                        chainmax(evs, mcol, first_chain[iw])
                        first_chain[iw] = False
                        p.wait("gpsimd", "vv")
                        dma("gpsimd", sp_d.ap()[oc, b * NTT + qb, :, :], evs, sem="st")
                        p.wait("tensor", "vv")

    def ffn0_elemwise():
        p.barrier()
        ga = A4[:, 0:512]
        gb = A4[:, 512:1024]
        gc = A4[:, 1024:1536]
        up = A4[:, 1536:2048]
        trash = A4[:, 2048:2560]
        first = True
        for oc in range(NOC0):
            for tq in range(NB * NTT):
                p.wait("sync", "vv")
                stF(); p.wait("sync", "gp")
                dma("sync", ga, gt_d.ap()[oc, tq, :, :])
                ldF(); p.wait("vector", "ld")
                vts(ga.bitcast(I32), ga, tiny[:, 53:54], Op.mult)
                p.drain("vector")
                vts(ga, ga.bitcast(I32), tiny[:, 52:53], Op.mult, inc=True)
                p.drain("vector")
                p.wait("scalar", "vv")
                act(gb, ga, AF.Sigmoid, inc=True)
                p.wait("vector", "ss")
                vts(gb.bitcast(I32), gb, 65536.0, Op.mult)
                p.drain("vector")
                vts(gb, gb.bitcast(I32), 65535.0, Op.min)
                p.drain("vector")
                vts(gb, gb, 1.0 / 65536.0, Op.mult)
                p.drain("vector")
                vtt(gc, ga, gb, Op.mult)
                p.drain("vector")
                chainmax(gc, 46, first)
                first = False
                p.wait("gpsimd", "vv")
                dma("gpsimd", t_d.ap()[oc, tq, :, :], gc, sem="st")
                stF(); p.wait("vector", "gp")
        finish_scales([(46, 54, 55, 32767.0, 1e-12)])
        first = True
        for oc in range(NOC0):
            for tq in range(NB * NTT):
                p.wait("sync", "vv")
                stF(); p.wait("sync", "gp")
                dma("sync", ga, t_d.ap()[oc, tq, :, :])
                dma("sync", up, up_d.ap()[oc, tq, :, :])
                ldF(); p.wait("vector", "ld")
                vts(ga.bitcast(I32), ga, tiny[:, 55:56], Op.mult)
                p.drain("vector")
                vts(ga, ga.bitcast(I32), tiny[:, 54:55], Op.mult)
                vts(up.bitcast(I32), up, tiny[:, 51:52], Op.mult)
                p.drain("vector")
                vts(up, up.bitcast(I32), tiny[:, 50:51], Op.mult)
                p.drain("vector")
                vtt(gc, ga, up, Op.mult)
                p.drain("vector")
                chainmax(gc, 47, first)
                first = False
                p.wait("gpsimd", "vv")
                dma("gpsimd", or_d.ap()[oc, tq, :, :], gc, sem="st")
                stF(); p.wait("vector", "gp")
        finish_scales([(47, 56, 57, 32767.0, 1e-12)])

    def prep_wdownT():
        p.barrier()
        wstage = A4[:, 0:3072]
        sblk = A4[:, 3072:3168]
        sfull = A2[:, 0:3072]
        trash = A2[:, 3072:6144]
        cachec = A2[:, 6144:6144 + 3072]
        for dc in range(NDC):
            p.wait("sync", "vv")
            stF(); p.wait("sync", "gp")
            dma("sync", wstage, wdown_h[dc * 128:(dc + 1) * 128, :])
            ldF(); p.wait("vector", "ld")
            stF(); p.wait("vector", "gp")
            dequant_tile(wstage, I0, sblk, sfull, trash)
            p.wait("tensor", "vv")
            transpose_chunks(wstage, I0, cachec)
            p.wait("gpsimd", "ss")
            dma("gpsimd", wdT_d.ap()[dc, :, :, :].rearrange("c p m -> p c m"),
                cachec.rearrange("p (c m) -> p c m", c=NOC0), sem="st")
            stF(); p.wait("scalar", "gp")

    def ffn0_down_combine():
        p.barrier()
        gtl = A4[:, 3712:4224]
        xtl = A4[:, 4224:4736]
        stl = A4[:, 4736:5248]
        wdc0 = A4[:, 0:3072]
        oq = A1
        for b in range(NB):
            for qb in range(NTT):
                tq = b * NTT + qb
                for oc in range(NOC0):
                    sl = oq[:, oc * 512:(oc + 1) * 512]
                    p.wait("sync", "vv")
                    dma("sync", sl, or_d.ap()[oc, tq, :, :])
                    ldF(); p.wait("vector", "ld")
                    vts(sl.bitcast(I32), sl, tiny[:, 57:58], Op.mult)
                    p.drain("vector")
                    vts(sl, sl.bitcast(I32), tiny[:, 56:57], Op.mult, inc=True)
                    p.drain("vector")
                for dc in range(NDC):
                    p.wait("sync", "mm")
                    dma("sync", wdc0.rearrange("p (c m) -> p c m", c=NOC0),
                        wdT_d.ap()[dc, :, :, :].rearrange("c p m -> p c m"))
                    ldF(); p.wait("tensor", "ld")
                    p.wait("tensor", "vv")
                    bk = dc % 2
                    for oc in range(NOC0):
                        mm(psb[bk][:, 0:512], wdc0[:, oc * 128:(oc + 1) * 128],
                           oq[:, oc * 512:(oc + 1) * 512], oc == 0, oc == NOC0 - 1,
                           inc=(oc == NOC0 - 1))
                    p.wait("scalar", "mm")
                    p.wait("scalar", "vv")
                    act(gtl, psb[bk][:, 0:512], AF.Silu, inc=True)
                    p.wait("sync", "ss")
                    dma("sync", xtl, x_h[b, dc * 128:(dc + 1) * 128, qb * 512:(qb + 1) * 512])
                    dma("sync", stl, s_d.ap()[b, dc * 128:(dc + 1) * 128, qb * 512:(qb + 1) * 512])
                    ldF(); p.wait("vector", "ld")
                    p.wait("vector", "ss")
                    vtt(gtl, gtl, stl, Op.mult)
                    p.drain("vector")
                    vtt(gtl, gtl, xtl, Op.add, inc=True)
                    p.drain("vector")
                    p.wait("gpsimd", "vv")
                    dma("gpsimd", x1_d.ap()[b, dc * 128:(dc + 1) * 128, qb * 512:(qb + 1) * 512],
                        gtl, sem="st")
                    if debug:
                        dma("gpsimd", dbg["d_x1"][b, dc * 128:(dc + 1) * 128,
                                                  qb * 512:(qb + 1) * 512], gtl, sem="st")
                    stF(); p.wait("vector", "gp")
                    stF(); p.wait("scalar", "gp")

    def prep_ternary(w_h_, nrows, ncols, wscol, t_dram):
        p.barrier()
        stage = A4[:, 0:ncols]
        astage = A2[:, 0:ncols]
        srow = A2[0:1, 4096:4097]
        nrc = nrows // 128
        for rc in range(nrc):
            p.wait("sync", "mm")
            p.wait("sync", "ss")
            dma("sync", stage, w_h_[rc * 128:(rc + 1) * 128, :])
            ldF(); p.wait("scalar", "ld")
            p.wait("scalar", "mm")
            act(astage, stage, AF.Abs, inc=True)
            p.wait("tensor", "ss")
            for lo in range(0, ncols, 512):
                w = min(512, ncols - lo)
                mm(psb[6][0:1, 0:w], ones[:, :], astage[:, lo:lo + w],
                   rc == 0 and lo == 0, rc == nrc - 1 and lo + 512 >= ncols,
                   inc=(lo + 512 >= ncols))
        p.wait("vector", "mm")
        vred(srow, psb[6][0:1, 0:512], Op.add)
        p.drain("vector")
        vts(srow, srow, 1.0 / (nrows * ncols), Op.mult)
        p.drain("vector")
        vts(srow, srow, 1e-5, Op.max)
        p.drain("vector")
        vcopy(tiny[0:1, wscol + 1:wscol + 2], srow)
        p.drain("vector")
        vrecip(tiny[0:1, wscol:wscol + 1], srow, inc=True)
        p.drain("vector")
        p.wait("tensor", "vv")
        bcast_col(2, wscol + 2, wscol)
        bcast_col(3, wscol + 6, wscol + 1)
        bstage = BFA[:, 0:512]
        for rc in range(nrc):
            p.wait("sync", "vv")
            stF(); p.wait("sync", "gp")
            dma("sync", stage, w_h_[rc * 128:(rc + 1) * 128, :])
            ldF(); p.wait("vector", "ld")
            p.wait("vector", "mm")
            vts(stage.bitcast(I32), stage, tiny[:, wscol + 2:wscol + 3], Op.mult)
            p.drain("vector")
            vts(stage, stage.bitcast(I32), 1.0, Op.min)
            p.drain("vector")
            vts(stage, stage, -1.0, Op.max, inc=True)
            p.drain("vector")
            p.wait("tensor", "vv")
            for ic in range(ncols // 128):
                bk = 2 + (ic % 2)
                stF(); p.wait("tensor", "gp")
                ttrans(psb[bk][:, 0:128], stage[:, ic * 128:(ic + 1) * 128], inc=True)
                p.wait("vector", "mm")
                vcopy(bstage[:, 0:128], psb[bk][:, 0:128], inc=True)
                p.drain("vector")
                p.wait("gpsimd", "vv")
                dma("gpsimd", t_dram.ap()[ic, rc, :, :], bstage[:, 0:128], sem="st")
                stF(); p.wait("vector", "gp")

    def lru(b):
        p.barrier()
        cosb = A3[:, 0:2048]
        sinb = A3[:, 2048:4096]
        u_re = A3[:, 4096:6144]
        u_im = A3[:, 6144:8192]
        w_re = A4[:, 0:2048]
        w_im = A4[:, 2048:4096]
        rbt = A4[:, 4096:6144]
        ring = A4[:, 6144:7168]
        ev = A2[:, 2560:3072]
        tr1 = A2[:, 0:512]
        tr2 = A2[:, 512:1024]
        kint = A2[:, 1024:1536]
        rem = A2[:, 1536:2048]
        iorow = A2[0:1, 2048:2560]
        for pc in range(8):
            p.wait("sync", "vv")
            p.wait("sync", "ss")
            dma("sync", prow[0:1, 0:128], par_d.ap()[0:1, pc * 128:pc * 128 + 128])
            dma("sync", prow[0:1, 128:256], par_d.ap()[2:3, pc * 128:pc * 128 + 128])
            dma("sync", prow[0:1, 256:384], par_d.ap()[1:2, pc * 128:pc * 128 + 128])
            ldF(); p.wait("tensor", "ld")
            col_from_prow(4, 7, 0)
            col_from_prow(4, 8, 128)
            vts(tiny[:, 9:10], tiny[:, 8:9], -1.0, Op.mult)
            p.drain("vector")
            vmemset(rbt, 1.0)
            p.drain("vector")
            vts(rbt, rbt, tiny[:, 7:8], Op.mult, inc=True)
            p.drain("vector")
            for nt in range(NTT):
                tsl = slice(nt * 512, nt * 512 + 512)
                p.wait("tensor", "vv")
                p.wait("tensor", "ss")
                mm(psb[6][0:128, 0:512], prow[0:1, 256:384], iotas[0:1, tsl], True, True,
                   inc=True)
                p.wait("vector", "mm")
                vts(kint.bitcast(I32), psb[6][:, 0:512], INV2PI, Op.mult)
                p.drain("vector")
                vstt(rem, kint.bitcast(I32), -TWO_PI_HI, Op.mult, psb[6][:, 0:512], Op.add)
                p.drain("vector")
                vstt(rem, kint.bitcast(I32), -TWO_PI_LO, Op.mult, rem, Op.add)
                p.drain("vector")
                vts(rem, rem, 3.141592, Op.min)
                p.drain("vector")
                vts(rem, rem, -3.141592, Op.max, inc=True)
                p.drain("vector")
                p.wait("scalar", "vv")
                act(sinb[:, tsl], rem, AF.Sin, inc=True)
                p.wait("vector", "ss")
                vts(kint.bitcast(I32), psb[6][:, 0:512], INV2PI, Op.mult, 0.25, Op.add)
                p.drain("vector")
                vstt(rem, kint.bitcast(I32), -TWO_PI_HI, Op.mult, psb[6][:, 0:512], Op.add)
                p.drain("vector")
                vstt(rem, kint.bitcast(I32), -TWO_PI_LO, Op.mult, rem, Op.add)
                p.drain("vector")
                vts(rem, rem, HPI, Op.add)
                p.drain("vector")
                vts(rem, rem, 3.141592, Op.min)
                p.drain("vector")
                vts(rem, rem, -3.141592, Op.max, inc=True)
                p.drain("vector")
                p.wait("scalar", "vv")
                act(cosb[:, tsl], rem, AF.Sin, inc=True)
                p.wait("vector", "ss")
            p.wait("tensor", "ss")
            for hc in range(NDC):
                p.wait("sync", "mm")
                dma("sync", ring[:, 0:128],
                    bre_h[hc * 128:(hc + 1) * 128, pc * 128:(pc + 1) * 128])
                dma("sync", ring[:, 128:256],
                    bim_h[hc * 128:(hc + 1) * 128, pc * 128:(pc + 1) * 128])
                ldF(); p.wait("tensor", "ld")
                for nt in range(NTT):
                    xsl = xnb[:, hc * 2048 + nt * 512:hc * 2048 + nt * 512 + 512]
                    mm(psb[nt][:, 0:512], ring[:, 0:128], xsl, hc == 0, hc == NDC - 1)
                    mm(psb[4 + nt][:, 0:512], ring[:, 128:256], xsl, hc == 0, hc == NDC - 1,
                       inc=(nt == NTT - 1))
            p.wait("vector", "mm")
            for nt in range(NTT):
                tsl = slice(nt * 512, nt * 512 + 512)
                vtt(tr1, psb[nt][:, 0:512], cosb[:, tsl], Op.mult)
                vtt(tr2, psb[4 + nt][:, 0:512], sinb[:, tsl], Op.mult)
                p.drain("vector")
                vtt(u_re[:, tsl], tr1, tr2, Op.add)
                p.drain("vector")
                vtt(tr1, psb[4 + nt][:, 0:512], cosb[:, tsl], Op.mult)
                vtt(tr2, psb[nt][:, 0:512], sinb[:, tsl], Op.mult)
                p.drain("vector")
                vtt(u_im[:, tsl], tr1, tr2, Op.subtract, inc=True)
                p.drain("vector")
            p.wait("tensor", "vv")
            vscan(w_re, rbt, u_re)
            p.drain("vector")
            vscan(w_im, rbt, u_im, inc=True)
            p.drain("vector")
            for nt in range(NTT):
                tsl = slice(nt * 512, nt * 512 + 512)
                vtt(tr1, w_re[:, tsl], cosb[:, tsl], Op.mult)
                vtt(tr2, w_im[:, tsl], sinb[:, tsl], Op.mult)
                p.drain("vector")
                vtt(u_re[:, tsl], tr1, tr2, Op.subtract)
                p.drain("vector")
                vtt(tr1, w_im[:, tsl], cosb[:, tsl], Op.mult)
                vtt(tr2, w_re[:, tsl], sinb[:, tsl], Op.mult)
                p.drain("vector")
                vtt(u_im[:, tsl], tr1, tr2, Op.add, inc=True)
                p.drain("vector")
            for hc in range(NDC):
                p.wait("sync", "mm")
                p.wait("sync", "vv")
                dma("sync", ring[:, 256:384],
                    cre_h[pc * 128:(pc + 1) * 128, hc * 128:(hc + 1) * 128])
                dma("sync", ring[:, 384:512],
                    cim_h[pc * 128:(pc + 1) * 128, hc * 128:(hc + 1) * 128])
                if pc == 0:
                    p.wait("sync", "ss")
                    dma("sync", prow[0:1, 384:512],
                        par_d.ap()[3:4, hc * 128:hc * 128 + 128])
                ldF(); p.wait("vector", "ld")
                vts(ring[:, 256:384], ring[:, 256:384], tiny[:, 8:9], Op.mult)
                p.drain("vector")
                vts(ring[:, 384:512], ring[:, 384:512], tiny[:, 9:10], Op.mult, inc=True)
                p.drain("vector")
                if pc == 0:
                    ldF(); p.wait("tensor", "ld")
                    col_from_prow(5, 11, 384)
                p.wait("tensor", "vv")
                for nt in range(NTT):
                    tsl = slice(nt * 512, nt * 512 + 512)
                    bk = 6 + (nt % 2)
                    mm(psb[bk][:, 0:512], ring[:, 256:384], u_re[:, tsl], True, False)
                    mm(psb[bk][:, 0:512], ring[:, 384:512], u_im[:, tsl], False, True,
                       inc=True)
                    p.wait("scalar", "mm")
                    stF(); p.wait("scalar", "gp")
                    scopy(ev, psb[bk][:, 0:512], inc=True)
                    if pc == 0:
                        p.wait("vector", "ss")
                        vstt(ev, xnb[:, hc * 2048 + nt * 512:hc * 2048 + nt * 512 + 512],
                             tiny[:, 11:12], Op.mult, ev, Op.add, inc=True)
                        p.drain("vector")
                        p.wait("gpsimd", "vv")
                        dma("gpsimd", s1_d.ap()[b, hc * 128:(hc + 1) * 128,
                                                nt * 512:nt * 512 + 512], ev, sem="st")
                        stF(); p.wait("vector", "gp")
                    else:
                        p.wait("gpsimd", "ss")
                        dma_acc(s1_d.ap()[b, hc * 128:(hc + 1) * 128,
                                          nt * 512:nt * 512 + 512], ev)
                        stF(); p.wait("scalar", "gp")
                    stF(); p.wait("tensor", "gp")
        if debug:
            p.barrier()
            for dc in range(NDC):
                dma("gpsimd", dbg["d_s1"][b, dc * 128:(dc + 1) * 128, :],
                    s1_d.ap()[b, dc * 128:(dc + 1) * 128, :], sem="st")
            p.barrier()

    def ffn1(b):
        p.barrier()
        scb = A2[:, 0:512]
        sq = A2[:, 512:1024]
        tmax = A2[:, 1024:1536]
        fbc_all = A2[:, 4608:6656]
        for qb in range(NTT):
            tsl = slice(qb * 512, qb * 512 + 512)
            for dc in range(NDC):
                sl = xnb[:, dc * 2048 + qb * 512:dc * 2048 + qb * 512 + 512]
                if dc == 0:
                    vtt(tmax, sl, sl, Op.mult)
                else:
                    vtt(sq, sl, sl, Op.mult)
                    p.drain("vector")
                    vtt(tmax, tmax, sq, Op.max)
                p.drain("vector")
            p.rec("vector", lambda v: v.memset(tiny[0:1, 63:64], 0.0), inc=[("vv", 1)])
            p.wait("tensor", "vv")
            for k in range(4):
                ttrans(psb[6][:, k * 128:(k + 1) * 128], tmax[:, k * 128:(k + 1) * 128],
                       inc=(k == 3))
            p.wait("vector", "mm")
            mcol = A2[:, 2048:2052]
            vred(mcol, psb[6][:, 0:512].rearrange("p (k i) -> p k i", i=128), Op.max,
                 inc=True)
            p.drain("vector")
            p.wait("scalar", "vv")
            act(mcol, mcol, AF.Sqrt, inc=True)
            p.wait("vector", "ss")
            vts(mcol, mcol, 1e-5, Op.max)
            p.drain("vector")
            fcol = A2[:, 2052:2056]
            vts(fcol, mcol, 1.0 / 127.0, Op.mult)
            p.drain("vector")
            vts(fcol, fcol, tiny[:, 26:27], Op.mult)
            p.drain("vector")
            vrecip(mcol, mcol)
            p.drain("vector")
            vts(mcol, mcol, 127.0, Op.mult, inc=True)
            p.drain("vector")
            p.wait("tensor", "vv")
            ttrans(psb[6][0:4, 0:128], mcol)
            ttrans(psb[7][0:4, 0:128], fcol, inc=True)
            p.wait("scalar", "mm")
            scrows = A2[0:4, 2176:2304]
            frows = A2[0:4, 2304:2432]
            scopy(scrows, psb[6][0:4, 0:128])
            scopy(frows, psb[7][0:4, 0:128], inc=True)
            p.wait("gpsimd", "ss")
            scrow_r = A2[0:1, 2432:2944]
            frow_r = A2[0:1, 2944:3456]
            dma("gpsimd", scrow_r.rearrange("a (k i) -> a k i", i=128), scrows.unsqueeze(2).transpose([0, 2, 1]) if False else scrows, sem="gl")
            dma("gpsimd", frow_r.rearrange("a (k i) -> a k i", i=128), frows, sem="gl")
            glF()
            p.wait("tensor", "gp")
            p.wait("vector", "gp")
            bcast_row(4, scb, scrow_r, 512)
            bcast_row(5, fbc_all[:, qb * 512:qb * 512 + 512], frow_r, 512)
            p.wait("vector", "ss")
            for dc in range(NDC):
                sl = xnb[:, dc * 2048 + qb * 512:dc * 2048 + qb * 512 + 512]
                vtt(sq.bitcast(I32), sl, scb, Op.mult)
                p.drain("vector")
                vcopy(BFA[:, dc * 512:dc * 512 + 512], sq.bitcast(I32), inc=True)
                p.drain("vector")
            p.wait("gpsimd", "vv")
            p.wait("gpsimd", "st")
            for dc in range(NDC):
                dma("gpsimd", xq_d.ap()[b, dc, :, tsl], BFA[:, dc * 512:dc * 512 + 512],
                    sem="st")
            stF(); p.wait("vector", "gp")
        hfull = A3[:, 0:2048]
        h2 = A3[:, 2048:4096]
        hmax = A2[:, 8192:10240]
        t1c = BFA[:, 0:1024]
        rhs = BFA[:, 1024:5120]
        cwc = A2[:, 4096:4099]
        vmemset(hmax, 0.0, inc=True)
        for hc in range(NHC1):
            p.wait("sync", "vv")
            p.wait("sync", "ss")
            dma("sync", t1c.rearrange("p (c m) -> p c m", c=NDC),
                t1_d.ap()[:, hc, :, :].rearrange("c p m -> p c m"))
            dma("sync", cwc, cw_h[hc * 128:(hc + 1) * 128, :])
            dma("sync", A2[0:1, 4100:4228], cb_h[0:1, hc * 128:(hc + 1) * 128])
            ldF(); p.wait("tensor", "ld")
            mm(psb[5][0:128, 0:1], A2[0:1, 4100:4228], ones[0:1, 0:1],
               True, True, inc=True)
            p.wait("scalar", "mm")
            scopy(tiny[:, 12:13], psb[5][0:128, 0:1], inc=True)
            p.wait("vector", "ss")
            p.wait("tensor", "ss")
            for qb in range(NTT):
                p.wait("sync", "mm")
                dma("sync", rhs.rearrange("p (c n) -> p c n", c=NDC),
                    xq_d.ap()[b, :, :, qb * 512:qb * 512 + 512].rearrange("c p n -> p c n"))
                ldF(); p.wait("tensor", "ld")
                for ic in range(NDC):
                    mm(psb[qb % 2][:, 0:512], t1c[:, ic * 128:(ic + 1) * 128],
                       rhs[:, ic * 512:(ic + 1) * 512],
                       ic == 0, ic == NDC - 1, inc=(ic == NDC - 1))
                p.wait("vector", "mm")
                vtt(hfull[:, qb * 512:qb * 512 + 512], psb[qb % 2][:, 0:512],
                    fbc_all[:, qb * 512:qb * 512 + 512], Op.mult, inc=True)
                p.drain("vector")
                p.wait("tensor", "vv")
            vts(h2, hfull, cwc[:, 1:2], Op.mult, tiny[:, 12:13], Op.add)
            p.drain("vector")
            vstt(h2[:, 1:2048], hfull[:, 0:2047], cwc[:, 0:1], Op.mult, h2[:, 1:2048], Op.add)
            p.drain("vector")
            vstt(h2[:, 0:2047], hfull[:, 1:2048], cwc[:, 2:3], Op.mult, h2[:, 0:2047], Op.add,
                 inc=True)
            p.drain("vector")
            p.wait("scalar", "vv")
            act(hfull, h2, AF.Silu, inc=True)
            p.wait("vector", "ss")
            vtt(h2, hfull, hfull, Op.mult)
            p.drain("vector")
            vtt(hmax, hmax, h2, Op.max, inc=True)
            p.drain("vector")
            p.wait("gpsimd", "vv")
            dma("gpsimd", h1_d.ap()[b, hc, :, :], hfull, sem="st")
            stF(); p.wait("vector", "gp")
            stF(); p.wait("scalar", "gp")
        p.wait("tensor", "vv")
        p.wait("tensor", "ss")
        for k in range(16):
            ttrans(psb[4 + k // 4][:, (k % 4) * 128:(k % 4) * 128 + 128],
                   hmax[:, k * 128:(k + 1) * 128], inc=(k == 15))
        p.wait("vector", "mm")
        m16 = A2[:, 2048:2064]
        for bk in range(4):
            vred(m16[:, bk * 4:(bk + 1) * 4],
                 psb[4 + bk][:, 0:512].rearrange("p (k i) -> p k i", i=128), Op.max,
                 inc=(bk == 3))
            p.drain("vector")
        p.wait("scalar", "vv")
        act(m16, m16, AF.Sqrt, inc=True)
        p.wait("vector", "ss")
        vts(m16, m16, 1e-5, Op.max)
        p.drain("vector")
        f16 = A2[:, 2064:2080]
        vts(f16, m16, 1.0 / 127.0, Op.mult)
        p.drain("vector")
        vts(f16, f16, tiny[:, 29:30], Op.mult)
        p.drain("vector")
        vrecip(m16, m16)
        p.drain("vector")
        vts(m16, m16, 127.0, Op.mult, inc=True)
        p.drain("vector")
        p.wait("tensor", "vv")
        ttrans(psb[4][0:16, 0:128], m16)
        ttrans(psb[5][0:16, 0:128], f16, inc=True)
        p.wait("scalar", "mm")
        scrows2 = A2[0:16, 2176:2304]
        frows2 = A2[0:16, 2304:2432]
        scopy(scrows2, psb[4][0:16, 0:128])
        scopy(frows2, psb[5][0:16, 0:128], inc=True)
        p.wait("gpsimd", "ss")
        sc2row_r = A2[0:1, 10240:12288]
        f2row_r = A2[0:1, 6656:8704]
        dma("gpsimd", sc2row_r.rearrange("a (k i) -> a k i", i=128), scrows2, sem="gl")
        dma("gpsimd", f2row_r.rearrange("a (k i) -> a k i", i=128), frows2, sem="gl")
        glF()
        p.wait("tensor", "gp")
        p.wait("vector", "gp")
        t2c = BFA[:, 0:1024]
        sc2b = A2[:, 0:512]
        f2b = A2[:, 512:1024]
        hrt = A2[:, 1024:1536]
        g1t = A2[:, 1536:2048]
        x1t = A2[:, 2560:3072]
        s1t = A2[:, 3072:3584]
        qrh = BFA[:, 5120:5632]
        for qb in range(NTT):
            tsl = slice(qb * 512, qb * 512 + 512)
            p.wait("tensor", "vv")
            p.wait("tensor", "ss")
            bcast_row(4, sc2b, sc2row_r[0:1, tsl], 512)
            bcast_row(5, f2b, f2row_r[0:1, tsl], 512)
            p.wait("sync", "ss")
            for hc in range(NHC1):
                p.wait("sync", "vv")
                dma("sync", hrt, h1_d.ap()[b, hc, :, tsl])
                dma("sync", t2c.rearrange("p (c m) -> p c m", c=NDC),
                    t2_d.ap()[hc, :, :, :].rearrange("c p m -> p c m"))
                ldF(); p.wait("vector", "ld")
                vtt(hrt.bitcast(I32), hrt, sc2b, Op.mult)
                p.drain("vector")
                vcopy(qrh, hrt.bitcast(I32), inc=True)
                p.drain("vector")
                p.wait("tensor", "vv")
                ldF(); p.wait("tensor", "ld")
                for ic in range(NDC):
                    mm(psb[ic][:, 0:512], t2c[:, ic * 128:(ic + 1) * 128], qrh,
                       hc == 0, hc == NHC1 - 1, inc=(ic == NDC - 1))
                p.wait("sync", "mm")
            p.wait("vector", "mm")
            for ic in range(NDC):
                vtt(g1t, psb[ic][:, 0:512], f2b, Op.mult, inc=True)
                p.drain("vector")
                p.wait("scalar", "vv")
                act(g1t, g1t, AF.Silu, inc=True)
                p.wait("sync", "ss")
                dma("sync", x1t, x1_d.ap()[b, ic * 128:(ic + 1) * 128, tsl])
                dma("sync", s1t, s1_d.ap()[b, ic * 128:(ic + 1) * 128, tsl])
                ldF(); p.wait("vector", "ld")
                p.wait("vector", "ss")
                vtt(g1t, g1t, s1t, Op.mult)
                p.drain("vector")
                vtt(g1t, g1t, x1t, Op.add, inc=True)
                p.drain("vector")
                p.wait("gpsimd", "vv")
                dma("gpsimd", out_h[b, ic * 128:(ic + 1) * 128, tsl], g1t, sem="st")
                stF(); p.wait("vector", "gp")
                stF(); p.wait("scalar", "gp")

    ln_and_xn(0, 0, x_h, maxcol=40, first_max=True)
    attention(0)
    ln_and_xn(0, 1, x_h, maxcol=40, first_max=False)
    attention(1)
    finish_scales([(40, 42, 43, 32767.0, 1e-12)])
    ffn0_matmuls()
    finish_scales([(44, 50, 51, 32767.0, 1e-12), (45, 52, 53, 32767.0, 1e-12)])
    ffn0_elemwise()
    prep_wdownT()
    ffn0_down_combine()
    p.barrier()
    if debug:
        dma("gpsimd", dbg["d_tiny"][:, :], tiny[:, :], sem="st")
        dma("gpsimd", dbg["d_upt"][:, :], up_d.ap()[0, 0, :, :], sem="st")
        dma("gpsimd", dbg["d_gtt"][:, :], gt_d.ap()[0, 0, :, :], sem="st")
        dma("gpsimd", dbg["d_ot"][:, :], or_d.ap()[0, 0, :, :], sem="st")
        p.barrier()
    prep_ternary(w1_h, I1, D, 20, t1_d)
    prep_ternary(w2_h, D, I1, 23, t2_d)
    prw = A2[0:1, 0:1024]
    for i, src in enumerate((nulog_h, thlog_h, gml_h, dvec_h)):
        p.wait("sync", "ss")
        stF(); p.wait("sync", "gp")
        dma("sync", prw, src[0:1, :])
        ldF(); p.wait("scalar", "ld")
        if i == 0:
            act(prw, prw, AF.Exp, inc=True)
            p.drain("scalar")
            act(prw, prw, AF.Exp, scale=-1.0, inc=True)
        elif i in (1, 2):
            act(prw, prw, AF.Exp, inc=True)
        else:
            scopy(prw, prw, inc=True)
        p.wait("gpsimd", "ss")
        dma("gpsimd", par_d.ap()[i:i + 1, :], prw, sem="st")
        stF(); p.wait("scalar", "gp")
    p.barrier()
    for b in range(NB):
        ln_and_xn(1, b, x1_d.ap())
        lru(b)
        ffn1(b)
    p.barrier()
    blk = eN(nc.Block())
    p.emit(blk)
    stk.close()
    return nc, dbg


_IOTA, _BANDM = None, None


def kernel(**inputs):
    global _IOTA, _BANDM
    if _IOTA is None:
        _IOTA, _BANDM = _host_constants()
    nc, dbg = build_nc(DEBUG)
    idn = np.eye(128, dtype=np.float32)
    x = np.asarray(inputs["L0_attn_hidden_states"])
    in_maps = []
    for c in range(NC):
        shard = np.ascontiguousarray(x[c * NB:(c + 1) * NB].transpose(0, 2, 1))
        in_maps.append({
            "x": shard,
            "wq": np.asarray(inputs["L0_attn_Wq"]),
            "wk": np.asarray(inputs["L0_attn_Wk"]),
            "wv": np.asarray(inputs["L0_attn_Wv"]),
            "wo": np.asarray(inputs["L0_attn_Wo"]),
            "relb": np.asarray(inputs["L0_attn_rel_bias"]),
            "wgate": np.asarray(inputs["L0_ffn_w_gate"]),
            "wup": np.asarray(inputs["L0_ffn_w_up"]),
            "wdown": np.asarray(inputs["L0_ffn_w_down"]),
            "nulog": np.asarray(inputs["L1_attn_nu_log"])[None, :],
            "thlog": np.asarray(inputs["L1_attn_theta_log"])[None, :],
            "bre": np.ascontiguousarray(np.asarray(inputs["L1_attn_B_re"]).T),
            "bim": np.ascontiguousarray(np.asarray(inputs["L1_attn_B_im"]).T),
            "cre": np.ascontiguousarray(np.asarray(inputs["L1_attn_C_re"]).T),
            "cim": np.ascontiguousarray(np.asarray(inputs["L1_attn_C_im"]).T),
            "dvec": np.asarray(inputs["L1_attn_D"])[None, :],
            "gml": np.asarray(inputs["L1_attn_gamma_log"])[None, :],
            "w1": np.asarray(inputs["L1_ffn_w1"]),
            "cw": np.ascontiguousarray(np.asarray(inputs["L1_ffn_conv_w"]).reshape(I1, 3)),
            "cb": np.asarray(inputs["L1_ffn_conv_b"])[None, :],
            "w2": np.asarray(inputs["L1_ffn_w2"]),
            "iota": _IOTA,
            "bandm": _BANDM,
            "ident_h": idn,
        })
    res = run_bass_kernel_spmd(nc, in_maps, core_ids=list(range(NC)))
    out = np.empty((B, S, D), dtype=np.float32)
    for c in range(NC):
        o = np.asarray(res.results[c]["out"])
        out[c * NB:(c + 1) * NB] = o.transpose(0, 2, 1)
    return out


# version 07
# speedup vs baseline: 1.2067x; 1.0967x over previous
import sys
if '/opt/trn_rl_repo' not in sys.path:
    sys.path.insert(0, '/opt/trn_rl_repo')
import math
from contextlib import ExitStack
import numpy as np
import concourse.bass as bass
import concourse.mybir as mybir
from concourse.bass_utils import run_bass_kernel_spmd
from concourse.alu_op_type import AluOpType as Op

F32 = mybir.dt.float32
BF16 = mybir.dt.bfloat16
I32 = mybir.dt.int32
AF = mybir.ActivationFunctionType
AX = mybir.AxisListType

NC = 8
B, S, D = 16, 2048, 1024
NB = B // NC
H, DK = 16, 64
I0, I1 = 3072, 4096
NDC = D // 128
NTT = S // 512
NKT = S // 128
NOC0 = I0 // 128
NHC1 = I1 // 128
TWO_PI_HI = float(np.float32(2.0 * math.pi))
TWO_PI_LO = float(np.float32(2.0 * math.pi - np.float64(np.float32(2.0 * math.pi))))
INV2PI = float(np.float32(1.0 / (2.0 * math.pi)))
HPI = float(np.float32(0.5 * math.pi))
NUM_BUCKETS, MAX_DISTANCE = 32, 128
SLABW = 308
JSPLIT = 154
DEBUG = 0


def _bucket(rp):
    nb = NUM_BUCKETS // 2
    rb = nb if rp > 0 else 0
    rel = abs(rp)
    me = nb // 2
    if rel < me:
        return rb + rel
    f = me + int(math.log(max(rel, 1) / me) / math.log(MAX_DISTANCE / me) * (nb - me))
    return rb + min(f, nb - 1)


def _host_constants():
    iota = np.arange(S, dtype=np.float32)[None, :]
    M = np.zeros((NUM_BUCKETS, 128 * SLABW), dtype=np.float32)
    for kp in range(128):
        for jq in range(SLABW):
            d = kp - jq + 90
            base = _bucket(91) if jq < JSPLIT else _bucket(-91)
            idx = kp * SLABW + jq
            M[_bucket(d), idx] += 1.0
            M[base, idx] -= 1.0
    return iota, M


class Prog:
    def __init__(self, nc, stk):
        self.nc = nc
        self.ops = {k: [] for k in ("sync", "vector", "scalar", "tensor", "gpsimd")}
        self.sems = {}
        self.cnt = {}
        for s in ("dl", "st", "mm", "vv", "ss", "cc", "ld", "gp", "gl"):
            self.sems[s] = stk.enter_context(nc.semaphore(s))
            self.cnt[s] = 0

    def rec(self, eng, fn, inc=None):
        tgt = []
        if inc:
            for snm, n in inc:
                self.cnt[snm] += n
                tgt.append((self.sems[snm], n))

        def run(e, fn=fn, tgt=tgt):
            r = fn(e)
            for s, n in tgt:
                r.then_inc(s, n)
        self.ops[eng].append(run)

    def wait(self, eng, snm, val=None):
        v = self.cnt[snm] if val is None else val
        s = self.sems[snm]
        self.ops[eng].append(lambda e, s=s, v=v: e.wait_ge(s, v))
        return v

    def drain(self, eng):
        self.ops[eng].append(lambda e: e.drain())

    def barrier(self):
        if self.cnt["dl"] > 0:
            self.wait("sync", "dl")
            self.rec("sync", lambda E: E.nop(), inc=[("ld", 1)])
        if self.cnt["st"] > 0 or self.cnt["gl"] > 0:
            self.wait("gpsimd", "st")
            self.wait("gpsimd", "gl")
            self.rec("gpsimd", lambda E: E.nop(), inc=[("gp", 1)])
        snap = dict(self.cnt)
        for eng in self.ops:
            for snm in ("mm", "vv", "ss", "cc", "ld", "gp"):
                v = snap[snm]
                if v > 0:
                    self.wait(eng, snm, v)

    def emit(self, blk):
        for eng in ("sync", "vector", "scalar", "tensor", "gpsimd"):
            ops = self.ops[eng]
            getattr(blk, eng)(lambda e, ops=ops: [op(e) for op in ops])


def build_nc(debug=DEBUG):
    nc = bass.Bass("TRN2", target_bir_lowering=False)
    dt = nc.dram_tensor

    x_h = dt("x", [NB, D, S], F32, kind="ExternalInput").ap()
    wq_h = dt("wq", [D, D], F32, kind="ExternalInput").ap()
    wk_h = dt("wk", [D, D], F32, kind="ExternalInput").ap()
    wv_h = dt("wv", [D, D], F32, kind="ExternalInput").ap()
    wo_h = dt("wo", [D, D], F32, kind="ExternalInput").ap()
    rb_h = dt("relb", [NUM_BUCKETS, H], F32, kind="ExternalInput").ap()
    wgate_h = dt("wgate", [I0, D], F32, kind="ExternalInput").ap()
    wup_h = dt("wup", [I0, D], F32, kind="ExternalInput").ap()
    wdown_h = dt("wdown", [D, I0], F32, kind="ExternalInput").ap()
    nulog_h = dt("nulog", [1, D], F32, kind="ExternalInput").ap()
    thlog_h = dt("thlog", [1, D], F32, kind="ExternalInput").ap()
    bre_h = dt("bre", [D, D], F32, kind="ExternalInput").ap()
    bim_h = dt("bim", [D, D], F32, kind="ExternalInput").ap()
    cre_h = dt("cre", [D, D], F32, kind="ExternalInput").ap()
    cim_h = dt("cim", [D, D], F32, kind="ExternalInput").ap()
    dvec_h = dt("dvec", [1, D], F32, kind="ExternalInput").ap()
    gml_h = dt("gml", [1, D], F32, kind="ExternalInput").ap()
    w1_h = dt("w1", [I1, D], F32, kind="ExternalInput").ap()
    cw_h = dt("cw", [I1, 3], F32, kind="ExternalInput").ap()
    cb_h = dt("cb", [1, I1], F32, kind="ExternalInput").ap()
    w2_h = dt("w2", [D, I1], F32, kind="ExternalInput").ap()
    iota_h = dt("iota", [1, S], F32, kind="ExternalInput").ap()
    bandm_h = dt("bandm", [NUM_BUCKETS, 128 * SLABW], F32, kind="ExternalInput").ap()
    ident_h = dt("ident_h", [128, 128], F32, kind="ExternalInput").ap()

    out_h = dt("out", [NB, D, S], F32, kind="ExternalOutput").ap()
    dbg = {}
    if debug:
        for nm in ("d_xn", "d_s", "d_x1", "d_s1", "d_g"):
            dbg[nm] = dt(nm, [NB, D, S], F32, kind="ExternalOutput").ap()
        dbg["d_tiny"] = dt("d_tiny", [128, 64], F32, kind="ExternalOutput").ap()
        dbg["d_upt"] = dt("d_upt", [128, 512], F32, kind="ExternalOutput").ap()
        dbg["d_gtt"] = dt("d_gtt", [128, 512], F32, kind="ExternalOutput").ap()
        dbg["d_ot"] = dt("d_ot", [128, 512], F32, kind="ExternalOutput").ap()
        dbg["d_wq"] = dt("d_wq", [128, 1024], F32, kind="ExternalOutput").ap()
        dbg["d_wst"] = dt("d_wst", [128, 1024], F32, kind="ExternalOutput").ap()
        dbg["d_x0"] = dt("d_x0", [128, 2048], F32, kind="ExternalOutput").ap()

    av_d = dt("av_d", [NB, NTT, D, 512], F32)
    s_d = dt("s_d", [NB, D, S], F32)
    x1_d = dt("x1_d", [NB, D, S], F32)
    up_d = dt("up_d", [NOC0, NB * NTT, 128, 512], F32)
    gt_d = dt("gt_d", [NOC0, NB * NTT, 128, 512], F32)
    t_d = dt("t_d", [NOC0, NB * NTT, 128, 512], F32)
    or_d = dt("or_d", [NOC0, NB * NTT, 128, 512], F32)
    s1_d = dt("s1_d", [NB, D, S], F32)
    h1_d = dt("h1_d", [NB, NHC1, 128, S], F32)
    t1_d = dt("t1_d", [NDC, NHC1, 128, 128], BF16)
    t2_d = dt("t2_d", [NHC1, NDC, 128, 128], BF16)
    wdT_d = dt("wdT_d", [NDC, NOC0, 128, 128], F32)
    xq_d = dt("xq_d", [NB, NDC, 128, S], BF16)
    pat_d = dt("pat_d", [H, 128 * SLABW], F32)
    rows_d = dt("rows_d", [16, S], F32)
    par_d = dt("par_d", [4, D], F32)
    ccb_i = dt("ccb_i", [1, 4], F32)
    ccb_o = dt("ccb_o", [1, 4], F32)

    stk = ExitStack()
    eN = stk.enter_context
    p = Prog(nc, stk)
    sbuf = lambda nm, shape, dtp=F32: eN(nc.sbuf_tensor(nm, shape, dtp))

    A1 = sbuf("A1", [128, 16384])
    A2 = sbuf("A2", [128, 12288])
    A3 = sbuf("A3", [128, 8192])
    A4 = sbuf("A4", [128, 7168])
    BFA = sbuf("BFA", [128, 8192], BF16)
    ones = sbuf("ones", [128, 1])
    onesr = sbuf("onesr", [1, 128])
    zt = sbuf("zt", [128, 512])
    tiny = sbuf("tiny", [128, 64])
    rbs = sbuf("rbs", [NUM_BUCKETS, 64])
    prow = sbuf("prow", [1, 640])
    ident = sbuf("ident", [128, 128])
    iotas = sbuf("iotas", [1, S])
    tsp = sbuf("tsp", [32, 32])
    tsp2 = sbuf("tsp2", [32, 32])
    psb = [eN(nc.psum_tensor(f"pb{i}", [128, 512], F32)) for i in range(8)]

    def dma(eng, out, in_, sem="dl"):
        p.rec(eng, lambda E: E.dma_start(out=out, in_=in_), inc=[(sem, 16)])

    def dma_acc(out, in_):
        p.rec("gpsimd", lambda E: E.dma_start(out=out, in_=in_, accum_op=Op.add),
              inc=[("st", 16)])

    def chainmax(src, col, first, absv=True):
        vred(tiny[:, 49:50], src, Op.max, absv=absv)
        p.drain("vector")
        if first:
            vcopy(tiny[:, col:col + 1], tiny[:, 49:50], inc=True)
        else:
            vtt(tiny[:, col:col + 1], tiny[:, col:col + 1], tiny[:, 49:50], Op.max,
                inc=True)
        p.drain("vector")

    def ldF():
        p.wait("sync", "dl")
        p.rec("sync", lambda E: E.nop(), inc=[("ld", 1)])

    def stF():
        p.wait("gpsimd", "st")
        p.rec("gpsimd", lambda E: E.nop(), inc=[("gp", 1)])

    def glF():
        p.wait("gpsimd", "gl")
        p.rec("gpsimd", lambda E: E.nop(), inc=[("gp", 1)])

    def mm(out, lhsT, rhs, start, stop, inc=False):
        p.rec("tensor", lambda t: t.matmul(out, lhsT, rhs, start=start, stop=stop),
              inc=[("mm", 1)] if inc else None)

    def ttrans(out, in_, inc=False, k=128):
        p.rec("tensor", lambda t: t.transpose(out, in_, ident[0:k, 0:k]),
              inc=[("mm", 1)] if inc else None)

    def vcopy(out, in_, inc=False):
        p.rec("vector", lambda v: v.tensor_copy(out, in_), inc=[("vv", 1)] if inc else None)

    def vtt(out, in0, in1, op, inc=False):
        p.rec("vector", lambda v: v.tensor_tensor(out=out, in0=in0, in1=in1, op=op),
              inc=[("vv", 1)] if inc else None)

    def vts(out, in0, s1, op0, s2=None, op1=None, inc=False):
        if s2 is None:
            p.rec("vector", lambda v: v.tensor_scalar(out=out, in0=in0, scalar1=s1,
                                                      scalar2=None, op0=op0),
                  inc=[("vv", 1)] if inc else None)
        else:
            p.rec("vector", lambda v: v.tensor_scalar(out=out, in0=in0, scalar1=s1,
                                                      scalar2=s2, op0=op0, op1=op1),
                  inc=[("vv", 1)] if inc else None)

    def vstt(out, in0, scalar, op0, in1, op1, inc=False):
        p.rec("vector", lambda v: v.scalar_tensor_tensor(out=out, in0=in0, scalar=scalar,
                                                         in1=in1, op0=op0, op1=op1),
              inc=[("vv", 1)] if inc else None)

    def vttr(out, in0, in1, op0, op1, scalar, accum, inc=False):
        p.rec("vector", lambda v: v.tensor_tensor_reduce(out=out, in0=in0, in1=in1, scale=1.0,
                                                         scalar=scalar, op0=op0, op1=op1,
                                                         accum_out=accum),
              inc=[("vv", 1)] if inc else None)

    def vred(out, in_, op, absv=False, inc=False):
        p.rec("vector", lambda v: v.tensor_reduce(out=out, in_=in_, axis=AX.X, op=op,
                                                  apply_absolute_value=absv),
              inc=[("vv", 1)] if inc else None)

    def vrecip(out, in_, inc=False):
        p.rec("vector", lambda v: v.reciprocal(out, in_), inc=[("vv", 1)] if inc else None)

    def vmemset(ap, val, inc=False):
        p.rec("vector", lambda v: v.memset(ap, val), inc=[("vv", 1)] if inc else None)

    def vscan(out, data0, data1, inc=False):
        p.rec("vector", lambda v: v.tensor_tensor_scan(out=out, data0=data0, data1=data1,
                                                       initial=0.0, op0=Op.mult, op1=Op.add),
              inc=[("vv", 1)] if inc else None)

    def act(out, in_, func, bias=None, scale=None, inc=False):
        kw = {}
        if bias is not None:
            kw["bias"] = bias
        if scale is not None:
            kw["scale"] = scale
        p.rec("scalar", lambda s: s.activation(out=out, in_=in_, func=func, **kw),
              inc=[("ss", 1)] if inc else None)

    def scopy(out, in_, inc=False):
        p.rec("scalar", lambda s: s.copy(out, in_), inc=[("ss", 1)] if inc else None)

    def ptree(col, ncols=128):
        p.wait("tensor", "vv")
        ttrans(psb[3][0:1, 0:ncols], tiny[0:ncols, col:col + 1], inc=True, k=ncols)
        p.wait("vector", "mm")
        vred(tiny[0:1, col:col + 1], psb[3][0:1, 0:ncols], Op.max)
        p.drain("vector")

    def preduce_row(src32, tmpT, tmpR, dstrow, width):
        J = width // 32
        p.rec("vector", lambda v: v.transpose(tmpT, src32))
        p.drain("vector")
        vmemset(tmpR[:, 0:64], -3.0e38)
        p.drain("vector")
        vred(tmpR[:, 0:J], tmpT.rearrange("p (j i) -> p j i", i=32), Op.max)
        p.drain("vector")
        p.rec("vector", lambda v: v.transpose(tmpR[:, 64:128], tmpR[:, 0:64]))
        p.drain("vector")
        p.rec("vector", lambda v: v.memset(tiny[0:1, 63:64], 0.0), inc=[("vv", 1)])
        p.wait("gpsimd", "vv")
        if J <= 32:
            dma("gpsimd", dstrow.rearrange("a2 (j q) -> a2 j q", q=32),
                tmpR[0:J, 64:96], sem="gl")
        else:
            for k in range(J // 32):
                dma("gpsimd", dstrow[0:1, 1024 * k:1024 * k + 1024]
                    .rearrange("a2 (a q) -> a2 a q", q=32),
                    tmpR[:, 64 + 32 * k:96 + 32 * k], sem="gl")
        glF()
        p.wait("vector", "gp")

    def bcast_row(bank, dst, row, n, np_=128):
        for lo in range(0, n, 512):
            w = min(512, n - lo)
            mm(psb[bank][0:np_, 0:w], onesr[0:1, 0:np_], row[0:1, lo:lo + w], True, True,
               inc=True)
            p.wait("scalar", "mm")
            scopy(dst[:, lo:lo + w], psb[bank][0:np_, 0:w], inc=True)
            p.wait("tensor", "ss")

    def bcast_col(bank, dstcol, srccol):
        mm(psb[bank][0:128, 0:1], onesr[0:1, 0:128], tiny[0:1, srccol:srccol + 1],
           True, True, inc=True)
        p.wait("scalar", "mm")
        scopy(tiny[:, dstcol:dstcol + 1], psb[bank][0:128, 0:1], inc=True)
        p.wait("tensor", "ss")
        p.wait("vector", "ss")

    def col_from_prow(bank, dstcol, off):
        mm(psb[bank][0:128, 0:1], prow[0:1, off:off + 128], ones[0:1, 0:1], True, True,
           inc=True)
        p.wait("scalar", "mm")
        scopy(tiny[:, dstcol:dstcol + 1], psb[bank][0:128, 0:1], inc=True)
        p.wait("tensor", "ss")
        p.wait("vector", "ss")

    vmemset(ones[:, :], 1.0)
    vmemset(onesr[:, :], 1.0)
    vmemset(tiny[:, :], 0.0)
    vmemset(zt[:, :], 0.0, inc=True)
    dma("sync", rbs[:, 0:H], rb_h[:, :])
    dma("sync", rbs[0:1, 32:32 + H], rb_h[31:32, :])
    dma("sync", iotas[0:1, :], iota_h[0:1, :])
    dma("sync", rbs[0:1, 48:48 + H], rb_h[15:16, :])
    dma("sync", ident[:, :], ident_h[:, :])
    nby = 128 * SLABW
    p.wait("tensor", "vv")
    stg = A4[0:NUM_BUCKETS, 0:512]
    pev = A4[0:H, 512:1024]
    for ci in range((nby + 511) // 512):
        lo = ci * 512
        w = min(512, nby - lo)
        stF(); p.wait("sync", "gp")
        dma("sync", stg[:, 0:w], bandm_h[:, lo:lo + w])
        ldF(); p.wait("tensor", "ld")
        bk = ci % 2
        mm(psb[bk][0:H, 0:w], rbs[:, 0:H], stg[:, 0:w], True, True, inc=True)
        p.wait("scalar", "mm")
        scopy(pev[:, 0:w], psb[bk][0:H, 0:w], inc=True)
        p.wait("gpsimd", "ss")
        dma("gpsimd", pat_d.ap()[:, lo:lo + w], pev[:, 0:w], sem="st")
        stF(); p.wait("scalar", "gp")
        pass
    stF(); p.wait("sync", "gp")
    patt = A3[:, 0:H * SLABW]
    dma("sync", patt.rearrange("p (h j) -> p h j", h=H),
        pat_d.ap().rearrange("h (p j) -> p h j", p=128))
    ldF(); p.wait("tensor", "ld")
    bcast_row(2, tiny[:, 0:H], rbs[0:1, 32:32 + H], H)
    bcast_row(3, tiny[:, H:2 * H], rbs[0:1, 48:48 + H], H)
    p.wait("vector", "ss")
    vred(tiny[0:NUM_BUCKETS, 33:34], rbs[:, 0:H], Op.max, absv=True, inc=True)
    p.drain("vector")
    ptree(33, 32)
    vcopy(tiny[0:1, 34:35], tiny[0:1, 33:34], inc=True)
    p.wait("tensor", "vv")
    bcast_col(2, 34, 34)
    p.barrier()

    xnb = A1
    mbc = A2[:, 8192:10240]
    rbc = A2[:, 10240:12288]
    sxr = A2[0:1, 4096:6144]
    sx2r = A2[0:1, 6144:8192]

    def ln_and_xn(li, b, src_d, maxcol=None, first_max=False):
        xt = A4[:, 0:2048]
        x2t = A4[:, 2048:4096]
        for dc in range(NDC):
            p.wait("sync", "mm")
            p.wait("sync", "vv")
            dma("sync", xt, src_d[b, dc * 128:(dc + 1) * 128, :])
            ldF(); p.wait("scalar", "ld")
            p.wait("scalar", "mm")
            act(x2t, xt, AF.Square, inc=True)
            ldF(); p.wait("tensor", "ld")
            p.wait("tensor", "ss")
            for q in range(NTT):
                mm(psb[q][0:1, 0:512], ones[:, :], xt[:, q * 512:(q + 1) * 512],
                   dc == 0, dc == NDC - 1)
                mm(psb[4 + q][0:1, 0:512], ones[:, :], x2t[:, q * 512:(q + 1) * 512],
                   dc == 0, dc == NDC - 1, inc=(q == NTT - 1))
            ldF(); p.wait("vector", "ld")
            vcopy(xnb[:, dc * 2048:(dc + 1) * 2048], xt, inc=True)
        p.wait("vector", "mm")
        for q in range(NTT):
            vts(sxr[0:1, q * 512:(q + 1) * 512], psb[q][0:1, 0:512], 1.0 / D, Op.mult)
            vts(sx2r[0:1, q * 512:(q + 1) * 512], psb[4 + q][0:1, 0:512], 1.0 / D, Op.mult)
        p.drain("vector")
        p.rec("vector", lambda v: v.memset(tiny[0:1, 63:64], 0.0), inc=[("vv", 1)])
        p.wait("gpsimd", "vv")
        dma("gpsimd", rows_d.ap()[4 * li + 2 * b:4 * li + 2 * b + 1, :], sxr, sem="st")
        p.wait("tensor", "vv")
        bcast_row(6, mbc, sxr, 2048)
        stF(); p.wait("vector", "gp")
        p.wait("vector", "ss")
        vtt(sxr, sxr, sxr, Op.mult)
        p.drain("vector")
        vtt(sx2r, sx2r, sxr, Op.subtract)
        p.drain("vector")
        vts(sx2r, sx2r, 1e-5, Op.add)
        p.drain("vector")
        vrecip(sx2r, sx2r, inc=True)
        p.drain("vector")
        p.wait("scalar", "vv")
        act(sx2r, sx2r, AF.Sqrt, inc=True)
        p.wait("gpsimd", "ss")
        dma("gpsimd", rows_d.ap()[4 * li + 2 * b + 1:4 * li + 2 * b + 2, :], sx2r, sem="st")
        p.wait("tensor", "ss")
        bcast_row(7, rbc, sx2r, 2048)
        p.wait("vector", "ss")
        stF(); p.wait("vector", "gp")
        for dc in range(NDC):
            sl = xnb[:, dc * 2048:(dc + 1) * 2048]
            vtt(sl, sl, mbc, Op.subtract)
            p.drain("vector")
            vtt(sl, sl, rbc, Op.mult)
            p.drain("vector")
            if maxcol is not None:
                chainmax(sl, maxcol, first_max and dc == 0)
        p.rec("vector", lambda v: v.memset(tiny[0:1, 63:64], 0.0), inc=[("vv", 1)])
        if debug and li == 0:
            p.barrier()
            for dc in range(NDC):
                dma("gpsimd", dbg["d_xn"][b, dc * 128:(dc + 1) * 128, :],
                    xnb[:, dc * 2048:(dc + 1) * 2048], sem="st")
            p.barrier()

    def rebuild_xn(li, b, src_d):
        dma("sync", sxr, rows_d.ap()[4 * li + 2 * b:4 * li + 2 * b + 1, :])
        dma("sync", sx2r, rows_d.ap()[4 * li + 2 * b + 1:4 * li + 2 * b + 2, :])
        ldF(); p.wait("tensor", "ld")
        bcast_row(6, mbc, sxr, 2048)
        bcast_row(7, rbc, sx2r, 2048)
        p.wait("vector", "ss")
        for dc in range(NDC):
            xt = xnb[:, dc * 2048:(dc + 1) * 2048]
            p.wait("sync", "vv")
            dma("sync", xt, src_d[b, dc * 128:(dc + 1) * 128, :])
            ldF(); p.wait("vector", "ld")
            vtt(xt, xt, mbc, Op.subtract)
            p.drain("vector")
            vtt(xt, xt, rbc, Op.mult, inc=True)
            p.drain("vector")

    def allreduce_max(cols):
        stgr = A2[0:1, 4096:4100]
        vmemset(stgr, -3.0e38)
        p.drain("vector")
        for i, c in enumerate(cols):
            vcopy(stgr[0:1, i:i + 1], tiny[0:1, c:c + 1])
        p.drain("vector")
        p.rec("vector", lambda v: v.memset(tiny[0:1, 63:64], 0.0), inc=[("vv", 1)])
        p.wait("gpsimd", "vv")
        dma("gpsimd", ccb_i.ap(), stgr, sem="st")
        p.wait("gpsimd", "st")
        p.rec("gpsimd", lambda g: g.collective_compute(
            "AllReduce", Op.max, replica_groups=[list(range(NC))],
            ins=[ccb_i.ap().opt()], outs=[ccb_o.ap().opt()]), inc=[("cc", 1)])
        p.wait("gpsimd", "cc")
        dma("gpsimd", stgr, ccb_o.ap(), sem="gl")
        glF()
        p.wait("vector", "gp")
        for i, c in enumerate(cols):
            vcopy(tiny[0:1, c:c + 1], stgr[0:1, i:i + 1])
        p.drain("vector")
        p.rec("vector", lambda v: v.memset(tiny[0:1, 63:64], 0.0), inc=[("vv", 1)])
        p.barrier()

    def finish_scales(specs):
        for i, (src, _, _, _, _) in enumerate(specs):
            ptree(src)
            vcopy(tiny[0:1, 60 + i:61 + i], tiny[0:1, src:src + 1], inc=True)
            p.drain("vector")
        allreduce_max([60 + i for i in range(len(specs))])
        for i, (_, scol, icol, qmax, floor) in enumerate(specs):
            c = 60 + i
            vts(tiny[0:1, c:c + 1], tiny[0:1, c:c + 1], 1.0 / qmax, Op.mult)
            p.drain("vector")
            vts(tiny[0:1, c:c + 1], tiny[0:1, c:c + 1], floor, Op.max)
            p.drain("vector")
            vrecip(tiny[0:1, 58:59], tiny[0:1, c:c + 1], inc=True)
            p.drain("vector")
            p.wait("tensor", "vv")
            mm(psb[2][0:128, 0:1], onesr[0:1, 0:128], tiny[0:1, c:c + 1], True, True, inc=True)
            mm(psb[3][0:128, 0:1], onesr[0:1, 0:128], tiny[0:1, 58:59], True, True, inc=True)
            p.wait("scalar", "mm")
            scopy(tiny[:, scol:scol + 1], psb[2][0:128, 0:1], inc=True)
            scopy(tiny[:, icol:icol + 1], psb[3][0:128, 0:1], inc=True)
            p.wait("vector", "ss")
            p.wait("tensor", "ss")
        p.barrier()

    def attention(b):
        p.barrier()
        qhi = BFA[:, 0:2048]
        qlo = BFA[:, 2048:4096]
        khi = BFA[:, 4096:6144]
        klo = BFA[:, 6144:8192]
        vbuf = A3[:, 4992:4992 + NKT * 130]
        raws = A2[:, 0:NKT * 512]
        wte = A4[:, 0:1024]
        wpair = A4[:, 1024:4096]
        avs = A4[:, 4096:4608]
        avbc = A4[:, 4608:5120]
        rrow_ = A4[0:1, 5120:5632]
        for hp in range(H // 2):
            p.wait("sync", "mm")
            dma("sync", wpair[:, 0:1024].rearrange("p (c m) -> p c m", c=NDC),
                wq_h.rearrange("(c p) m -> p c m", p=128)[:, :, hp * 128:(hp + 1) * 128])
            dma("sync", wpair[:, 1024:2048].rearrange("p (c m) -> p c m", c=NDC),
                wk_h.rearrange("(c p) m -> p c m", p=128)[:, :, hp * 128:(hp + 1) * 128])
            dma("sync", wpair[:, 2048:3072].rearrange("p (c m) -> p c m", c=NDC),
                wv_h.rearrange("(c p) m -> p c m", p=128)[:, :, hp * 128:(hp + 1) * 128])
            ldF(); p.wait("tensor", "ld")
            p.wait("tensor", "vv")
            p.wait("tensor", "ss")
            for q in range(NTT):
                for dc in range(NDC):
                    mm(psb[0][:, 0:512], wpair[:, dc * 128:(dc + 1) * 128],
                       xnb[:, dc * 2048 + q * 512:dc * 2048 + q * 512 + 512],
                       dc == 0, dc == NDC - 1, inc=(dc == NDC - 1))
                for dc in range(NDC):
                    mm(psb[1][:, 0:512], wpair[:, 1024 + dc * 128:1024 + (dc + 1) * 128],
                       xnb[:, dc * 2048 + q * 512:dc * 2048 + q * 512 + 512],
                       dc == 0, dc == NDC - 1, inc=(dc == NDC - 1))
                p.wait("vector", "mm")
                sl = slice(q * 512, q * 512 + 512)
                vcopy(qhi[:, sl], psb[0][:, 0:512])
                vcopy(khi[:, sl], psb[1][:, 0:512])
                p.drain("vector")
                vtt(qlo[:, sl], psb[0][:, 0:512], qhi[:, sl], Op.subtract)
                vtt(klo[:, sl], psb[1][:, 0:512], khi[:, sl], Op.subtract, inc=True)
                p.drain("vector")
                p.wait("tensor", "vv")
            vmemset(vbuf, 1.0, inc=True)
            p.wait("tensor", "vv")
            for kt in range(NKT):
                for dc in range(NDC):
                    mm(psb[2][:, 0:128],
                       xnb[:, dc * 2048 + kt * 128:dc * 2048 + kt * 128 + 128],
                       wpair[:, 2048 + dc * 128:2048 + (dc + 1) * 128],
                       dc == 0, dc == NDC - 1, inc=(dc == NDC - 1))
                p.wait("vector", "mm")
                vcopy(vbuf[:, kt * 130:kt * 130 + 64], psb[2][:, 0:64])
                vcopy(vbuf[:, kt * 130 + 65:kt * 130 + 129], psb[2][:, 64:128], inc=True)
                p.drain("vector")
                p.wait("tensor", "vv")
            for hh in range(2):
                h = 2 * hp + hh
                hsl = slice(64 * hh, 64 * hh + 64)
                for qb in range(NTT):
                    q0 = qb * 512
                    for kt in range(NKT):
                        bk = kt % 2
                        p.wait("tensor", "vv")
                        trip = ((khi, qhi), (khi, qlo), (klo, qhi))
                        for ti, (ka, qa) in enumerate(trip):
                            mm(psb[bk][:, 0:512], ka[hsl, kt * 128:kt * 128 + 128],
                               qa[hsl, q0:q0 + 512], ti == 0, ti == 2, inc=(ti == 2))
                        p.wait("scalar", "mm")
                        p.wait("scalar", "vv")
                        scopy(raws[:, kt * 512:kt * 512 + 512], psb[bk][:, 0:512], inc=True)
                        p.wait("vector", "ss")
                        chainmax(raws[:, kt * 512:kt * 512 + 512], 32, kt == 0, absv=False)
                    for kt in range(NKT):
                        k0 = kt * 128
                        lo = max(k0 - 90, q0)
                        hi = min(k0 + 218, q0 + 512, S)
                        if lo < hi:
                            joff = lo - (k0 - 90)
                            vtt(raws[:, kt * 512 + (lo - q0):kt * 512 + (hi - q0)],
                                raws[:, kt * 512 + (lo - q0):kt * 512 + (hi - q0)],
                                patt[:, h * SLABW + joff:h * SLABW + joff + (hi - lo)],
                                Op.add)
                            p.drain("vector")
                    ptree(32)
                    vcopy(tiny[0:1, 35:36], tiny[0:1, 32:33], inc=True)
                    p.wait("tensor", "vv")
                    bcast_col(3, 35, 35)
                    vtt(tiny[:, 36:37], tiny[:, 35:36], tiny[:, 34:35], Op.add)
                    p.drain("vector")
                    vtt(tiny[:, 37:38], tiny[:, h:h + 1], tiny[:, 36:37], Op.subtract)
                    vtt(tiny[:, 38:39], tiny[:, H + h:H + h + 1], tiny[:, 36:37],
                        Op.subtract, inc=True)
                    p.drain("vector")
                    p.wait("scalar", "vv")
                    for kt in range(NKT):
                        k0 = kt * 128
                        wsl = slice((kt % 2) * 512, (kt % 2) * 512 + 512)
                        p.wait("scalar", "mm")
                        p.drain("scalar")
                        split = min(max(k0 + 64 - q0, 0), 512)
                        if split > 0:
                            act(wte[:, wsl][:, 0:split], raws[:, kt * 512:kt * 512 + split],
                                AF.Exp, bias=tiny[:, 37:38], inc=(split == 512))
                        if split < 512:
                            act(wte[:, wsl][:, split:512],
                                raws[:, kt * 512 + split:kt * 512 + 512],
                                AF.Exp, bias=tiny[:, 38:39], inc=True)
                        p.wait("tensor", "ss")
                        mm(psb[4][0:65, 0:512],
                           vbuf[:, kt * 130 + 65 * hh:kt * 130 + 65 * hh + 65],
                           wte[:, wsl], kt == 0, kt == NKT - 1, inc=True)
                    p.wait("vector", "mm")
                    vrecip(rrow_, psb[4][64:65, 0:512], inc=True)
                    p.wait("tensor", "vv")
                    bcast_row(5, avbc[0:64, :], rrow_, 512, np_=64)
                    p.wait("scalar", "vv")
                    scopy(avs[0:64, :], psb[4][0:64, 0:512], inc=True)
                    p.wait("vector", "ss")
                    vtt(avs[0:64, :], avs[0:64, :], avbc[0:64, :], Op.mult, inc=True)
                    p.drain("vector")
                    p.wait("gpsimd", "vv")
                    dma("gpsimd", av_d.ap()[b, qb, h * 64:(h + 1) * 64, :], avs[0:64, :],
                        sem="st")
                    stF(); p.wait("vector", "gp")
                    stF(); p.wait("scalar", "gp")
                    stF(); p.wait("tensor", "gp")
        p.barrier()
        avl = A4[:, 0:4096]
        wos = A4[:, 4096:5120]
        wev = A4[:, 5120:5632]
        for qb in range(NTT):
            p.wait("sync", "mm")
            for pc in range(NDC):
                dma("sync", avl[:, pc * 512:(pc + 1) * 512],
                    av_d.ap()[b, qb, pc * 128:(pc + 1) * 128, :])
            for dc in range(NDC):
                p.wait("sync", "mm")
                dma("sync", wos.rearrange("p (c m) -> p c m", c=NDC),
                    wo_h.rearrange("(c p) m -> p c m", p=128)[:, :, dc * 128:(dc + 1) * 128])
                ldF(); p.wait("tensor", "ld")
                for pc in range(NDC):
                    mm(psb[dc % 2][:, 0:512], wos[:, pc * 128:(pc + 1) * 128],
                       avl[:, pc * 512:(pc + 1) * 512], pc == 0, pc == NDC - 1,
                       inc=(pc == NDC - 1))
                p.wait("scalar", "mm")
                stF(); p.wait("scalar", "gp")
                scopy(wev, psb[dc % 2][:, 0:512], inc=True)
                p.wait("gpsimd", "ss")
                dma("gpsimd", s_d.ap()[b, dc * 128:(dc + 1) * 128, qb * 512:(qb + 1) * 512],
                    wev, sem="st")
                stF(); p.wait("tensor", "gp")
        if debug:
            p.barrier()
            for dc in range(NDC):
                dma("gpsimd", dbg["d_s"][b, dc * 128:(dc + 1) * 128, :],
                    s_d.ap()[b, dc * 128:(dc + 1) * 128, :], sem="st")
            p.barrier()

    def dequant_tile(stage, nin, sblk, sfull, trash):
        nb_ = nin // 32
        vred(sblk[:, 0:nb_], stage.rearrange("p (nb bs) -> p nb bs", bs=32), Op.max,
             absv=True)
        p.drain("vector")
        for j in range(32):
            vcopy(sfull[:, 0:nin].rearrange("p (nb bs) -> p nb bs", bs=32)[:, :, j:j + 1],
                  sblk[:, 0:nb_].unsqueeze(2))
        p.drain("vector")
        vrecip(trash[:, 0:nin], sfull[:, 0:nin])
        p.drain("vector")
        vstt(stage.bitcast(I32), stage, 7.0, Op.mult, trash[:, 0:nin], Op.mult)
        p.drain("vector")
        vstt(stage, stage.bitcast(I32), 1.0 / 7.0, Op.mult, sfull[:, 0:nin], Op.mult,
             inc=True)
        p.drain("vector")

    def transpose_chunks(stage, nin, cache):
        for ic in range(nin // 128):
            bk = 2 + (ic % 2)
            p.wait("tensor", "ss")
            ttrans(psb[bk][:, 0:128], stage[:, ic * 128:(ic + 1) * 128], inc=True)
            p.wait("scalar", "mm")
            scopy(cache[:, ic * 128:(ic + 1) * 128], psb[bk][:, 0:128], inc=True)
        p.wait("tensor", "ss")

    def ffn0_matmuls():
        p.barrier()
        wst = [A4[:, 0:1024], A4[:, 5632:6656]]
        sblk = A4[:, 1024:1056]
        sfull = A4[:, 2048:3072]
        trash = A4[:, 3072:4096]
        cache = A4[:, 4096:5120]
        evs = [A4[:, 5120:5632], A4[:, 6656:7168]]
        first_chain = [True, True]
        whs = (wup_h, wgate_h)
        spds = (up_d, gt_d)
        mcols = (44, 45)
        for b in (1, 0):
            if b == 0:
                rebuild_xn(0, 0, x_h)
            for dc in range(NDC):
                sl = xnb[:, dc * 2048:(dc + 1) * 2048]
                vts(sl.bitcast(I32), sl, tiny[:, 43:44], Op.mult)
                p.drain("vector")
                vts(sl, sl.bitcast(I32), tiny[:, 42:43], Op.mult, inc=True)
                p.drain("vector")
            jobs = [(iw, oc) for iw in range(2) for oc in range(NOC0)]
            ld_mark = [0, 0]
            gp_mark = [p.cnt["gp"], p.cnt["gp"]]
            for i, (iw, oc) in enumerate(jobs):
                s = i % 2
                if i < 2:
                    dma("sync", wst[s], whs[iw][oc * 128:(oc + 1) * 128, :])
                    ldF()
                    ld_mark[s] = p.cnt["ld"]
                p.wait("vector", "ld", ld_mark[s])
                p.wait("vector", "mm")
                dequant_tile(wst[s], D, sblk, sfull, trash)
                p.wait("tensor", "vv")
                transpose_chunks(wst[s], D, cache)
                if i + 2 < len(jobs):
                    iw2, oc2 = jobs[i + 2]
                    p.wait("sync", "vv")
                    p.wait("sync", "mm")
                    dma("sync", wst[s], whs[iw2][oc2 * 128:(oc2 + 1) * 128, :])
                    ldF()
                    ld_mark[s] = p.cnt["ld"]
                for qb in range(NTT):
                    bk = qb % 2
                    es = evs[qb % 2]
                    for ic in range(NDC):
                        mm(psb[bk][:, 0:512], cache[:, ic * 128:(ic + 1) * 128],
                           xnb[:, ic * 2048 + qb * 512:ic * 2048 + qb * 512 + 512],
                           ic == 0, ic == NDC - 1, inc=(ic == NDC - 1))
                    p.wait("scalar", "mm")
                    p.wait("scalar", "gp", gp_mark[qb % 2])
                    p.wait("scalar", "vv")
                    scopy(es, psb[bk][:, 0:512], inc=True)
                    p.wait("vector", "ss")
                    chainmax(es, mcols[iw], first_chain[iw])
                    first_chain[iw] = False
                    p.wait("gpsimd", "vv")
                    dma("gpsimd", spds[iw].ap()[oc, b * NTT + qb, :, :], es, sem="st")
                    stF()
                    gp_mark[qb % 2] = p.cnt["gp"]
                    p.wait("tensor", "vv")

    def ffn0_elemwise():
        p.barrier()
        def slots(s):
            o = s * 2048
            return (A4[:, o:o + 512], A4[:, o + 512:o + 1024],
                    A4[:, o + 1024:o + 1536], A4[:, o + 1536:o + 2048])
        ld_mark = [0, 0]
        gp_mark = [0, 0]
        vv_mark = [0, 0]
        tiles = [(oc, tq) for oc in range(NOC0) for tq in range(NB * NTT)]

        def pass_run(which):
            first = True
            for i, (oc, tq) in enumerate(tiles):
                s = i % 2
                ga, gb, gc, up = slots(s)
                if i < 2:
                    if which == 0:
                        dma("sync", ga, gt_d.ap()[oc, tq, :, :])
                    else:
                        dma("sync", ga, t_d.ap()[oc, tq, :, :])
                        dma("sync", up, up_d.ap()[oc, tq, :, :])
                    ldF()
                    ld_mark[s] = p.cnt["ld"]
                p.wait("vector", "ld", ld_mark[s])
                if which == 0:
                    vts(ga.bitcast(I32), ga, tiny[:, 53:54], Op.mult)
                    p.drain("vector")
                    vts(ga, ga.bitcast(I32), tiny[:, 52:53], Op.mult, inc=True)
                    p.drain("vector")
                    p.wait("scalar", "vv")
                    act(gb, ga, AF.Sigmoid, inc=True)
                    p.wait("vector", "ss")
                    vts(gb.bitcast(I32), gb, 65536.0, Op.mult)
                    p.drain("vector")
                    vts(gb, gb.bitcast(I32), 65535.0, Op.min)
                    p.drain("vector")
                    vts(gb, gb, 1.0 / 65536.0, Op.mult)
                    p.drain("vector")
                    vtt(gc, ga, gb, Op.mult, inc=True)
                    p.drain("vector")
                    chainmax(gc, 46, first)
                else:
                    vts(ga.bitcast(I32), ga, tiny[:, 55:56], Op.mult)
                    p.drain("vector")
                    vts(ga, ga.bitcast(I32), tiny[:, 54:55], Op.mult)
                    vts(up.bitcast(I32), up, tiny[:, 51:52], Op.mult)
                    p.drain("vector")
                    vts(up, up.bitcast(I32), tiny[:, 50:51], Op.mult)
                    p.drain("vector")
                    vtt(gc, ga, up, Op.mult, inc=True)
                    p.drain("vector")
                    chainmax(gc, 47, first)
                first = False
                vv_mark[s] = p.cnt["vv"]
                p.wait("gpsimd", "vv", vv_mark[s])
                if which == 0:
                    dma("gpsimd", t_d.ap()[oc, tq, :, :], gc, sem="st")
                else:
                    dma("gpsimd", or_d.ap()[oc, tq, :, :], gc, sem="st")
                stF()
                gp_mark[s] = p.cnt["gp"]
                j = i + 2
                if j < len(tiles):
                    oc2, tq2 = tiles[j]
                    p.wait("sync", "vv", vv_mark[s])
                    p.wait("sync", "gp", gp_mark[s])
                    if which == 0:
                        dma("sync", ga, gt_d.ap()[oc2, tq2, :, :])
                    else:
                        dma("sync", ga, t_d.ap()[oc2, tq2, :, :])
                        dma("sync", up, up_d.ap()[oc2, tq2, :, :])
                    ldF()
                    ld_mark[s] = p.cnt["ld"]

        pass_run(0)
        finish_scales([(46, 54, 55, 32767.0, 1e-12)])
        pass_run(1)
        finish_scales([(47, 56, 57, 32767.0, 1e-12)])

    def prep_wdownT():
        p.barrier()
        wstage = A4[:, 0:3072]
        sblk = A4[:, 3072:3168]
        sfull = A2[:, 0:3072]
        trash = A2[:, 3072:6144]
        cachec = A2[:, 6144:6144 + 3072]
        for dc in range(NDC):
            p.wait("sync", "vv")
            stF(); p.wait("sync", "gp")
            dma("sync", wstage, wdown_h[dc * 128:(dc + 1) * 128, :])
            ldF(); p.wait("vector", "ld")
            stF(); p.wait("vector", "gp")
            dequant_tile(wstage, I0, sblk, sfull, trash)
            p.wait("tensor", "vv")
            transpose_chunks(wstage, I0, cachec)
            p.wait("gpsimd", "ss")
            dma("gpsimd", wdT_d.ap()[dc, :, :, :].rearrange("c p m -> p c m"),
                cachec.rearrange("p (c m) -> p c m", c=NOC0), sem="st")
            stF(); p.wait("scalar", "gp")

    def ffn0_down_combine():
        p.barrier()
        gtl = A4[:, 3712:4224]
        xtl = A4[:, 4224:4736]
        stl = A4[:, 4736:5248]
        wdc0 = A4[:, 0:3072]
        oq = A1
        for b in range(NB):
            for qb in range(NTT):
                tq = b * NTT + qb
                for oc in range(NOC0):
                    sl = oq[:, oc * 512:(oc + 1) * 512]
                    p.wait("sync", "vv")
                    dma("sync", sl, or_d.ap()[oc, tq, :, :])
                    ldF(); p.wait("vector", "ld")
                    vts(sl.bitcast(I32), sl, tiny[:, 57:58], Op.mult)
                    p.drain("vector")
                    vts(sl, sl.bitcast(I32), tiny[:, 56:57], Op.mult, inc=True)
                    p.drain("vector")
                for dc in range(NDC):
                    p.wait("sync", "mm")
                    dma("sync", wdc0.rearrange("p (c m) -> p c m", c=NOC0),
                        wdT_d.ap()[dc, :, :, :].rearrange("c p m -> p c m"))
                    ldF(); p.wait("tensor", "ld")
                    p.wait("tensor", "vv")
                    bk = dc % 2
                    for oc in range(NOC0):
                        mm(psb[bk][:, 0:512], wdc0[:, oc * 128:(oc + 1) * 128],
                           oq[:, oc * 512:(oc + 1) * 512], oc == 0, oc == NOC0 - 1,
                           inc=(oc == NOC0 - 1))
                    p.wait("scalar", "mm")
                    p.wait("scalar", "vv")
                    act(gtl, psb[bk][:, 0:512], AF.Silu, inc=True)
                    p.wait("sync", "ss")
                    dma("sync", xtl, x_h[b, dc * 128:(dc + 1) * 128, qb * 512:(qb + 1) * 512])
                    dma("sync", stl, s_d.ap()[b, dc * 128:(dc + 1) * 128, qb * 512:(qb + 1) * 512])
                    ldF(); p.wait("vector", "ld")
                    p.wait("vector", "ss")
                    vtt(gtl, gtl, stl, Op.mult)
                    p.drain("vector")
                    vtt(gtl, gtl, xtl, Op.add, inc=True)
                    p.drain("vector")
                    p.wait("gpsimd", "vv")
                    dma("gpsimd", x1_d.ap()[b, dc * 128:(dc + 1) * 128, qb * 512:(qb + 1) * 512],
                        gtl, sem="st")
                    if debug:
                        dma("gpsimd", dbg["d_x1"][b, dc * 128:(dc + 1) * 128,
                                                  qb * 512:(qb + 1) * 512], gtl, sem="st")
                    stF(); p.wait("vector", "gp")
                    stF(); p.wait("scalar", "gp")

    def prep_ternary(w_h_, nrows, ncols, wscol, t_dram):
        p.barrier()
        stage = A4[:, 0:ncols]
        astage = A2[:, 0:ncols]
        srow = A2[0:1, 4096:4097]
        nrc = nrows // 128
        for rc in range(nrc):
            p.wait("sync", "mm")
            p.wait("sync", "ss")
            dma("sync", stage, w_h_[rc * 128:(rc + 1) * 128, :])
            ldF(); p.wait("scalar", "ld")
            p.wait("scalar", "mm")
            act(astage, stage, AF.Abs, inc=True)
            p.wait("tensor", "ss")
            for lo in range(0, ncols, 512):
                w = min(512, ncols - lo)
                mm(psb[6][0:1, 0:w], ones[:, :], astage[:, lo:lo + w],
                   rc == 0 and lo == 0, rc == nrc - 1 and lo + 512 >= ncols,
                   inc=(lo + 512 >= ncols))
        p.wait("vector", "mm")
        vred(srow, psb[6][0:1, 0:512], Op.add)
        p.drain("vector")
        vts(srow, srow, 1.0 / (nrows * ncols), Op.mult)
        p.drain("vector")
        vts(srow, srow, 1e-5, Op.max)
        p.drain("vector")
        vcopy(tiny[0:1, wscol + 1:wscol + 2], srow)
        p.drain("vector")
        vrecip(tiny[0:1, wscol:wscol + 1], srow, inc=True)
        p.drain("vector")
        p.wait("tensor", "vv")
        bcast_col(2, wscol + 2, wscol)
        bcast_col(3, wscol + 6, wscol + 1)
        bstage = BFA[:, 0:512]
        for rc in range(nrc):
            p.wait("sync", "vv")
            stF(); p.wait("sync", "gp")
            dma("sync", stage, w_h_[rc * 128:(rc + 1) * 128, :])
            ldF(); p.wait("vector", "ld")
            p.wait("vector", "mm")
            vts(stage.bitcast(I32), stage, tiny[:, wscol + 2:wscol + 3], Op.mult)
            p.drain("vector")
            vts(stage, stage.bitcast(I32), 1.0, Op.min)
            p.drain("vector")
            vts(stage, stage, -1.0, Op.max, inc=True)
            p.drain("vector")
            p.wait("tensor", "vv")
            for ic in range(ncols // 128):
                bk = 2 + (ic % 2)
                stF(); p.wait("tensor", "gp")
                ttrans(psb[bk][:, 0:128], stage[:, ic * 128:(ic + 1) * 128], inc=True)
                p.wait("vector", "mm")
                vcopy(bstage[:, 0:128], psb[bk][:, 0:128], inc=True)
                p.drain("vector")
                p.wait("gpsimd", "vv")
                dma("gpsimd", t_dram.ap()[ic, rc, :, :], bstage[:, 0:128], sem="st")
                stF(); p.wait("vector", "gp")

    def lru(b):
        p.barrier()
        cosb = A3[:, 0:2048]
        sinb = A3[:, 2048:4096]
        u_re = A3[:, 4096:6144]
        u_im = A3[:, 6144:8192]
        w_re = A4[:, 0:2048]
        w_im = A4[:, 2048:4096]
        rbt = A4[:, 4096:6144]
        ring = A4[:, 6144:7168]
        evsl = [A2[:, 2560:3072], A2[:, 3072:3584]]
        gp_ev = [p.cnt["gp"], p.cnt["gp"]]
        ss_bank = [p.cnt["ss"], p.cnt["ss"]]
        tr1 = A2[:, 0:512]
        tr2 = A2[:, 512:1024]
        kint = A2[:, 1024:1536]
        rem = A2[:, 1536:2048]
        iorow = A2[0:1, 2048:2560]
        for pc in range(8):
            p.wait("sync", "vv")
            p.wait("sync", "ss")
            dma("sync", prow[0:1, 0:128], par_d.ap()[0:1, pc * 128:pc * 128 + 128])
            dma("sync", prow[0:1, 128:256], par_d.ap()[2:3, pc * 128:pc * 128 + 128])
            dma("sync", prow[0:1, 256:384], par_d.ap()[1:2, pc * 128:pc * 128 + 128])
            ldF(); p.wait("tensor", "ld")
            col_from_prow(4, 7, 0)
            col_from_prow(4, 8, 128)
            vts(tiny[:, 9:10], tiny[:, 8:9], -1.0, Op.mult)
            p.drain("vector")
            vmemset(rbt, 1.0)
            p.drain("vector")
            vts(rbt, rbt, tiny[:, 7:8], Op.mult, inc=True)
            p.drain("vector")
            for nt in range(NTT):
                tsl = slice(nt * 512, nt * 512 + 512)
                p.wait("tensor", "vv")
                p.wait("tensor", "ss")
                mm(psb[6][0:128, 0:512], prow[0:1, 256:384], iotas[0:1, tsl], True, True,
                   inc=True)
                p.wait("vector", "mm")
                vts(kint.bitcast(I32), psb[6][:, 0:512], INV2PI, Op.mult)
                p.drain("vector")
                vstt(rem, kint.bitcast(I32), -TWO_PI_HI, Op.mult, psb[6][:, 0:512], Op.add)
                p.drain("vector")
                vstt(rem, kint.bitcast(I32), -TWO_PI_LO, Op.mult, rem, Op.add)
                p.drain("vector")
                vts(rem, rem, 3.141592, Op.min)
                p.drain("vector")
                vts(rem, rem, -3.141592, Op.max, inc=True)
                p.drain("vector")
                p.wait("scalar", "vv")
                act(sinb[:, tsl], rem, AF.Sin, inc=True)
                p.wait("vector", "ss")
                vts(kint.bitcast(I32), psb[6][:, 0:512], INV2PI, Op.mult, 0.25, Op.add)
                p.drain("vector")
                vstt(rem, kint.bitcast(I32), -TWO_PI_HI, Op.mult, psb[6][:, 0:512], Op.add)
                p.drain("vector")
                vstt(rem, kint.bitcast(I32), -TWO_PI_LO, Op.mult, rem, Op.add)
                p.drain("vector")
                vts(rem, rem, HPI, Op.add)
                p.drain("vector")
                vts(rem, rem, 3.141592, Op.min)
                p.drain("vector")
                vts(rem, rem, -3.141592, Op.max, inc=True)
                p.drain("vector")
                p.wait("scalar", "vv")
                act(cosb[:, tsl], rem, AF.Sin, inc=True)
                p.wait("vector", "ss")
            p.wait("tensor", "ss")
            for hc in range(NDC):
                p.wait("sync", "mm")
                dma("sync", ring[:, 0:128],
                    bre_h[hc * 128:(hc + 1) * 128, pc * 128:(pc + 1) * 128])
                dma("sync", ring[:, 128:256],
                    bim_h[hc * 128:(hc + 1) * 128, pc * 128:(pc + 1) * 128])
                ldF(); p.wait("tensor", "ld")
                for nt in range(NTT):
                    xsl = xnb[:, hc * 2048 + nt * 512:hc * 2048 + nt * 512 + 512]
                    mm(psb[nt][:, 0:512], ring[:, 0:128], xsl, hc == 0, hc == NDC - 1)
                    mm(psb[4 + nt][:, 0:512], ring[:, 128:256], xsl, hc == 0, hc == NDC - 1,
                       inc=(nt == NTT - 1))
            p.wait("vector", "mm")
            for nt in range(NTT):
                tsl = slice(nt * 512, nt * 512 + 512)
                vtt(tr1, psb[nt][:, 0:512], cosb[:, tsl], Op.mult)
                vtt(tr2, psb[4 + nt][:, 0:512], sinb[:, tsl], Op.mult)
                p.drain("vector")
                vtt(u_re[:, tsl], tr1, tr2, Op.add)
                p.drain("vector")
                vtt(tr1, psb[4 + nt][:, 0:512], cosb[:, tsl], Op.mult)
                vtt(tr2, psb[nt][:, 0:512], sinb[:, tsl], Op.mult)
                p.drain("vector")
                vtt(u_im[:, tsl], tr1, tr2, Op.subtract, inc=True)
                p.drain("vector")
            p.wait("tensor", "vv")
            vscan(w_re, rbt, u_re)
            p.drain("vector")
            vscan(w_im, rbt, u_im, inc=True)
            p.drain("vector")
            for nt in range(NTT):
                tsl = slice(nt * 512, nt * 512 + 512)
                vtt(tr1, w_re[:, tsl], cosb[:, tsl], Op.mult)
                vtt(tr2, w_im[:, tsl], sinb[:, tsl], Op.mult)
                p.drain("vector")
                vtt(u_re[:, tsl], tr1, tr2, Op.subtract)
                p.drain("vector")
                vtt(tr1, w_im[:, tsl], cosb[:, tsl], Op.mult)
                vtt(tr2, w_re[:, tsl], sinb[:, tsl], Op.mult)
                p.drain("vector")
                vtt(u_im[:, tsl], tr1, tr2, Op.add, inc=True)
                p.drain("vector")
            for hc in range(NDC):
                p.wait("sync", "mm")
                p.wait("sync", "vv")
                dma("sync", ring[:, 256:384],
                    cre_h[pc * 128:(pc + 1) * 128, hc * 128:(hc + 1) * 128])
                dma("sync", ring[:, 384:512],
                    cim_h[pc * 128:(pc + 1) * 128, hc * 128:(hc + 1) * 128])
                if pc == 0:
                    p.wait("sync", "ss")
                    dma("sync", prow[0:1, 384:512],
                        par_d.ap()[3:4, hc * 128:hc * 128 + 128])
                ldF(); p.wait("vector", "ld")
                vts(ring[:, 256:384], ring[:, 256:384], tiny[:, 8:9], Op.mult)
                p.drain("vector")
                vts(ring[:, 384:512], ring[:, 384:512], tiny[:, 9:10], Op.mult, inc=True)
                p.drain("vector")
                if pc == 0:
                    ldF(); p.wait("tensor", "ld")
                    col_from_prow(5, 11, 384)
                p.wait("tensor", "vv")
                for nt in range(NTT):
                    tsl = slice(nt * 512, nt * 512 + 512)
                    sl2 = nt % 2
                    bk = 6 + sl2
                    evx = evsl[sl2]
                    p.wait("tensor", "ss", ss_bank[sl2])
                    mm(psb[bk][:, 0:512], ring[:, 256:384], u_re[:, tsl], True, False)
                    mm(psb[bk][:, 0:512], ring[:, 384:512], u_im[:, tsl], False, True,
                       inc=True)
                    p.wait("scalar", "mm")
                    p.wait("scalar", "gp", gp_ev[sl2])
                    if pc == 0:
                        p.wait("scalar", "vv")
                    scopy(evx, psb[bk][:, 0:512], inc=True)
                    ss_bank[sl2] = p.cnt["ss"]
                    if pc == 0:
                        p.wait("vector", "ss")
                        vstt(evx, xnb[:, hc * 2048 + nt * 512:hc * 2048 + nt * 512 + 512],
                             tiny[:, 11:12], Op.mult, evx, Op.add, inc=True)
                        p.drain("vector")
                        p.wait("gpsimd", "vv")
                        dma("gpsimd", s1_d.ap()[b, hc * 128:(hc + 1) * 128,
                                                nt * 512:nt * 512 + 512], evx, sem="st")
                    else:
                        p.wait("gpsimd", "ss")
                        dma_acc(s1_d.ap()[b, hc * 128:(hc + 1) * 128,
                                          nt * 512:nt * 512 + 512], evx)
                    stF()
                    gp_ev[sl2] = p.cnt["gp"]
        if debug:
            p.barrier()
            for dc in range(NDC):
                dma("gpsimd", dbg["d_s1"][b, dc * 128:(dc + 1) * 128, :],
                    s1_d.ap()[b, dc * 128:(dc + 1) * 128, :], sem="st")
            p.barrier()

    def ffn1(b):
        p.barrier()
        scb = A2[:, 0:512]
        sq = A2[:, 512:1024]
        tmax = A2[:, 1024:1536]
        fbc_all = A2[:, 4608:6656]
        for qb in range(NTT):
            tsl = slice(qb * 512, qb * 512 + 512)
            for dc in range(NDC):
                sl = xnb[:, dc * 2048 + qb * 512:dc * 2048 + qb * 512 + 512]
                if dc == 0:
                    vtt(tmax, sl, sl, Op.mult)
                else:
                    vtt(sq, sl, sl, Op.mult)
                    p.drain("vector")
                    vtt(tmax, tmax, sq, Op.max)
                p.drain("vector")
            p.rec("vector", lambda v: v.memset(tiny[0:1, 63:64], 0.0), inc=[("vv", 1)])
            p.wait("tensor", "vv")
            for k in range(4):
                ttrans(psb[6][:, k * 128:(k + 1) * 128], tmax[:, k * 128:(k + 1) * 128],
                       inc=(k == 3))
            p.wait("vector", "mm")
            mcol = A2[:, 2048:2052]
            vred(mcol, psb[6][:, 0:512].rearrange("p (k i) -> p k i", i=128), Op.max,
                 inc=True)
            p.drain("vector")
            p.wait("scalar", "vv")
            act(mcol, mcol, AF.Sqrt, inc=True)
            p.wait("vector", "ss")
            vts(mcol, mcol, 1e-5, Op.max)
            p.drain("vector")
            fcol = A2[:, 2052:2056]
            vts(fcol, mcol, 1.0 / 127.0, Op.mult)
            p.drain("vector")
            vts(fcol, fcol, tiny[:, 26:27], Op.mult)
            p.drain("vector")
            vrecip(mcol, mcol)
            p.drain("vector")
            vts(mcol, mcol, 127.0, Op.mult, inc=True)
            p.drain("vector")
            p.wait("tensor", "vv")
            ttrans(psb[6][0:4, 0:128], mcol)
            ttrans(psb[7][0:4, 0:128], fcol, inc=True)
            p.wait("scalar", "mm")
            scrows = A2[0:4, 2176:2304]
            frows = A2[0:4, 2304:2432]
            scopy(scrows, psb[6][0:4, 0:128])
            scopy(frows, psb[7][0:4, 0:128], inc=True)
            p.wait("gpsimd", "ss")
            scrow_r = A2[0:1, 2432:2944]
            frow_r = A2[0:1, 2944:3456]
            dma("gpsimd", scrow_r.rearrange("a (k i) -> a k i", i=128), scrows.unsqueeze(2).transpose([0, 2, 1]) if False else scrows, sem="gl")
            dma("gpsimd", frow_r.rearrange("a (k i) -> a k i", i=128), frows, sem="gl")
            glF()
            p.wait("tensor", "gp")
            p.wait("vector", "gp")
            bcast_row(4, scb, scrow_r, 512)
            bcast_row(5, fbc_all[:, qb * 512:qb * 512 + 512], frow_r, 512)
            p.wait("vector", "ss")
            for dc in range(NDC):
                sl = xnb[:, dc * 2048 + qb * 512:dc * 2048 + qb * 512 + 512]
                vtt(sq.bitcast(I32), sl, scb, Op.mult)
                p.drain("vector")
                vcopy(BFA[:, dc * 512:dc * 512 + 512], sq.bitcast(I32), inc=True)
                p.drain("vector")
            p.wait("gpsimd", "vv")
            p.wait("gpsimd", "st")
            for dc in range(NDC):
                dma("gpsimd", xq_d.ap()[b, dc, :, tsl], BFA[:, dc * 512:dc * 512 + 512],
                    sem="st")
            stF(); p.wait("vector", "gp")
        hfull = A3[:, 0:2048]
        h2 = A3[:, 2048:4096]
        hmax = A2[:, 8192:10240]
        t1c = BFA[:, 0:1024]
        rhs = BFA[:, 1024:5120]
        cwc = A2[:, 4096:4099]
        vmemset(hmax, 0.0, inc=True)
        for hc in range(NHC1):
            p.wait("sync", "vv")
            p.wait("sync", "ss")
            dma("sync", t1c.rearrange("p (c m) -> p c m", c=NDC),
                t1_d.ap()[:, hc, :, :].rearrange("c p m -> p c m"))
            dma("sync", cwc, cw_h[hc * 128:(hc + 1) * 128, :])
            dma("sync", A2[0:1, 4100:4228], cb_h[0:1, hc * 128:(hc + 1) * 128])
            ldF(); p.wait("tensor", "ld")
            mm(psb[5][0:128, 0:1], A2[0:1, 4100:4228], ones[0:1, 0:1],
               True, True, inc=True)
            p.wait("scalar", "mm")
            scopy(tiny[:, 12:13], psb[5][0:128, 0:1], inc=True)
            p.wait("vector", "ss")
            p.wait("tensor", "ss")
            for qb in range(NTT):
                p.wait("sync", "mm")
                dma("sync", rhs.rearrange("p (c n) -> p c n", c=NDC),
                    xq_d.ap()[b, :, :, qb * 512:qb * 512 + 512].rearrange("c p n -> p c n"))
                ldF(); p.wait("tensor", "ld")
                for ic in range(NDC):
                    mm(psb[qb % 2][:, 0:512], t1c[:, ic * 128:(ic + 1) * 128],
                       rhs[:, ic * 512:(ic + 1) * 512],
                       ic == 0, ic == NDC - 1, inc=(ic == NDC - 1))
                p.wait("vector", "mm")
                vtt(hfull[:, qb * 512:qb * 512 + 512], psb[qb % 2][:, 0:512],
                    fbc_all[:, qb * 512:qb * 512 + 512], Op.mult, inc=True)
                p.drain("vector")
                p.wait("tensor", "vv")
            vts(h2, hfull, cwc[:, 1:2], Op.mult, tiny[:, 12:13], Op.add)
            p.drain("vector")
            vstt(h2[:, 1:2048], hfull[:, 0:2047], cwc[:, 0:1], Op.mult, h2[:, 1:2048], Op.add)
            p.drain("vector")
            vstt(h2[:, 0:2047], hfull[:, 1:2048], cwc[:, 2:3], Op.mult, h2[:, 0:2047], Op.add,
                 inc=True)
            p.drain("vector")
            p.wait("scalar", "vv")
            act(hfull, h2, AF.Silu, inc=True)
            p.wait("vector", "ss")
            vtt(h2, hfull, hfull, Op.mult)
            p.drain("vector")
            vtt(hmax, hmax, h2, Op.max, inc=True)
            p.drain("vector")
            p.wait("gpsimd", "vv")
            dma("gpsimd", h1_d.ap()[b, hc, :, :], hfull, sem="st")
            stF(); p.wait("vector", "gp")
            stF(); p.wait("scalar", "gp")
        p.wait("tensor", "vv")
        p.wait("tensor", "ss")
        for k in range(16):
            ttrans(psb[4 + k // 4][:, (k % 4) * 128:(k % 4) * 128 + 128],
                   hmax[:, k * 128:(k + 1) * 128], inc=(k == 15))
        p.wait("vector", "mm")
        m16 = A2[:, 2048:2064]
        for bk in range(4):
            vred(m16[:, bk * 4:(bk + 1) * 4],
                 psb[4 + bk][:, 0:512].rearrange("p (k i) -> p k i", i=128), Op.max,
                 inc=(bk == 3))
            p.drain("vector")
        p.wait("scalar", "vv")
        act(m16, m16, AF.Sqrt, inc=True)
        p.wait("vector", "ss")
        vts(m16, m16, 1e-5, Op.max)
        p.drain("vector")
        f16 = A2[:, 2064:2080]
        vts(f16, m16, 1.0 / 127.0, Op.mult)
        p.drain("vector")
        vts(f16, f16, tiny[:, 29:30], Op.mult)
        p.drain("vector")
        vrecip(m16, m16)
        p.drain("vector")
        vts(m16, m16, 127.0, Op.mult, inc=True)
        p.drain("vector")
        p.wait("tensor", "vv")
        ttrans(psb[4][0:16, 0:128], m16)
        ttrans(psb[5][0:16, 0:128], f16, inc=True)
        p.wait("scalar", "mm")
        scrows2 = A2[0:16, 2176:2304]
        frows2 = A2[0:16, 2304:2432]
        scopy(scrows2, psb[4][0:16, 0:128])
        scopy(frows2, psb[5][0:16, 0:128], inc=True)
        p.wait("gpsimd", "ss")
        sc2row_r = A2[0:1, 10240:12288]
        f2row_r = A2[0:1, 6656:8704]
        dma("gpsimd", sc2row_r.rearrange("a (k i) -> a k i", i=128), scrows2, sem="gl")
        dma("gpsimd", f2row_r.rearrange("a (k i) -> a k i", i=128), frows2, sem="gl")
        glF()
        p.wait("tensor", "gp")
        p.wait("vector", "gp")
        t2c = BFA[:, 0:1024]
        sc2b = A2[:, 0:512]
        f2b = A2[:, 512:1024]
        hrt = A2[:, 1024:1536]
        g1t = A2[:, 1536:2048]
        x1t = A2[:, 2560:3072]
        s1t = A2[:, 3072:3584]
        qrh = BFA[:, 5120:5632]
        for qb in range(NTT):
            tsl = slice(qb * 512, qb * 512 + 512)
            p.wait("tensor", "vv")
            p.wait("tensor", "ss")
            bcast_row(4, sc2b, sc2row_r[0:1, tsl], 512)
            bcast_row(5, f2b, f2row_r[0:1, tsl], 512)
            p.wait("sync", "ss")
            for hc in range(NHC1):
                p.wait("sync", "vv")
                dma("sync", hrt, h1_d.ap()[b, hc, :, tsl])
                dma("sync", t2c.rearrange("p (c m) -> p c m", c=NDC),
                    t2_d.ap()[hc, :, :, :].rearrange("c p m -> p c m"))
                ldF(); p.wait("vector", "ld")
                vtt(hrt.bitcast(I32), hrt, sc2b, Op.mult)
                p.drain("vector")
                vcopy(qrh, hrt.bitcast(I32), inc=True)
                p.drain("vector")
                p.wait("tensor", "vv")
                ldF(); p.wait("tensor", "ld")
                for ic in range(NDC):
                    mm(psb[ic][:, 0:512], t2c[:, ic * 128:(ic + 1) * 128], qrh,
                       hc == 0, hc == NHC1 - 1, inc=(ic == NDC - 1))
                p.wait("sync", "mm")
            p.wait("vector", "mm")
            for ic in range(NDC):
                vtt(g1t, psb[ic][:, 0:512], f2b, Op.mult, inc=True)
                p.drain("vector")
                p.wait("scalar", "vv")
                act(g1t, g1t, AF.Silu, inc=True)
                p.wait("sync", "ss")
                dma("sync", x1t, x1_d.ap()[b, ic * 128:(ic + 1) * 128, tsl])
                dma("sync", s1t, s1_d.ap()[b, ic * 128:(ic + 1) * 128, tsl])
                ldF(); p.wait("vector", "ld")
                p.wait("vector", "ss")
                vtt(g1t, g1t, s1t, Op.mult)
                p.drain("vector")
                vtt(g1t, g1t, x1t, Op.add, inc=True)
                p.drain("vector")
                p.wait("gpsimd", "vv")
                dma("gpsimd", out_h[b, ic * 128:(ic + 1) * 128, tsl], g1t, sem="st")
                stF(); p.wait("vector", "gp")
                stF(); p.wait("scalar", "gp")

    ln_and_xn(0, 0, x_h, maxcol=40, first_max=True)
    attention(0)
    ln_and_xn(0, 1, x_h, maxcol=40, first_max=False)
    attention(1)
    finish_scales([(40, 42, 43, 32767.0, 1e-12)])
    ffn0_matmuls()
    finish_scales([(44, 50, 51, 32767.0, 1e-12), (45, 52, 53, 32767.0, 1e-12)])
    ffn0_elemwise()
    prep_wdownT()
    ffn0_down_combine()
    p.barrier()
    if debug:
        dma("gpsimd", dbg["d_tiny"][:, :], tiny[:, :], sem="st")
        dma("gpsimd", dbg["d_upt"][:, :], up_d.ap()[0, 0, :, :], sem="st")
        dma("gpsimd", dbg["d_gtt"][:, :], gt_d.ap()[0, 0, :, :], sem="st")
        dma("gpsimd", dbg["d_ot"][:, :], or_d.ap()[0, 0, :, :], sem="st")
        p.barrier()
    prep_ternary(w1_h, I1, D, 20, t1_d)
    prep_ternary(w2_h, D, I1, 23, t2_d)
    prw = A2[0:1, 0:1024]
    for i, src in enumerate((nulog_h, thlog_h, gml_h, dvec_h)):
        p.wait("sync", "ss")
        stF(); p.wait("sync", "gp")
        dma("sync", prw, src[0:1, :])
        ldF(); p.wait("scalar", "ld")
        if i == 0:
            act(prw, prw, AF.Exp, inc=True)
            p.drain("scalar")
            act(prw, prw, AF.Exp, scale=-1.0, inc=True)
        elif i in (1, 2):
            act(prw, prw, AF.Exp, inc=True)
        else:
            scopy(prw, prw, inc=True)
        p.wait("gpsimd", "ss")
        dma("gpsimd", par_d.ap()[i:i + 1, :], prw, sem="st")
        stF(); p.wait("scalar", "gp")
    p.barrier()
    for b in range(NB):
        ln_and_xn(1, b, x1_d.ap())
        lru(b)
        ffn1(b)
    p.barrier()
    blk = eN(nc.Block())
    p.emit(blk)
    stk.close()
    return nc, dbg


_IOTA, _BANDM = None, None


def kernel(**inputs):
    global _IOTA, _BANDM
    if _IOTA is None:
        _IOTA, _BANDM = _host_constants()
    nc, dbg = build_nc(DEBUG)
    idn = np.eye(128, dtype=np.float32)
    x = np.asarray(inputs["L0_attn_hidden_states"])
    in_maps = []
    for c in range(NC):
        shard = np.ascontiguousarray(x[c * NB:(c + 1) * NB].transpose(0, 2, 1))
        in_maps.append({
            "x": shard,
            "wq": np.asarray(inputs["L0_attn_Wq"]),
            "wk": np.asarray(inputs["L0_attn_Wk"]),
            "wv": np.asarray(inputs["L0_attn_Wv"]),
            "wo": np.asarray(inputs["L0_attn_Wo"]),
            "relb": np.asarray(inputs["L0_attn_rel_bias"]),
            "wgate": np.asarray(inputs["L0_ffn_w_gate"]),
            "wup": np.asarray(inputs["L0_ffn_w_up"]),
            "wdown": np.asarray(inputs["L0_ffn_w_down"]),
            "nulog": np.asarray(inputs["L1_attn_nu_log"])[None, :],
            "thlog": np.asarray(inputs["L1_attn_theta_log"])[None, :],
            "bre": np.ascontiguousarray(np.asarray(inputs["L1_attn_B_re"]).T),
            "bim": np.ascontiguousarray(np.asarray(inputs["L1_attn_B_im"]).T),
            "cre": np.ascontiguousarray(np.asarray(inputs["L1_attn_C_re"]).T),
            "cim": np.ascontiguousarray(np.asarray(inputs["L1_attn_C_im"]).T),
            "dvec": np.asarray(inputs["L1_attn_D"])[None, :],
            "gml": np.asarray(inputs["L1_attn_gamma_log"])[None, :],
            "w1": np.asarray(inputs["L1_ffn_w1"]),
            "cw": np.ascontiguousarray(np.asarray(inputs["L1_ffn_conv_w"]).reshape(I1, 3)),
            "cb": np.asarray(inputs["L1_ffn_conv_b"])[None, :],
            "w2": np.asarray(inputs["L1_ffn_w2"]),
            "iota": _IOTA,
            "bandm": _BANDM,
            "ident_h": idn,
        })
    res = run_bass_kernel_spmd(nc, in_maps, core_ids=list(range(NC)))
    out = np.empty((B, S, D), dtype=np.float32)
    for c in range(NC):
        o = np.asarray(res.results[c]["out"])
        out[c * NB:(c + 1) * NB] = o.transpose(0, 2, 1)
    return out


# version 08 (final)
# speedup vs baseline: 1.4155x; 1.1730x over previous
import sys
if '/opt/trn_rl_repo' not in sys.path:
    sys.path.insert(0, '/opt/trn_rl_repo')
import math
from contextlib import ExitStack
import numpy as np
import concourse.bass as bass
import concourse.mybir as mybir
from concourse.bass_utils import run_bass_kernel_spmd
from concourse.alu_op_type import AluOpType as Op

F32 = mybir.dt.float32
BF16 = mybir.dt.bfloat16
I32 = mybir.dt.int32
AF = mybir.ActivationFunctionType
AX = mybir.AxisListType

NC = 8
B, S, D = 16, 2048, 1024
NB = B // NC
H, DK = 16, 64
I0, I1 = 3072, 4096
NDC = D // 128
NTT = S // 512
NKT = S // 128
NOC0 = I0 // 128
NHC1 = I1 // 128
TWO_PI_HI = float(np.float32(2.0 * math.pi))
TWO_PI_LO = float(np.float32(2.0 * math.pi - np.float64(np.float32(2.0 * math.pi))))
INV2PI = float(np.float32(1.0 / (2.0 * math.pi)))
HPI = float(np.float32(0.5 * math.pi))
NUM_BUCKETS, MAX_DISTANCE = 32, 128
SLABW = 308
JSPLIT = 154
DEBUG = 0


def _bucket(rp):
    nb = NUM_BUCKETS // 2
    rb = nb if rp > 0 else 0
    rel = abs(rp)
    me = nb // 2
    if rel < me:
        return rb + rel
    f = me + int(math.log(max(rel, 1) / me) / math.log(MAX_DISTANCE / me) * (nb - me))
    return rb + min(f, nb - 1)


def _host_constants():
    iota = np.arange(S, dtype=np.float32)[None, :]
    M = np.zeros((NUM_BUCKETS, 128 * SLABW), dtype=np.float32)
    for kp in range(128):
        for jq in range(SLABW):
            d = kp - jq + 90
            base = _bucket(91) if jq < JSPLIT else _bucket(-91)
            idx = kp * SLABW + jq
            M[_bucket(d), idx] += 1.0
            M[base, idx] -= 1.0
    return iota, M


class Prog:
    def __init__(self, nc, stk):
        self.nc = nc
        self.ops = {k: [] for k in ("sync", "vector", "scalar", "tensor", "gpsimd")}
        self.sems = {}
        self.cnt = {}
        for s in ("dl", "st", "mm", "vv", "ss", "cc", "ld", "gp", "gl"):
            self.sems[s] = stk.enter_context(nc.semaphore(s))
            self.cnt[s] = 0

    def rec(self, eng, fn, inc=None):
        tgt = []
        if inc:
            for snm, n in inc:
                self.cnt[snm] += n
                tgt.append((self.sems[snm], n))

        def run(e, fn=fn, tgt=tgt):
            r = fn(e)
            for s, n in tgt:
                r.then_inc(s, n)
        self.ops[eng].append(run)

    def wait(self, eng, snm, val=None):
        v = self.cnt[snm] if val is None else val
        s = self.sems[snm]
        self.ops[eng].append(lambda e, s=s, v=v: e.wait_ge(s, v))
        return v

    def drain(self, eng):
        self.ops[eng].append(lambda e: e.drain())

    def barrier(self):
        if self.cnt["dl"] > 0:
            self.wait("sync", "dl")
            self.rec("sync", lambda E: E.nop(), inc=[("ld", 1)])
        if self.cnt["st"] > 0 or self.cnt["gl"] > 0:
            self.wait("gpsimd", "st")
            self.wait("gpsimd", "gl")
            self.rec("gpsimd", lambda E: E.nop(), inc=[("gp", 1)])
        snap = dict(self.cnt)
        for eng in self.ops:
            for snm in ("mm", "vv", "ss", "cc", "ld", "gp"):
                v = snap[snm]
                if v > 0:
                    self.wait(eng, snm, v)

    def emit(self, blk):
        for eng in ("sync", "vector", "scalar", "tensor", "gpsimd"):
            ops = self.ops[eng]
            getattr(blk, eng)(lambda e, ops=ops: [op(e) for op in ops])


def build_nc(debug=DEBUG):
    nc = bass.Bass("TRN2", target_bir_lowering=False)
    dt = nc.dram_tensor

    x_h = dt("x", [NB, D, S], F32, kind="ExternalInput").ap()
    wq_h = dt("wq", [D, D], F32, kind="ExternalInput").ap()
    wk_h = dt("wk", [D, D], F32, kind="ExternalInput").ap()
    wv_h = dt("wv", [D, D], F32, kind="ExternalInput").ap()
    wo_h = dt("wo", [D, D], F32, kind="ExternalInput").ap()
    rb_h = dt("relb", [NUM_BUCKETS, H], F32, kind="ExternalInput").ap()
    wgate_h = dt("wgate", [I0, D], F32, kind="ExternalInput").ap()
    wup_h = dt("wup", [I0, D], F32, kind="ExternalInput").ap()
    wdown_h = dt("wdown", [D, I0], F32, kind="ExternalInput").ap()
    nulog_h = dt("nulog", [1, D], F32, kind="ExternalInput").ap()
    thlog_h = dt("thlog", [1, D], F32, kind="ExternalInput").ap()
    bre_h = dt("bre", [D, D], F32, kind="ExternalInput").ap()
    bim_h = dt("bim", [D, D], F32, kind="ExternalInput").ap()
    cre_h = dt("cre", [D, D], F32, kind="ExternalInput").ap()
    cim_h = dt("cim", [D, D], F32, kind="ExternalInput").ap()
    dvec_h = dt("dvec", [1, D], F32, kind="ExternalInput").ap()
    gml_h = dt("gml", [1, D], F32, kind="ExternalInput").ap()
    w1_h = dt("w1", [I1, D], F32, kind="ExternalInput").ap()
    cw_h = dt("cw", [I1, 3], F32, kind="ExternalInput").ap()
    cb_h = dt("cb", [1, I1], F32, kind="ExternalInput").ap()
    w2_h = dt("w2", [D, I1], F32, kind="ExternalInput").ap()
    iota_h = dt("iota", [1, S], F32, kind="ExternalInput").ap()
    bandm_h = dt("bandm", [NUM_BUCKETS, 128 * SLABW], F32, kind="ExternalInput").ap()
    ident_h = dt("ident_h", [128, 128], F32, kind="ExternalInput").ap()

    out_h = dt("out", [NB, D, S], F32, kind="ExternalOutput").ap()
    dbg = {}
    if debug:
        for nm in ("d_xn", "d_s", "d_x1", "d_s1", "d_g"):
            dbg[nm] = dt(nm, [NB, D, S], F32, kind="ExternalOutput").ap()
        dbg["d_tiny"] = dt("d_tiny", [128, 64], F32, kind="ExternalOutput").ap()
        dbg["d_upt"] = dt("d_upt", [128, 512], F32, kind="ExternalOutput").ap()
        dbg["d_gtt"] = dt("d_gtt", [128, 512], F32, kind="ExternalOutput").ap()
        dbg["d_ot"] = dt("d_ot", [128, 512], F32, kind="ExternalOutput").ap()
        dbg["d_wq"] = dt("d_wq", [128, 1024], F32, kind="ExternalOutput").ap()
        dbg["d_wst"] = dt("d_wst", [128, 1024], F32, kind="ExternalOutput").ap()
        dbg["d_x0"] = dt("d_x0", [128, 2048], F32, kind="ExternalOutput").ap()

    av_d = dt("av_d", [NB, NTT, D, 512], F32)
    s_d = dt("s_d", [NB, D, S], F32)
    x1_d = dt("x1_d", [NB, D, S], F32)
    up_d = dt("up_d", [NOC0, NB * NTT, 128, 512], F32)
    gt_d = dt("gt_d", [NOC0, NB * NTT, 128, 512], F32)
    t_d = dt("t_d", [NOC0, NB * NTT, 128, 512], F32)
    or_d = dt("or_d", [NOC0, NB * NTT, 128, 512], F32)
    s1_d = dt("s1_d", [NB, D, S], F32)
    h1_d = dt("h1_d", [NB, NHC1, 128, S], F32)
    t1_d = dt("t1_d", [NDC, NHC1, 128, 128], BF16)
    t2_d = dt("t2_d", [NHC1, NDC, 128, 128], BF16)
    wdT_d = dt("wdT_d", [NDC, NOC0, 128, 128], F32)
    xq_d = dt("xq_d", [NB, NDC, 128, S], BF16)
    pat_d = dt("pat_d", [H, 128 * SLABW], F32)
    rows_d = dt("rows_d", [16, S], F32)
    par_d = dt("par_d", [4, D], F32)
    ccb_i = dt("ccb_i", [1, 4], F32)
    ccb_o = dt("ccb_o", [1, 4], F32)

    stk = ExitStack()
    eN = stk.enter_context
    p = Prog(nc, stk)
    sbuf = lambda nm, shape, dtp=F32: eN(nc.sbuf_tensor(nm, shape, dtp))

    A1 = sbuf("A1", [128, 16384])
    A2 = sbuf("A2", [128, 12288])
    A3 = sbuf("A3", [128, 8192])
    A4 = sbuf("A4", [128, 7168])
    BFA = sbuf("BFA", [128, 8192], BF16)
    ones = sbuf("ones", [128, 1])
    onesr = sbuf("onesr", [1, 128])
    zt = sbuf("zt", [128, 512])
    tiny = sbuf("tiny", [128, 64])
    rbs = sbuf("rbs", [NUM_BUCKETS, 64])
    prow = sbuf("prow", [1, 640])
    ident = sbuf("ident", [128, 128])
    iotas = sbuf("iotas", [1, S])
    tsp = sbuf("tsp", [32, 32])
    tsp2 = sbuf("tsp2", [32, 32])
    psb = [eN(nc.psum_tensor(f"pb{i}", [128, 512], F32)) for i in range(8)]

    def dma(eng, out, in_, sem="dl"):
        p.rec(eng, lambda E: E.dma_start(out=out, in_=in_), inc=[(sem, 16)])

    def dma_acc(out, in_):
        p.rec("gpsimd", lambda E: E.dma_start(out=out, in_=in_, accum_op=Op.add),
              inc=[("st", 16)])

    def chainmax(src, col, first, absv=True):
        vred(tiny[:, 49:50], src, Op.max, absv=absv)
        p.drain("vector")
        if first:
            vcopy(tiny[:, col:col + 1], tiny[:, 49:50], inc=True)
        else:
            vtt(tiny[:, col:col + 1], tiny[:, col:col + 1], tiny[:, 49:50], Op.max,
                inc=True)
        p.drain("vector")

    def ldF():
        p.wait("sync", "dl")
        p.rec("sync", lambda E: E.nop(), inc=[("ld", 1)])

    def stF():
        p.wait("gpsimd", "st")
        p.rec("gpsimd", lambda E: E.nop(), inc=[("gp", 1)])

    def glF():
        p.wait("gpsimd", "gl")
        p.rec("gpsimd", lambda E: E.nop(), inc=[("gp", 1)])

    def mm(out, lhsT, rhs, start, stop, inc=False):
        p.rec("tensor", lambda t: t.matmul(out, lhsT, rhs, start=start, stop=stop),
              inc=[("mm", 1)] if inc else None)

    def ttrans(out, in_, inc=False, k=128):
        p.rec("tensor", lambda t: t.transpose(out, in_, ident[0:k, 0:k]),
              inc=[("mm", 1)] if inc else None)

    def vcopy(out, in_, inc=False):
        p.rec("vector", lambda v: v.tensor_copy(out, in_), inc=[("vv", 1)] if inc else None)

    def vtt(out, in0, in1, op, inc=False):
        p.rec("vector", lambda v: v.tensor_tensor(out=out, in0=in0, in1=in1, op=op),
              inc=[("vv", 1)] if inc else None)

    def vts(out, in0, s1, op0, s2=None, op1=None, inc=False):
        if s2 is None:
            p.rec("vector", lambda v: v.tensor_scalar(out=out, in0=in0, scalar1=s1,
                                                      scalar2=None, op0=op0),
                  inc=[("vv", 1)] if inc else None)
        else:
            p.rec("vector", lambda v: v.tensor_scalar(out=out, in0=in0, scalar1=s1,
                                                      scalar2=s2, op0=op0, op1=op1),
                  inc=[("vv", 1)] if inc else None)

    def vstt(out, in0, scalar, op0, in1, op1, inc=False):
        p.rec("vector", lambda v: v.scalar_tensor_tensor(out=out, in0=in0, scalar=scalar,
                                                         in1=in1, op0=op0, op1=op1),
              inc=[("vv", 1)] if inc else None)

    def vttr(out, in0, in1, op0, op1, scalar, accum, inc=False):
        p.rec("vector", lambda v: v.tensor_tensor_reduce(out=out, in0=in0, in1=in1, scale=1.0,
                                                         scalar=scalar, op0=op0, op1=op1,
                                                         accum_out=accum),
              inc=[("vv", 1)] if inc else None)

    def vred(out, in_, op, absv=False, inc=False):
        p.rec("vector", lambda v: v.tensor_reduce(out=out, in_=in_, axis=AX.X, op=op,
                                                  apply_absolute_value=absv),
              inc=[("vv", 1)] if inc else None)

    def vrecip(out, in_, inc=False):
        p.rec("vector", lambda v: v.reciprocal(out, in_), inc=[("vv", 1)] if inc else None)

    def vmemset(ap, val, inc=False):
        p.rec("vector", lambda v: v.memset(ap, val), inc=[("vv", 1)] if inc else None)

    def vscan(out, data0, data1, inc=False):
        p.rec("vector", lambda v: v.tensor_tensor_scan(out=out, data0=data0, data1=data1,
                                                       initial=0.0, op0=Op.mult, op1=Op.add),
              inc=[("vv", 1)] if inc else None)

    def act(out, in_, func, bias=None, scale=None, inc=False):
        kw = {}
        if bias is not None:
            kw["bias"] = bias
        if scale is not None:
            kw["scale"] = scale
        p.rec("scalar", lambda s: s.activation(out=out, in_=in_, func=func, **kw),
              inc=[("ss", 1)] if inc else None)

    def scopy(out, in_, inc=False):
        p.rec("scalar", lambda s: s.copy(out, in_), inc=[("ss", 1)] if inc else None)

    def ptree(col, ncols=128):
        p.wait("tensor", "vv")
        ttrans(psb[3][0:1, 0:ncols], tiny[0:ncols, col:col + 1], inc=True, k=ncols)
        p.wait("vector", "mm")
        vred(tiny[0:1, col:col + 1], psb[3][0:1, 0:ncols], Op.max)
        p.drain("vector")

    def preduce_row(src32, tmpT, tmpR, dstrow, width):
        J = width // 32
        p.rec("vector", lambda v: v.transpose(tmpT, src32))
        p.drain("vector")
        vmemset(tmpR[:, 0:64], -3.0e38)
        p.drain("vector")
        vred(tmpR[:, 0:J], tmpT.rearrange("p (j i) -> p j i", i=32), Op.max)
        p.drain("vector")
        p.rec("vector", lambda v: v.transpose(tmpR[:, 64:128], tmpR[:, 0:64]))
        p.drain("vector")
        p.rec("vector", lambda v: v.memset(tiny[0:1, 63:64], 0.0), inc=[("vv", 1)])
        p.wait("gpsimd", "vv")
        if J <= 32:
            dma("gpsimd", dstrow.rearrange("a2 (j q) -> a2 j q", q=32),
                tmpR[0:J, 64:96], sem="gl")
        else:
            for k in range(J // 32):
                dma("gpsimd", dstrow[0:1, 1024 * k:1024 * k + 1024]
                    .rearrange("a2 (a q) -> a2 a q", q=32),
                    tmpR[:, 64 + 32 * k:96 + 32 * k], sem="gl")
        glF()
        p.wait("vector", "gp")

    def bcast_row(bank, dst, row, n, np_=128):
        for lo in range(0, n, 512):
            w = min(512, n - lo)
            mm(psb[bank][0:np_, 0:w], onesr[0:1, 0:np_], row[0:1, lo:lo + w], True, True,
               inc=True)
            p.wait("scalar", "mm")
            scopy(dst[:, lo:lo + w], psb[bank][0:np_, 0:w], inc=True)
            p.wait("tensor", "ss")

    def bcast_col(bank, dstcol, srccol):
        mm(psb[bank][0:128, 0:1], onesr[0:1, 0:128], tiny[0:1, srccol:srccol + 1],
           True, True, inc=True)
        p.wait("scalar", "mm")
        scopy(tiny[:, dstcol:dstcol + 1], psb[bank][0:128, 0:1], inc=True)
        p.wait("tensor", "ss")
        p.wait("vector", "ss")

    def col_from_prow(bank, dstcol, off):
        mm(psb[bank][0:128, 0:1], prow[0:1, off:off + 128], ones[0:1, 0:1], True, True,
           inc=True)
        p.wait("scalar", "mm")
        scopy(tiny[:, dstcol:dstcol + 1], psb[bank][0:128, 0:1], inc=True)
        p.wait("tensor", "ss")
        p.wait("vector", "ss")

    vmemset(ones[:, :], 1.0)
    vmemset(onesr[:, :], 1.0)
    vmemset(tiny[:, :], 0.0)
    vmemset(zt[:, :], 0.0, inc=True)
    dma("sync", rbs[:, 0:H], rb_h[:, :])
    dma("sync", rbs[0:1, 32:32 + H], rb_h[31:32, :])
    dma("sync", iotas[0:1, :], iota_h[0:1, :])
    dma("sync", rbs[0:1, 48:48 + H], rb_h[15:16, :])
    dma("sync", ident[:, :], ident_h[:, :])
    nby = 128 * SLABW
    p.wait("tensor", "vv")
    stg = A4[0:NUM_BUCKETS, 0:512]
    pev = A4[0:H, 512:1024]
    for ci in range((nby + 511) // 512):
        lo = ci * 512
        w = min(512, nby - lo)
        stF(); p.wait("sync", "gp")
        dma("sync", stg[:, 0:w], bandm_h[:, lo:lo + w])
        ldF(); p.wait("tensor", "ld")
        bk = ci % 2
        mm(psb[bk][0:H, 0:w], rbs[:, 0:H], stg[:, 0:w], True, True, inc=True)
        p.wait("scalar", "mm")
        scopy(pev[:, 0:w], psb[bk][0:H, 0:w], inc=True)
        p.wait("gpsimd", "ss")
        dma("gpsimd", pat_d.ap()[:, lo:lo + w], pev[:, 0:w], sem="st")
        stF(); p.wait("scalar", "gp")
        pass
    stF(); p.wait("sync", "gp")
    patt = A3[:, 0:H * SLABW]
    dma("sync", patt.rearrange("p (h j) -> p h j", h=H),
        pat_d.ap().rearrange("h (p j) -> p h j", p=128))
    ldF(); p.wait("tensor", "ld")
    bcast_row(2, tiny[:, 0:H], rbs[0:1, 32:32 + H], H)
    bcast_row(3, tiny[:, H:2 * H], rbs[0:1, 48:48 + H], H)
    p.wait("vector", "ss")
    vred(tiny[0:NUM_BUCKETS, 33:34], rbs[:, 0:H], Op.max, absv=True, inc=True)
    p.drain("vector")
    ptree(33, 32)
    vcopy(tiny[0:1, 34:35], tiny[0:1, 33:34], inc=True)
    p.wait("tensor", "vv")
    bcast_col(2, 34, 34)
    p.barrier()

    xnb = A1
    mbc = A2[:, 8192:10240]
    rbc = A2[:, 10240:12288]
    sxr = A2[0:1, 4096:6144]
    sx2r = A2[0:1, 6144:8192]

    def ln_and_xn(li, b, src_d, maxcol=None, first_max=False):
        xt = A4[:, 0:2048]
        x2t = A4[:, 2048:4096]
        for dc in range(NDC):
            p.wait("sync", "mm")
            p.wait("sync", "vv")
            dma("sync", xt, src_d[b, dc * 128:(dc + 1) * 128, :])
            ldF(); p.wait("scalar", "ld")
            p.wait("scalar", "mm")
            act(x2t, xt, AF.Square, inc=True)
            ldF(); p.wait("tensor", "ld")
            p.wait("tensor", "ss")
            for q in range(NTT):
                mm(psb[q][0:1, 0:512], ones[:, :], xt[:, q * 512:(q + 1) * 512],
                   dc == 0, dc == NDC - 1)
                mm(psb[4 + q][0:1, 0:512], ones[:, :], x2t[:, q * 512:(q + 1) * 512],
                   dc == 0, dc == NDC - 1, inc=(q == NTT - 1))
            ldF(); p.wait("vector", "ld")
            vcopy(xnb[:, dc * 2048:(dc + 1) * 2048], xt, inc=True)
        p.wait("vector", "mm")
        for q in range(NTT):
            vts(sxr[0:1, q * 512:(q + 1) * 512], psb[q][0:1, 0:512], 1.0 / D, Op.mult)
            vts(sx2r[0:1, q * 512:(q + 1) * 512], psb[4 + q][0:1, 0:512], 1.0 / D, Op.mult)
        p.drain("vector")
        p.rec("vector", lambda v: v.memset(tiny[0:1, 63:64], 0.0), inc=[("vv", 1)])
        p.wait("gpsimd", "vv")
        dma("gpsimd", rows_d.ap()[4 * li + 2 * b:4 * li + 2 * b + 1, :], sxr, sem="st")
        p.wait("tensor", "vv")
        bcast_row(6, mbc, sxr, 2048)
        stF(); p.wait("vector", "gp")
        p.wait("vector", "ss")
        vtt(sxr, sxr, sxr, Op.mult)
        p.drain("vector")
        vtt(sx2r, sx2r, sxr, Op.subtract)
        p.drain("vector")
        vts(sx2r, sx2r, 1e-5, Op.add)
        p.drain("vector")
        vrecip(sx2r, sx2r, inc=True)
        p.drain("vector")
        p.wait("scalar", "vv")
        act(sx2r, sx2r, AF.Sqrt, inc=True)
        p.wait("gpsimd", "ss")
        dma("gpsimd", rows_d.ap()[4 * li + 2 * b + 1:4 * li + 2 * b + 2, :], sx2r, sem="st")
        p.wait("tensor", "ss")
        bcast_row(7, rbc, sx2r, 2048)
        p.wait("vector", "ss")
        stF(); p.wait("vector", "gp")
        for dc in range(NDC):
            sl = xnb[:, dc * 2048:(dc + 1) * 2048]
            vtt(sl, sl, mbc, Op.subtract)
            p.drain("vector")
            vtt(sl, sl, rbc, Op.mult)
            p.drain("vector")
            if maxcol is not None:
                chainmax(sl, maxcol, first_max and dc == 0)
        p.rec("vector", lambda v: v.memset(tiny[0:1, 63:64], 0.0), inc=[("vv", 1)])
        if debug and li == 0:
            p.barrier()
            for dc in range(NDC):
                dma("gpsimd", dbg["d_xn"][b, dc * 128:(dc + 1) * 128, :],
                    xnb[:, dc * 2048:(dc + 1) * 2048], sem="st")
            p.barrier()

    def rebuild_xn(li, b, src_d):
        dma("sync", sxr, rows_d.ap()[4 * li + 2 * b:4 * li + 2 * b + 1, :])
        dma("sync", sx2r, rows_d.ap()[4 * li + 2 * b + 1:4 * li + 2 * b + 2, :])
        ldF(); p.wait("tensor", "ld")
        bcast_row(6, mbc, sxr, 2048)
        bcast_row(7, rbc, sx2r, 2048)
        p.wait("vector", "ss")
        for dc in range(NDC):
            xt = xnb[:, dc * 2048:(dc + 1) * 2048]
            p.wait("sync", "vv")
            dma("sync", xt, src_d[b, dc * 128:(dc + 1) * 128, :])
            ldF(); p.wait("vector", "ld")
            vtt(xt, xt, mbc, Op.subtract)
            p.drain("vector")
            vtt(xt, xt, rbc, Op.mult, inc=True)
            p.drain("vector")

    def allreduce_max(cols):
        stgr = A2[0:1, 4096:4100]
        vmemset(stgr, -3.0e38)
        p.drain("vector")
        for i, c in enumerate(cols):
            vcopy(stgr[0:1, i:i + 1], tiny[0:1, c:c + 1])
        p.drain("vector")
        p.rec("vector", lambda v: v.memset(tiny[0:1, 63:64], 0.0), inc=[("vv", 1)])
        p.wait("gpsimd", "vv")
        dma("gpsimd", ccb_i.ap(), stgr, sem="st")
        p.wait("gpsimd", "st")
        p.rec("gpsimd", lambda g: g.collective_compute(
            "AllReduce", Op.max, replica_groups=[list(range(NC))],
            ins=[ccb_i.ap().opt()], outs=[ccb_o.ap().opt()]), inc=[("cc", 1)])
        p.wait("gpsimd", "cc")
        dma("gpsimd", stgr, ccb_o.ap(), sem="gl")
        glF()
        p.wait("vector", "gp")
        for i, c in enumerate(cols):
            vcopy(tiny[0:1, c:c + 1], stgr[0:1, i:i + 1])
        p.drain("vector")
        p.rec("vector", lambda v: v.memset(tiny[0:1, 63:64], 0.0), inc=[("vv", 1)])
        p.barrier()

    def finish_scales(specs):
        for i, (src, _, _, _, _) in enumerate(specs):
            ptree(src)
            vcopy(tiny[0:1, 60 + i:61 + i], tiny[0:1, src:src + 1], inc=True)
            p.drain("vector")
        allreduce_max([60 + i for i in range(len(specs))])
        for i, (_, scol, icol, qmax, floor) in enumerate(specs):
            c = 60 + i
            vts(tiny[0:1, c:c + 1], tiny[0:1, c:c + 1], 1.0 / qmax, Op.mult)
            p.drain("vector")
            vts(tiny[0:1, c:c + 1], tiny[0:1, c:c + 1], floor, Op.max)
            p.drain("vector")
            vrecip(tiny[0:1, 58:59], tiny[0:1, c:c + 1], inc=True)
            p.drain("vector")
            p.wait("tensor", "vv")
            mm(psb[2][0:128, 0:1], onesr[0:1, 0:128], tiny[0:1, c:c + 1], True, True, inc=True)
            mm(psb[3][0:128, 0:1], onesr[0:1, 0:128], tiny[0:1, 58:59], True, True, inc=True)
            p.wait("scalar", "mm")
            scopy(tiny[:, scol:scol + 1], psb[2][0:128, 0:1], inc=True)
            scopy(tiny[:, icol:icol + 1], psb[3][0:128, 0:1], inc=True)
            p.wait("vector", "ss")
            p.wait("tensor", "ss")
        p.barrier()

    def attention(b):
        p.barrier()
        qhi = BFA[:, 0:2048]
        qlo = BFA[:, 2048:4096]
        khi = BFA[:, 4096:6144]
        klo = BFA[:, 6144:8192]
        vbuf = A3[:, 4992:4992 + NKT * 130]
        raws = A2[:, 0:NKT * 512]
        wtes = [A4[:, 0:512], A4[:, 512:1024], A4[:, 5632:6144], A4[:, 6144:6656]]
        wpair = A4[:, 1024:4096]
        avsl2 = [A4[:, 4096:4608], A4[:, 6656:7168]]
        avbc = A4[:, 4608:5120]
        rrow_ = A4[0:1, 5120:5632]
        for hp in range(H // 2):
            p.wait("sync", "mm")
            dma("sync", wpair[:, 0:1024].rearrange("p (c m) -> p c m", c=NDC),
                wq_h.rearrange("(c p) m -> p c m", p=128)[:, :, hp * 128:(hp + 1) * 128])
            dma("sync", wpair[:, 1024:2048].rearrange("p (c m) -> p c m", c=NDC),
                wk_h.rearrange("(c p) m -> p c m", p=128)[:, :, hp * 128:(hp + 1) * 128])
            dma("sync", wpair[:, 2048:3072].rearrange("p (c m) -> p c m", c=NDC),
                wv_h.rearrange("(c p) m -> p c m", p=128)[:, :, hp * 128:(hp + 1) * 128])
            ldF(); p.wait("tensor", "ld")
            p.wait("tensor", "vv")
            p.wait("tensor", "ss")
            for q in range(NTT):
                for dc in range(NDC):
                    mm(psb[0][:, 0:512], wpair[:, dc * 128:(dc + 1) * 128],
                       xnb[:, dc * 2048 + q * 512:dc * 2048 + q * 512 + 512],
                       dc == 0, dc == NDC - 1, inc=(dc == NDC - 1))
                for dc in range(NDC):
                    mm(psb[1][:, 0:512], wpair[:, 1024 + dc * 128:1024 + (dc + 1) * 128],
                       xnb[:, dc * 2048 + q * 512:dc * 2048 + q * 512 + 512],
                       dc == 0, dc == NDC - 1, inc=(dc == NDC - 1))
                p.wait("vector", "mm")
                sl = slice(q * 512, q * 512 + 512)
                vcopy(qhi[:, sl], psb[0][:, 0:512])
                vcopy(khi[:, sl], psb[1][:, 0:512])
                p.drain("vector")
                vtt(qlo[:, sl], psb[0][:, 0:512], qhi[:, sl], Op.subtract)
                vtt(klo[:, sl], psb[1][:, 0:512], khi[:, sl], Op.subtract, inc=True)
                p.drain("vector")
                p.wait("tensor", "vv")
            vmemset(vbuf, 1.0, inc=True)
            p.wait("tensor", "vv")
            for kt in range(NKT):
                for dc in range(NDC):
                    mm(psb[2][:, 0:128],
                       xnb[:, dc * 2048 + kt * 128:dc * 2048 + kt * 128 + 128],
                       wpair[:, 2048 + dc * 128:2048 + (dc + 1) * 128],
                       dc == 0, dc == NDC - 1, inc=(dc == NDC - 1))
                p.wait("vector", "mm")
                vcopy(vbuf[:, kt * 130:kt * 130 + 64], psb[2][:, 0:64])
                vcopy(vbuf[:, kt * 130 + 65:kt * 130 + 129], psb[2][:, 64:128], inc=True)
                p.drain("vector")
                p.wait("tensor", "vv")
            for hh in range(2):
                h = 2 * hp + hh
                hsl = slice(64 * hh, 64 * hh + 64)
                sc_ss = [p.cnt["ss"]] * 4
                av_gp = [p.cnt["gp"]] * 2
                for qb in range(NTT):
                    q0 = qb * 512
                    for kt in range(NKT):
                        bk = kt % 4
                        p.wait("tensor", "ss", sc_ss[bk])
                        trip = ((khi, qhi), (khi, qlo), (klo, qhi))
                        for ti, (ka, qa) in enumerate(trip):
                            mm(psb[bk][:, 0:512], ka[hsl, kt * 128:kt * 128 + 128],
                               qa[hsl, q0:q0 + 512], ti == 0, ti == 2, inc=(ti == 2))
                        p.wait("scalar", "mm")
                        p.wait("scalar", "vv")
                        scopy(raws[:, kt * 512:kt * 512 + 512], psb[bk][:, 0:512], inc=True)
                        sc_ss[bk] = p.cnt["ss"]
                        p.wait("vector", "ss")
                        chainmax(raws[:, kt * 512:kt * 512 + 512], 32, kt == 0, absv=False)
                    for kt in range(NKT):
                        k0 = kt * 128
                        lo = max(k0 - 90, q0)
                        hi = min(k0 + 218, q0 + 512, S)
                        if lo < hi:
                            joff = lo - (k0 - 90)
                            vtt(raws[:, kt * 512 + (lo - q0):kt * 512 + (hi - q0)],
                                raws[:, kt * 512 + (lo - q0):kt * 512 + (hi - q0)],
                                patt[:, h * SLABW + joff:h * SLABW + joff + (hi - lo)],
                                Op.add)
                            p.drain("vector")
                    ptree(32)
                    vcopy(tiny[0:1, 35:36], tiny[0:1, 32:33], inc=True)
                    p.wait("tensor", "vv")
                    bcast_col(6, 35, 35)
                    vtt(tiny[:, 36:37], tiny[:, 35:36], tiny[:, 34:35], Op.add)
                    p.drain("vector")
                    vtt(tiny[:, 37:38], tiny[:, h:h + 1], tiny[:, 36:37], Op.subtract)
                    vtt(tiny[:, 38:39], tiny[:, H + h:H + h + 1], tiny[:, 36:37],
                        Op.subtract, inc=True)
                    p.drain("vector")
                    p.wait("scalar", "vv")
                    mm_mark = [p.cnt["mm"]] * 4
                    for kt in range(NKT):
                        k0 = kt * 128
                        ws4 = kt % 4
                        wsl = wtes[ws4]
                        p.wait("scalar", "mm", mm_mark[ws4])
                        p.drain("scalar")
                        split = min(max(k0 + 64 - q0, 0), 512)
                        if split > 0:
                            act(wsl[:, 0:split], raws[:, kt * 512:kt * 512 + split],
                                AF.Exp, bias=tiny[:, 37:38], inc=(split == 512))
                        if split < 512:
                            act(wsl[:, split:512],
                                raws[:, kt * 512 + split:kt * 512 + 512],
                                AF.Exp, bias=tiny[:, 38:39], inc=True)
                        p.wait("tensor", "ss")
                        mm(psb[4][0:65, 0:512],
                           vbuf[:, kt * 130 + 65 * hh:kt * 130 + 65 * hh + 65],
                           wsl, kt == 0, kt == NKT - 1, inc=True)
                        mm_mark[ws4] = p.cnt["mm"]
                    p.wait("vector", "mm")
                    avx = avsl2[qb % 2]
                    p.wait("vector", "gp", av_gp[qb % 2])
                    vrecip(rrow_, psb[4][64:65, 0:512], inc=True)
                    p.wait("tensor", "vv")
                    bcast_row(5, avbc[0:64, :], rrow_, 512, np_=64)
                    p.wait("scalar", "vv")
                    p.wait("scalar", "gp", av_gp[qb % 2])
                    scopy(avx[0:64, :], psb[4][0:64, 0:512], inc=True)
                    p.wait("vector", "ss")
                    vtt(avx[0:64, :], avx[0:64, :], avbc[0:64, :], Op.mult, inc=True)
                    p.drain("vector")
                    p.wait("gpsimd", "vv")
                    dma("gpsimd", av_d.ap()[b, qb, h * 64:(h + 1) * 64, :], avx[0:64, :],
                        sem="st")
                    stF()
                    av_gp[qb % 2] = p.cnt["gp"]
        p.barrier()
        avl = A4[:, 0:4096]
        wos = A4[:, 4096:5120]
        wev = A4[:, 5120:5632]
        for qb in range(NTT):
            p.wait("sync", "mm")
            for pc in range(NDC):
                dma("sync", avl[:, pc * 512:(pc + 1) * 512],
                    av_d.ap()[b, qb, pc * 128:(pc + 1) * 128, :])
            for dc in range(NDC):
                p.wait("sync", "mm")
                dma("sync", wos.rearrange("p (c m) -> p c m", c=NDC),
                    wo_h.rearrange("(c p) m -> p c m", p=128)[:, :, dc * 128:(dc + 1) * 128])
                ldF(); p.wait("tensor", "ld")
                for pc in range(NDC):
                    mm(psb[dc % 2][:, 0:512], wos[:, pc * 128:(pc + 1) * 128],
                       avl[:, pc * 512:(pc + 1) * 512], pc == 0, pc == NDC - 1,
                       inc=(pc == NDC - 1))
                p.wait("scalar", "mm")
                stF(); p.wait("scalar", "gp")
                scopy(wev, psb[dc % 2][:, 0:512], inc=True)
                p.wait("gpsimd", "ss")
                dma("gpsimd", s_d.ap()[b, dc * 128:(dc + 1) * 128, qb * 512:(qb + 1) * 512],
                    wev, sem="st")
                stF(); p.wait("tensor", "gp")
        if debug:
            p.barrier()
            for dc in range(NDC):
                dma("gpsimd", dbg["d_s"][b, dc * 128:(dc + 1) * 128, :],
                    s_d.ap()[b, dc * 128:(dc + 1) * 128, :], sem="st")
            p.barrier()

    def dequant_tile(stage, nin, sblk, sfull, trash):
        nb_ = nin // 32
        vred(sblk[:, 0:nb_], stage.rearrange("p (nb bs) -> p nb bs", bs=32), Op.max,
             absv=True)
        p.drain("vector")
        for j in range(32):
            vcopy(sfull[:, 0:nin].rearrange("p (nb bs) -> p nb bs", bs=32)[:, :, j:j + 1],
                  sblk[:, 0:nb_].unsqueeze(2))
        p.drain("vector")
        vrecip(trash[:, 0:nin], sfull[:, 0:nin])
        p.drain("vector")
        vstt(stage.bitcast(I32), stage, 7.0, Op.mult, trash[:, 0:nin], Op.mult)
        p.drain("vector")
        vstt(stage, stage.bitcast(I32), 1.0 / 7.0, Op.mult, sfull[:, 0:nin], Op.mult,
             inc=True)
        p.drain("vector")

    def transpose_chunks(stage, nin, cache):
        for ic in range(nin // 128):
            bk = 2 + (ic % 2)
            p.wait("tensor", "ss")
            ttrans(psb[bk][:, 0:128], stage[:, ic * 128:(ic + 1) * 128], inc=True)
            p.wait("scalar", "mm")
            scopy(cache[:, ic * 128:(ic + 1) * 128], psb[bk][:, 0:128], inc=True)
        p.wait("tensor", "ss")

    def ffn0_matmuls():
        p.barrier()
        wst = [A4[:, 0:1024], A4[:, 5632:6656]]
        sblk = A4[:, 1024:1056]
        sfull = A4[:, 2048:3072]
        trash = A4[:, 3072:4096]
        cache = A4[:, 4096:5120]
        evs = [A4[:, 5120:5632], A4[:, 6656:7168]]
        first_chain = [True, True]
        whs = (wup_h, wgate_h)
        spds = (up_d, gt_d)
        mcols = (44, 45)
        for b in (1, 0):
            if b == 0:
                rebuild_xn(0, 0, x_h)
            for dc in range(NDC):
                sl = xnb[:, dc * 2048:(dc + 1) * 2048]
                vts(sl.bitcast(I32), sl, tiny[:, 43:44], Op.mult)
                p.drain("vector")
                vts(sl, sl.bitcast(I32), tiny[:, 42:43], Op.mult, inc=True)
                p.drain("vector")
            jobs = [(iw, oc) for iw in range(2) for oc in range(NOC0)]
            ld_mark = [0, 0]
            gp_mark = [p.cnt["gp"], p.cnt["gp"]]
            for i, (iw, oc) in enumerate(jobs):
                s = i % 2
                if i < 2:
                    dma("sync", wst[s], whs[iw][oc * 128:(oc + 1) * 128, :])
                    ldF()
                    ld_mark[s] = p.cnt["ld"]
                p.wait("vector", "ld", ld_mark[s])
                p.wait("vector", "mm")
                dequant_tile(wst[s], D, sblk, sfull, trash)
                p.wait("tensor", "vv")
                transpose_chunks(wst[s], D, cache)
                if i + 2 < len(jobs):
                    iw2, oc2 = jobs[i + 2]
                    p.wait("sync", "vv")
                    p.wait("sync", "mm")
                    dma("sync", wst[s], whs[iw2][oc2 * 128:(oc2 + 1) * 128, :])
                    ldF()
                    ld_mark[s] = p.cnt["ld"]
                for qb in range(NTT):
                    bk = qb % 2
                    es = evs[qb % 2]
                    for ic in range(NDC):
                        mm(psb[bk][:, 0:512], cache[:, ic * 128:(ic + 1) * 128],
                           xnb[:, ic * 2048 + qb * 512:ic * 2048 + qb * 512 + 512],
                           ic == 0, ic == NDC - 1, inc=(ic == NDC - 1))
                    p.wait("scalar", "mm")
                    p.wait("scalar", "gp", gp_mark[qb % 2])
                    p.wait("scalar", "vv")
                    scopy(es, psb[bk][:, 0:512], inc=True)
                    p.wait("vector", "ss")
                    chainmax(es, mcols[iw], first_chain[iw])
                    first_chain[iw] = False
                    p.wait("gpsimd", "vv")
                    dma("gpsimd", spds[iw].ap()[oc, b * NTT + qb, :, :], es, sem="st")
                    stF()
                    gp_mark[qb % 2] = p.cnt["gp"]
                    p.wait("tensor", "vv")

    def ffn0_elemwise():
        p.barrier()
        def slots(s):
            o = s * 2048
            return (A4[:, o:o + 512], A4[:, o + 512:o + 1024],
                    A4[:, o + 1024:o + 1536], A4[:, o + 1536:o + 2048])
        ld_mark = [0, 0]
        gp_mark = [0, 0]
        vv_mark = [0, 0]
        tiles = [(oc, tq) for oc in range(NOC0) for tq in range(NB * NTT)]

        def pass_run(which):
            first = True
            for i, (oc, tq) in enumerate(tiles):
                s = i % 2
                ga, gb, gc, up = slots(s)
                if i < 2:
                    if which == 0:
                        dma("sync", ga, gt_d.ap()[oc, tq, :, :])
                    else:
                        dma("sync", ga, t_d.ap()[oc, tq, :, :])
                        dma("sync", up, up_d.ap()[oc, tq, :, :])
                    ldF()
                    ld_mark[s] = p.cnt["ld"]
                p.wait("vector", "ld", ld_mark[s])
                if which == 0:
                    vts(ga.bitcast(I32), ga, tiny[:, 53:54], Op.mult)
                    p.drain("vector")
                    vts(ga, ga.bitcast(I32), tiny[:, 52:53], Op.mult, inc=True)
                    p.drain("vector")
                    p.wait("scalar", "vv")
                    act(gb, ga, AF.Sigmoid, inc=True)
                    p.wait("vector", "ss")
                    vts(gb.bitcast(I32), gb, 65536.0, Op.mult)
                    p.drain("vector")
                    vts(gb, gb.bitcast(I32), 65535.0, Op.min)
                    p.drain("vector")
                    vts(gb, gb, 1.0 / 65536.0, Op.mult)
                    p.drain("vector")
                    vtt(gc, ga, gb, Op.mult, inc=True)
                    p.drain("vector")
                    chainmax(gc, 46, first)
                else:
                    vts(ga.bitcast(I32), ga, tiny[:, 55:56], Op.mult)
                    p.drain("vector")
                    vts(ga, ga.bitcast(I32), tiny[:, 54:55], Op.mult)
                    vts(up.bitcast(I32), up, tiny[:, 51:52], Op.mult)
                    p.drain("vector")
                    vts(up, up.bitcast(I32), tiny[:, 50:51], Op.mult)
                    p.drain("vector")
                    vtt(gc, ga, up, Op.mult, inc=True)
                    p.drain("vector")
                    chainmax(gc, 47, first)
                first = False
                vv_mark[s] = p.cnt["vv"]
                p.wait("gpsimd", "vv", vv_mark[s])
                if which == 0:
                    dma("gpsimd", t_d.ap()[oc, tq, :, :], gc, sem="st")
                else:
                    dma("gpsimd", or_d.ap()[oc, tq, :, :], gc, sem="st")
                stF()
                gp_mark[s] = p.cnt["gp"]
                j = i + 2
                if j < len(tiles):
                    oc2, tq2 = tiles[j]
                    p.wait("sync", "vv", vv_mark[s])
                    p.wait("sync", "gp", gp_mark[s])
                    if which == 0:
                        dma("sync", ga, gt_d.ap()[oc2, tq2, :, :])
                    else:
                        dma("sync", ga, t_d.ap()[oc2, tq2, :, :])
                        dma("sync", up, up_d.ap()[oc2, tq2, :, :])
                    ldF()
                    ld_mark[s] = p.cnt["ld"]

        pass_run(0)
        finish_scales([(46, 54, 55, 32767.0, 1e-12)])
        pass_run(1)
        finish_scales([(47, 56, 57, 32767.0, 1e-12)])

    def prep_wdownT():
        p.barrier()
        wstage = A4[:, 0:3072]
        sblk = A4[:, 3072:3168]
        sfull = A2[:, 0:3072]
        trash = A2[:, 3072:6144]
        cachec = A2[:, 6144:6144 + 3072]
        for dc in range(NDC):
            p.wait("sync", "vv")
            stF(); p.wait("sync", "gp")
            dma("sync", wstage, wdown_h[dc * 128:(dc + 1) * 128, :])
            ldF(); p.wait("vector", "ld")
            stF(); p.wait("vector", "gp")
            dequant_tile(wstage, I0, sblk, sfull, trash)
            p.wait("tensor", "vv")
            transpose_chunks(wstage, I0, cachec)
            p.wait("gpsimd", "ss")
            dma("gpsimd", wdT_d.ap()[dc, :, :, :].rearrange("c p m -> p c m"),
                cachec.rearrange("p (c m) -> p c m", c=NOC0), sem="st")
            stF(); p.wait("scalar", "gp")

    def ffn0_down_combine():
        p.barrier()
        gtl = A4[:, 3712:4224]
        xtl = A4[:, 4224:4736]
        stl = A4[:, 4736:5248]
        wdc0 = A4[:, 0:3072]
        oq = A1
        for b in range(NB):
            for qb in range(NTT):
                tq = b * NTT + qb
                for oc in range(NOC0):
                    sl = oq[:, oc * 512:(oc + 1) * 512]
                    p.wait("sync", "vv")
                    dma("sync", sl, or_d.ap()[oc, tq, :, :])
                    ldF(); p.wait("vector", "ld")
                    vts(sl.bitcast(I32), sl, tiny[:, 57:58], Op.mult)
                    p.drain("vector")
                    vts(sl, sl.bitcast(I32), tiny[:, 56:57], Op.mult, inc=True)
                    p.drain("vector")
                for dc in range(NDC):
                    p.wait("sync", "mm")
                    dma("sync", wdc0.rearrange("p (c m) -> p c m", c=NOC0),
                        wdT_d.ap()[dc, :, :, :].rearrange("c p m -> p c m"))
                    ldF(); p.wait("tensor", "ld")
                    p.wait("tensor", "vv")
                    bk = dc % 2
                    for oc in range(NOC0):
                        mm(psb[bk][:, 0:512], wdc0[:, oc * 128:(oc + 1) * 128],
                           oq[:, oc * 512:(oc + 1) * 512], oc == 0, oc == NOC0 - 1,
                           inc=(oc == NOC0 - 1))
                    p.wait("scalar", "mm")
                    p.wait("scalar", "vv")
                    act(gtl, psb[bk][:, 0:512], AF.Silu, inc=True)
                    p.wait("sync", "ss")
                    dma("sync", xtl, x_h[b, dc * 128:(dc + 1) * 128, qb * 512:(qb + 1) * 512])
                    dma("sync", stl, s_d.ap()[b, dc * 128:(dc + 1) * 128, qb * 512:(qb + 1) * 512])
                    ldF(); p.wait("vector", "ld")
                    p.wait("vector", "ss")
                    vtt(gtl, gtl, stl, Op.mult)
                    p.drain("vector")
                    vtt(gtl, gtl, xtl, Op.add, inc=True)
                    p.drain("vector")
                    p.wait("gpsimd", "vv")
                    dma("gpsimd", x1_d.ap()[b, dc * 128:(dc + 1) * 128, qb * 512:(qb + 1) * 512],
                        gtl, sem="st")
                    if debug:
                        dma("gpsimd", dbg["d_x1"][b, dc * 128:(dc + 1) * 128,
                                                  qb * 512:(qb + 1) * 512], gtl, sem="st")
                    stF(); p.wait("vector", "gp")
                    stF(); p.wait("scalar", "gp")

    def prep_ternary(w_h_, nrows, ncols, wscol, t_dram):
        p.barrier()
        stage = A4[:, 0:ncols]
        astage = A2[:, 0:ncols]
        srow = A2[0:1, 4096:4097]
        nrc = nrows // 128
        for rc in range(nrc):
            p.wait("sync", "mm")
            p.wait("sync", "ss")
            dma("sync", stage, w_h_[rc * 128:(rc + 1) * 128, :])
            ldF(); p.wait("scalar", "ld")
            p.wait("scalar", "mm")
            act(astage, stage, AF.Abs, inc=True)
            p.wait("tensor", "ss")
            for lo in range(0, ncols, 512):
                w = min(512, ncols - lo)
                mm(psb[6][0:1, 0:w], ones[:, :], astage[:, lo:lo + w],
                   rc == 0 and lo == 0, rc == nrc - 1 and lo + 512 >= ncols,
                   inc=(lo + 512 >= ncols))
        p.wait("vector", "mm")
        vred(srow, psb[6][0:1, 0:512], Op.add)
        p.drain("vector")
        vts(srow, srow, 1.0 / (nrows * ncols), Op.mult)
        p.drain("vector")
        vts(srow, srow, 1e-5, Op.max)
        p.drain("vector")
        vcopy(tiny[0:1, wscol + 1:wscol + 2], srow)
        p.drain("vector")
        vrecip(tiny[0:1, wscol:wscol + 1], srow, inc=True)
        p.drain("vector")
        p.wait("tensor", "vv")
        bcast_col(2, wscol + 2, wscol)
        bcast_col(3, wscol + 6, wscol + 1)
        bstage = BFA[:, 0:512]
        for rc in range(nrc):
            p.wait("sync", "vv")
            stF(); p.wait("sync", "gp")
            dma("sync", stage, w_h_[rc * 128:(rc + 1) * 128, :])
            ldF(); p.wait("vector", "ld")
            p.wait("vector", "mm")
            vts(stage.bitcast(I32), stage, tiny[:, wscol + 2:wscol + 3], Op.mult)
            p.drain("vector")
            vts(stage, stage.bitcast(I32), 1.0, Op.min)
            p.drain("vector")
            vts(stage, stage, -1.0, Op.max, inc=True)
            p.drain("vector")
            p.wait("tensor", "vv")
            for ic in range(ncols // 128):
                bk = 2 + (ic % 2)
                stF(); p.wait("tensor", "gp")
                ttrans(psb[bk][:, 0:128], stage[:, ic * 128:(ic + 1) * 128], inc=True)
                p.wait("vector", "mm")
                vcopy(bstage[:, 0:128], psb[bk][:, 0:128], inc=True)
                p.drain("vector")
                p.wait("gpsimd", "vv")
                dma("gpsimd", t_dram.ap()[ic, rc, :, :], bstage[:, 0:128], sem="st")
                stF(); p.wait("vector", "gp")

    def lru(b):
        p.barrier()
        cosb = A3[:, 0:2048]
        sinb = A3[:, 2048:4096]
        u_re = A3[:, 4096:6144]
        u_im = A3[:, 6144:8192]
        w_re = A4[:, 0:2048]
        w_im = A4[:, 2048:4096]
        rbt = A4[:, 4096:6144]
        ring = A4[:, 6144:7168]
        evsl = [A2[:, 2560:3072], A2[:, 3072:3584]]
        gp_ev = [p.cnt["gp"], p.cnt["gp"]]
        ss_bank = [p.cnt["ss"], p.cnt["ss"]]
        tr1 = A2[:, 0:512]
        tr2 = A2[:, 512:1024]
        kint = A2[:, 1024:1536]
        rem = A2[:, 1536:2048]
        iorow = A2[0:1, 2048:2560]
        for pc in range(8):
            p.wait("sync", "vv")
            p.wait("sync", "ss")
            dma("sync", prow[0:1, 0:128], par_d.ap()[0:1, pc * 128:pc * 128 + 128])
            dma("sync", prow[0:1, 128:256], par_d.ap()[2:3, pc * 128:pc * 128 + 128])
            dma("sync", prow[0:1, 256:384], par_d.ap()[1:2, pc * 128:pc * 128 + 128])
            ldF(); p.wait("tensor", "ld")
            col_from_prow(4, 7, 0)
            col_from_prow(4, 8, 128)
            vts(tiny[:, 9:10], tiny[:, 8:9], -1.0, Op.mult)
            p.drain("vector")
            vmemset(rbt, 1.0)
            p.drain("vector")
            vts(rbt, rbt, tiny[:, 7:8], Op.mult, inc=True)
            p.drain("vector")
            for nt in range(NTT):
                tsl = slice(nt * 512, nt * 512 + 512)
                p.wait("tensor", "vv")
                p.wait("tensor", "ss")
                mm(psb[6][0:128, 0:512], prow[0:1, 256:384], iotas[0:1, tsl], True, True,
                   inc=True)
                p.wait("vector", "mm")
                vts(kint.bitcast(I32), psb[6][:, 0:512], INV2PI, Op.mult)
                p.drain("vector")
                vstt(rem, kint.bitcast(I32), -TWO_PI_HI, Op.mult, psb[6][:, 0:512], Op.add)
                p.drain("vector")
                vstt(rem, kint.bitcast(I32), -TWO_PI_LO, Op.mult, rem, Op.add)
                p.drain("vector")
                vts(rem, rem, 3.141592, Op.min)
                p.drain("vector")
                vts(rem, rem, -3.141592, Op.max, inc=True)
                p.drain("vector")
                p.wait("scalar", "vv")
                act(sinb[:, tsl], rem, AF.Sin, inc=True)
                p.wait("vector", "ss")
                vts(kint.bitcast(I32), psb[6][:, 0:512], INV2PI, Op.mult, 0.25, Op.add)
                p.drain("vector")
                vstt(rem, kint.bitcast(I32), -TWO_PI_HI, Op.mult, psb[6][:, 0:512], Op.add)
                p.drain("vector")
                vstt(rem, kint.bitcast(I32), -TWO_PI_LO, Op.mult, rem, Op.add)
                p.drain("vector")
                vts(rem, rem, HPI, Op.add)
                p.drain("vector")
                vts(rem, rem, 3.141592, Op.min)
                p.drain("vector")
                vts(rem, rem, -3.141592, Op.max, inc=True)
                p.drain("vector")
                p.wait("scalar", "vv")
                act(cosb[:, tsl], rem, AF.Sin, inc=True)
                p.wait("vector", "ss")
            p.wait("tensor", "ss")
            for hc in range(NDC):
                p.wait("sync", "mm")
                dma("sync", ring[:, 0:128],
                    bre_h[hc * 128:(hc + 1) * 128, pc * 128:(pc + 1) * 128])
                dma("sync", ring[:, 128:256],
                    bim_h[hc * 128:(hc + 1) * 128, pc * 128:(pc + 1) * 128])
                ldF(); p.wait("tensor", "ld")
                for nt in range(NTT):
                    xsl = xnb[:, hc * 2048 + nt * 512:hc * 2048 + nt * 512 + 512]
                    mm(psb[nt][:, 0:512], ring[:, 0:128], xsl, hc == 0, hc == NDC - 1)
                    mm(psb[4 + nt][:, 0:512], ring[:, 128:256], xsl, hc == 0, hc == NDC - 1,
                       inc=(nt == NTT - 1))
            p.wait("vector", "mm")
            for nt in range(NTT):
                tsl = slice(nt * 512, nt * 512 + 512)
                vtt(tr1, psb[nt][:, 0:512], cosb[:, tsl], Op.mult)
                vtt(tr2, psb[4 + nt][:, 0:512], sinb[:, tsl], Op.mult)
                p.drain("vector")
                vtt(u_re[:, tsl], tr1, tr2, Op.add)
                p.drain("vector")
                vtt(tr1, psb[4 + nt][:, 0:512], cosb[:, tsl], Op.mult)
                vtt(tr2, psb[nt][:, 0:512], sinb[:, tsl], Op.mult)
                p.drain("vector")
                vtt(u_im[:, tsl], tr1, tr2, Op.subtract, inc=True)
                p.drain("vector")
            p.wait("tensor", "vv")
            vscan(w_re, rbt, u_re)
            p.drain("vector")
            vscan(w_im, rbt, u_im, inc=True)
            p.drain("vector")
            for nt in range(NTT):
                tsl = slice(nt * 512, nt * 512 + 512)
                vtt(tr1, w_re[:, tsl], cosb[:, tsl], Op.mult)
                vtt(tr2, w_im[:, tsl], sinb[:, tsl], Op.mult)
                p.drain("vector")
                vtt(u_re[:, tsl], tr1, tr2, Op.subtract)
                p.drain("vector")
                vtt(tr1, w_im[:, tsl], cosb[:, tsl], Op.mult)
                vtt(tr2, w_re[:, tsl], sinb[:, tsl], Op.mult)
                p.drain("vector")
                vtt(u_im[:, tsl], tr1, tr2, Op.add, inc=True)
                p.drain("vector")
            for hc in range(NDC):
                p.wait("sync", "mm")
                p.wait("sync", "vv")
                dma("sync", ring[:, 256:384],
                    cre_h[pc * 128:(pc + 1) * 128, hc * 128:(hc + 1) * 128])
                dma("sync", ring[:, 384:512],
                    cim_h[pc * 128:(pc + 1) * 128, hc * 128:(hc + 1) * 128])
                if pc == 0:
                    p.wait("sync", "ss")
                    dma("sync", prow[0:1, 384:512],
                        par_d.ap()[3:4, hc * 128:hc * 128 + 128])
                ldF(); p.wait("vector", "ld")
                vts(ring[:, 256:384], ring[:, 256:384], tiny[:, 8:9], Op.mult)
                p.drain("vector")
                vts(ring[:, 384:512], ring[:, 384:512], tiny[:, 9:10], Op.mult, inc=True)
                p.drain("vector")
                if pc == 0:
                    ldF(); p.wait("tensor", "ld")
                    col_from_prow(5, 11, 384)
                p.wait("tensor", "vv")
                for nt in range(NTT):
                    tsl = slice(nt * 512, nt * 512 + 512)
                    sl2 = nt % 2
                    bk = 6 + sl2
                    evx = evsl[sl2]
                    p.wait("tensor", "ss", ss_bank[sl2])
                    mm(psb[bk][:, 0:512], ring[:, 256:384], u_re[:, tsl], True, False)
                    mm(psb[bk][:, 0:512], ring[:, 384:512], u_im[:, tsl], False, True,
                       inc=True)
                    p.wait("scalar", "mm")
                    p.wait("scalar", "gp", gp_ev[sl2])
                    if pc == 0:
                        p.wait("scalar", "vv")
                    scopy(evx, psb[bk][:, 0:512], inc=True)
                    ss_bank[sl2] = p.cnt["ss"]
                    if pc == 0:
                        p.wait("vector", "ss")
                        vstt(evx, xnb[:, hc * 2048 + nt * 512:hc * 2048 + nt * 512 + 512],
                             tiny[:, 11:12], Op.mult, evx, Op.add, inc=True)
                        p.drain("vector")
                        p.wait("gpsimd", "vv")
                        dma("gpsimd", s1_d.ap()[b, hc * 128:(hc + 1) * 128,
                                                nt * 512:nt * 512 + 512], evx, sem="st")
                    else:
                        p.wait("gpsimd", "ss")
                        dma_acc(s1_d.ap()[b, hc * 128:(hc + 1) * 128,
                                          nt * 512:nt * 512 + 512], evx)
                    stF()
                    gp_ev[sl2] = p.cnt["gp"]
        if debug:
            p.barrier()
            for dc in range(NDC):
                dma("gpsimd", dbg["d_s1"][b, dc * 128:(dc + 1) * 128, :],
                    s1_d.ap()[b, dc * 128:(dc + 1) * 128, :], sem="st")
            p.barrier()

    def ffn1(b):
        p.barrier()
        scb = A2[:, 0:512]
        sq = A2[:, 512:1024]
        tmax = A2[:, 1024:1536]
        fbc_all = A2[:, 4608:6656]
        for qb in range(NTT):
            tsl = slice(qb * 512, qb * 512 + 512)
            for dc in range(NDC):
                sl = xnb[:, dc * 2048 + qb * 512:dc * 2048 + qb * 512 + 512]
                if dc == 0:
                    vtt(tmax, sl, sl, Op.mult)
                else:
                    vtt(sq, sl, sl, Op.mult)
                    p.drain("vector")
                    vtt(tmax, tmax, sq, Op.max)
                p.drain("vector")
            p.rec("vector", lambda v: v.memset(tiny[0:1, 63:64], 0.0), inc=[("vv", 1)])
            p.wait("tensor", "vv")
            for k in range(4):
                ttrans(psb[6][:, k * 128:(k + 1) * 128], tmax[:, k * 128:(k + 1) * 128],
                       inc=(k == 3))
            p.wait("vector", "mm")
            mcol = A2[:, 2048:2052]
            vred(mcol, psb[6][:, 0:512].rearrange("p (k i) -> p k i", i=128), Op.max,
                 inc=True)
            p.drain("vector")
            p.wait("scalar", "vv")
            act(mcol, mcol, AF.Sqrt, inc=True)
            p.wait("vector", "ss")
            vts(mcol, mcol, 1e-5, Op.max)
            p.drain("vector")
            fcol = A2[:, 2052:2056]
            vts(fcol, mcol, 1.0 / 127.0, Op.mult)
            p.drain("vector")
            vts(fcol, fcol, tiny[:, 26:27], Op.mult)
            p.drain("vector")
            vrecip(mcol, mcol)
            p.drain("vector")
            vts(mcol, mcol, 127.0, Op.mult, inc=True)
            p.drain("vector")
            p.wait("tensor", "vv")
            ttrans(psb[6][0:4, 0:128], mcol)
            ttrans(psb[7][0:4, 0:128], fcol, inc=True)
            p.wait("scalar", "mm")
            scrows = A2[0:4, 2176:2304]
            frows = A2[0:4, 2304:2432]
            scopy(scrows, psb[6][0:4, 0:128])
            scopy(frows, psb[7][0:4, 0:128], inc=True)
            p.wait("gpsimd", "ss")
            scrow_r = A2[0:1, 2432:2944]
            frow_r = A2[0:1, 2944:3456]
            dma("gpsimd", scrow_r.rearrange("a (k i) -> a k i", i=128), scrows.unsqueeze(2).transpose([0, 2, 1]) if False else scrows, sem="gl")
            dma("gpsimd", frow_r.rearrange("a (k i) -> a k i", i=128), frows, sem="gl")
            glF()
            p.wait("tensor", "gp")
            p.wait("vector", "gp")
            bcast_row(4, scb, scrow_r, 512)
            bcast_row(5, fbc_all[:, qb * 512:qb * 512 + 512], frow_r, 512)
            p.wait("vector", "ss")
            for dc in range(NDC):
                sl = xnb[:, dc * 2048 + qb * 512:dc * 2048 + qb * 512 + 512]
                vtt(sq.bitcast(I32), sl, scb, Op.mult)
                p.drain("vector")
                vcopy(BFA[:, dc * 512:dc * 512 + 512], sq.bitcast(I32), inc=True)
                p.drain("vector")
            p.wait("gpsimd", "vv")
            p.wait("gpsimd", "st")
            for dc in range(NDC):
                dma("gpsimd", xq_d.ap()[b, dc, :, tsl], BFA[:, dc * 512:dc * 512 + 512],
                    sem="st")
            stF(); p.wait("vector", "gp")
        hfull = A3[:, 0:2048]
        h2 = A3[:, 2048:4096]
        hmax = A2[:, 8192:10240]
        t1c = BFA[:, 0:1024]
        rhs = BFA[:, 1024:5120]
        cwc = A2[:, 4096:4099]
        vmemset(hmax, 0.0, inc=True)
        for hc in range(NHC1):
            p.wait("sync", "vv")
            p.wait("sync", "ss")
            dma("sync", t1c.rearrange("p (c m) -> p c m", c=NDC),
                t1_d.ap()[:, hc, :, :].rearrange("c p m -> p c m"))
            dma("sync", cwc, cw_h[hc * 128:(hc + 1) * 128, :])
            dma("sync", A2[0:1, 4100:4228], cb_h[0:1, hc * 128:(hc + 1) * 128])
            ldF(); p.wait("tensor", "ld")
            mm(psb[5][0:128, 0:1], A2[0:1, 4100:4228], ones[0:1, 0:1],
               True, True, inc=True)
            p.wait("scalar", "mm")
            scopy(tiny[:, 12:13], psb[5][0:128, 0:1], inc=True)
            p.wait("vector", "ss")
            p.wait("tensor", "ss")
            for qb in range(NTT):
                p.wait("sync", "mm")
                dma("sync", rhs.rearrange("p (c n) -> p c n", c=NDC),
                    xq_d.ap()[b, :, :, qb * 512:qb * 512 + 512].rearrange("c p n -> p c n"))
                ldF(); p.wait("tensor", "ld")
                for ic in range(NDC):
                    mm(psb[qb % 2][:, 0:512], t1c[:, ic * 128:(ic + 1) * 128],
                       rhs[:, ic * 512:(ic + 1) * 512],
                       ic == 0, ic == NDC - 1, inc=(ic == NDC - 1))
                p.wait("vector", "mm")
                vtt(hfull[:, qb * 512:qb * 512 + 512], psb[qb % 2][:, 0:512],
                    fbc_all[:, qb * 512:qb * 512 + 512], Op.mult, inc=True)
                p.drain("vector")
                p.wait("tensor", "vv")
            vts(h2, hfull, cwc[:, 1:2], Op.mult, tiny[:, 12:13], Op.add)
            p.drain("vector")
            vstt(h2[:, 1:2048], hfull[:, 0:2047], cwc[:, 0:1], Op.mult, h2[:, 1:2048], Op.add)
            p.drain("vector")
            vstt(h2[:, 0:2047], hfull[:, 1:2048], cwc[:, 2:3], Op.mult, h2[:, 0:2047], Op.add,
                 inc=True)
            p.drain("vector")
            p.wait("scalar", "vv")
            act(hfull, h2, AF.Silu, inc=True)
            p.wait("vector", "ss")
            vtt(h2, hfull, hfull, Op.mult)
            p.drain("vector")
            vtt(hmax, hmax, h2, Op.max, inc=True)
            p.drain("vector")
            p.wait("gpsimd", "vv")
            dma("gpsimd", h1_d.ap()[b, hc, :, :], hfull, sem="st")
            stF(); p.wait("vector", "gp")
            stF(); p.wait("scalar", "gp")
        p.wait("tensor", "vv")
        p.wait("tensor", "ss")
        for k in range(16):
            ttrans(psb[4 + k // 4][:, (k % 4) * 128:(k % 4) * 128 + 128],
                   hmax[:, k * 128:(k + 1) * 128], inc=(k == 15))
        p.wait("vector", "mm")
        m16 = A2[:, 2048:2064]
        for bk in range(4):
            vred(m16[:, bk * 4:(bk + 1) * 4],
                 psb[4 + bk][:, 0:512].rearrange("p (k i) -> p k i", i=128), Op.max,
                 inc=(bk == 3))
            p.drain("vector")
        p.wait("scalar", "vv")
        act(m16, m16, AF.Sqrt, inc=True)
        p.wait("vector", "ss")
        vts(m16, m16, 1e-5, Op.max)
        p.drain("vector")
        f16 = A2[:, 2064:2080]
        vts(f16, m16, 1.0 / 127.0, Op.mult)
        p.drain("vector")
        vts(f16, f16, tiny[:, 29:30], Op.mult)
        p.drain("vector")
        vrecip(m16, m16)
        p.drain("vector")
        vts(m16, m16, 127.0, Op.mult, inc=True)
        p.drain("vector")
        p.wait("tensor", "vv")
        ttrans(psb[4][0:16, 0:128], m16)
        ttrans(psb[5][0:16, 0:128], f16, inc=True)
        p.wait("scalar", "mm")
        scrows2 = A2[0:16, 2176:2304]
        frows2 = A2[0:16, 2304:2432]
        scopy(scrows2, psb[4][0:16, 0:128])
        scopy(frows2, psb[5][0:16, 0:128], inc=True)
        p.wait("gpsimd", "ss")
        sc2row_r = A2[0:1, 10240:12288]
        f2row_r = A2[0:1, 6656:8704]
        dma("gpsimd", sc2row_r.rearrange("a (k i) -> a k i", i=128), scrows2, sem="gl")
        dma("gpsimd", f2row_r.rearrange("a (k i) -> a k i", i=128), frows2, sem="gl")
        glF()
        p.wait("tensor", "gp")
        p.wait("vector", "gp")
        t2c = BFA[:, 0:1024]
        sc2b = A2[:, 0:512]
        f2b = A2[:, 512:1024]
        hrt = A2[:, 1024:1536]
        g1t = A2[:, 1536:2048]
        x1t = A2[:, 2560:3072]
        s1t = A2[:, 3072:3584]
        qrh = BFA[:, 5120:5632]
        for qb in range(NTT):
            tsl = slice(qb * 512, qb * 512 + 512)
            p.wait("tensor", "vv")
            p.wait("tensor", "ss")
            bcast_row(4, sc2b, sc2row_r[0:1, tsl], 512)
            bcast_row(5, f2b, f2row_r[0:1, tsl], 512)
            p.wait("sync", "ss")
            for hc in range(NHC1):
                p.wait("sync", "vv")
                dma("sync", hrt, h1_d.ap()[b, hc, :, tsl])
                dma("sync", t2c.rearrange("p (c m) -> p c m", c=NDC),
                    t2_d.ap()[hc, :, :, :].rearrange("c p m -> p c m"))
                ldF(); p.wait("vector", "ld")
                vtt(hrt.bitcast(I32), hrt, sc2b, Op.mult)
                p.drain("vector")
                vcopy(qrh, hrt.bitcast(I32), inc=True)
                p.drain("vector")
                p.wait("tensor", "vv")
                ldF(); p.wait("tensor", "ld")
                for ic in range(NDC):
                    mm(psb[ic][:, 0:512], t2c[:, ic * 128:(ic + 1) * 128], qrh,
                       hc == 0, hc == NHC1 - 1, inc=(ic == NDC - 1))
                p.wait("sync", "mm")
            p.wait("vector", "mm")
            for ic in range(NDC):
                vtt(g1t, psb[ic][:, 0:512], f2b, Op.mult, inc=True)
                p.drain("vector")
                p.wait("scalar", "vv")
                act(g1t, g1t, AF.Silu, inc=True)
                p.wait("sync", "ss")
                dma("sync", x1t, x1_d.ap()[b, ic * 128:(ic + 1) * 128, tsl])
                dma("sync", s1t, s1_d.ap()[b, ic * 128:(ic + 1) * 128, tsl])
                ldF(); p.wait("vector", "ld")
                p.wait("vector", "ss")
                vtt(g1t, g1t, s1t, Op.mult)
                p.drain("vector")
                vtt(g1t, g1t, x1t, Op.add, inc=True)
                p.drain("vector")
                p.wait("gpsimd", "vv")
                dma("gpsimd", out_h[b, ic * 128:(ic + 1) * 128, tsl], g1t, sem="st")
                stF(); p.wait("vector", "gp")
                stF(); p.wait("scalar", "gp")

    ln_and_xn(0, 0, x_h, maxcol=40, first_max=True)
    attention(0)
    ln_and_xn(0, 1, x_h, maxcol=40, first_max=False)
    attention(1)
    finish_scales([(40, 42, 43, 32767.0, 1e-12)])
    ffn0_matmuls()
    finish_scales([(44, 50, 51, 32767.0, 1e-12), (45, 52, 53, 32767.0, 1e-12)])
    ffn0_elemwise()
    prep_wdownT()
    ffn0_down_combine()
    p.barrier()
    if debug:
        dma("gpsimd", dbg["d_tiny"][:, :], tiny[:, :], sem="st")
        dma("gpsimd", dbg["d_upt"][:, :], up_d.ap()[0, 0, :, :], sem="st")
        dma("gpsimd", dbg["d_gtt"][:, :], gt_d.ap()[0, 0, :, :], sem="st")
        dma("gpsimd", dbg["d_ot"][:, :], or_d.ap()[0, 0, :, :], sem="st")
        p.barrier()
    prep_ternary(w1_h, I1, D, 20, t1_d)
    prep_ternary(w2_h, D, I1, 23, t2_d)
    prw = A2[0:1, 0:1024]
    for i, src in enumerate((nulog_h, thlog_h, gml_h, dvec_h)):
        p.wait("sync", "ss")
        stF(); p.wait("sync", "gp")
        dma("sync", prw, src[0:1, :])
        ldF(); p.wait("scalar", "ld")
        if i == 0:
            act(prw, prw, AF.Exp, inc=True)
            p.drain("scalar")
            act(prw, prw, AF.Exp, scale=-1.0, inc=True)
        elif i in (1, 2):
            act(prw, prw, AF.Exp, inc=True)
        else:
            scopy(prw, prw, inc=True)
        p.wait("gpsimd", "ss")
        dma("gpsimd", par_d.ap()[i:i + 1, :], prw, sem="st")
        stF(); p.wait("scalar", "gp")
    p.barrier()
    for b in range(NB):
        ln_and_xn(1, b, x1_d.ap())
        lru(b)
        ffn1(b)
    p.barrier()
    blk = eN(nc.Block())
    p.emit(blk)
    stk.close()
    return nc, dbg


_IOTA, _BANDM = None, None


def kernel(**inputs):
    global _IOTA, _BANDM
    if _IOTA is None:
        _IOTA, _BANDM = _host_constants()
    nc, dbg = build_nc(DEBUG)
    idn = np.eye(128, dtype=np.float32)
    x = np.asarray(inputs["L0_attn_hidden_states"])
    in_maps = []
    for c in range(NC):
        shard = np.ascontiguousarray(x[c * NB:(c + 1) * NB].transpose(0, 2, 1))
        in_maps.append({
            "x": shard,
            "wq": np.asarray(inputs["L0_attn_Wq"]),
            "wk": np.asarray(inputs["L0_attn_Wk"]),
            "wv": np.asarray(inputs["L0_attn_Wv"]),
            "wo": np.asarray(inputs["L0_attn_Wo"]),
            "relb": np.asarray(inputs["L0_attn_rel_bias"]),
            "wgate": np.asarray(inputs["L0_ffn_w_gate"]),
            "wup": np.asarray(inputs["L0_ffn_w_up"]),
            "wdown": np.asarray(inputs["L0_ffn_w_down"]),
            "nulog": np.asarray(inputs["L1_attn_nu_log"])[None, :],
            "thlog": np.asarray(inputs["L1_attn_theta_log"])[None, :],
            "bre": np.ascontiguousarray(np.asarray(inputs["L1_attn_B_re"]).T),
            "bim": np.ascontiguousarray(np.asarray(inputs["L1_attn_B_im"]).T),
            "cre": np.ascontiguousarray(np.asarray(inputs["L1_attn_C_re"]).T),
            "cim": np.ascontiguousarray(np.asarray(inputs["L1_attn_C_im"]).T),
            "dvec": np.asarray(inputs["L1_attn_D"])[None, :],
            "gml": np.asarray(inputs["L1_attn_gamma_log"])[None, :],
            "w1": np.asarray(inputs["L1_ffn_w1"]),
            "cw": np.ascontiguousarray(np.asarray(inputs["L1_ffn_conv_w"]).reshape(I1, 3)),
            "cb": np.asarray(inputs["L1_ffn_conv_b"])[None, :],
            "w2": np.asarray(inputs["L1_ffn_w2"]),
            "iota": _IOTA,
            "bandm": _BANDM,
            "ident_h": idn,
        })
    res = run_bass_kernel_spmd(nc, in_maps, core_ids=list(range(NC)))
    out = np.empty((B, S, D), dtype=np.float32)
    for c in range(NC):
        o = np.asarray(res.results[c]["out"])
        out[c * NB:(c + 1) * NB] = o.transpose(0, 2, 1)
    return out
